# version 23
# baseline (speedup 1.0000x reference)
import math
import contextlib
import numpy as np
import concourse.bass as bass
import concourse.mybir as mybir
from concourse.bass_utils import run_bass_kernel_spmd
import concourse.dve_ops as _dve_ops
from concourse.dve_op import DveOp as _DveOp
from concourse.dve_spec import Spec as _Spec, Src0 as _Src0, Src1 as _Src1, C0 as _C0, \
    Zero as _Zero, select as _select
from concourse.dve_table_gen import dve_ver_for as _dve_ver_for

F32 = mybir.dt.float32
BF16 = mybir.dt.bfloat16
AF = mybir.ActivationFunctionType
ALU = mybir.AluOpType
AX = mybir.AxisListType

NCORE = 8
D = 1024
T = 2048
CH = 8
DC = 4096
NE = 4096
NH = 4
TG = 1024
NTL = 8
EPS = 1e-5
NHEAD = 8
SCALE = 1.0 / math.sqrt(128.0)

TAPS = {}


def _sel_add_ref(in0, in1, s0, s1, imm2):
    return np.where(in0 >= s0, in0, 0.0) + in1


def _register_dve_op(name, spec, ref):
    for o in _dve_ops.OPS:
        if o.name == name:
            return o
    ver = _dve_ver_for("TRN2")
    op = _DveOp(name, spec, subdim=False, uops_sha={}, reference=ref)
    try:
        op.compile(ver)
        sha = None
    except ValueError as e:
        import re as _re
        m = _re.search(r"\(%s: (\w+)" % ver, str(e))
        sha = m.group(1)
    if sha is not None:
        op = _DveOp(name, spec, subdim=False, uops_sha={ver: sha}, reference=ref)
        op.compile(ver)
    _dve_ops.OPS.append(op)
    _dve_ops._SUB_OPCODE_FOR_NAME[op.name] = (_dve_ops._CUSTOM_DVE_ROW_BASE
                                              + len(_dve_ops.OPS) - 1)
    _dve_ops.CUSTOM_DVE_SPECS[op.name] = spec
    _dve_ops.CUSTOM_DVE_REFS[op.name] = ref
    return op


XBAR_TRANSPOSE = True


def _host_consts():
    pos = np.arange(T, dtype=np.float64)[:, None]
    inv = np.exp(-math.log(10000.0) * np.arange(0, 128, 2, dtype=np.float64) / 128.0)
    ang = pos * inv[None, :]
    pe = np.zeros((T, 128), dtype=np.float64)
    pe[:, 0::2] = np.sin(ang)
    pe[:, 1::2] = np.cos(ang)
    pet = np.ascontiguousarray(pe.T.astype(np.float32))
    pp = np.arange(128)[:, None, None]
    cc = np.arange(4)[None, :, None]
    jj = np.arange(512)[None, None, :]
    stair = ((128 * cc + pp) <= jj).astype(np.float32).reshape(128, 4 * 512)
    ident = np.eye(128, dtype=np.float32)
    return pet, stair, ident


def kernel(**inp):
    pet, stair, ident = _host_consts()
    x = np.ascontiguousarray(inp["L0_attn_x"])
    com = {
        "wq": np.ascontiguousarray(inp["L0_attn_Wq"]),
        "wkv": np.ascontiguousarray(inp["L0_attn_Wkv"]),
        "wo": np.ascontiguousarray(inp["L0_attn_Wo"]),
        "wg": np.ascontiguousarray(inp["L0_ffn_Wg"]),
        "wu": np.ascontiguousarray(inp["L0_ffn_Wu"]),
        "wd": np.ascontiguousarray(inp["L0_ffn_Wd"]),
        "wqf": np.ascontiguousarray(inp["L0_ffn_Wq"]),
        "dte": np.ascontiguousarray(inp["L0_ffn_down_embed"].T),
        "ue": np.ascontiguousarray(inp["L0_ffn_up_embed"]),
        "keyst": np.ascontiguousarray(inp["L0_ffn_keys"].transpose(2, 0, 3, 1)),
        "bqc": np.ascontiguousarray(inp["L0_attn_bq"].reshape(8, 128).T),
        "bkvc": np.ascontiguousarray(inp["L0_attn_bkv"].reshape(2, 128).T),
        "bgc": np.ascontiguousarray(inp["L0_ffn_bg"].reshape(32, 128).T),
        "buc": np.ascontiguousarray(inp["L0_ffn_bu"].reshape(32, 128).T),
        "bdr": np.ascontiguousarray(inp["L0_ffn_bd"].reshape(1, 1024)),
        "pet": pet, "stair": stair, "ident": ident,
    }
    nc = build_nc()
    in_maps = []
    for b in range(NCORE):
        m = dict(com)
        m["xt"] = np.ascontiguousarray(x[b].T)
        in_maps.append(m)
    res = run_bass_kernel_spmd(nc, in_maps, core_ids=list(range(NCORE)))
    out = np.stack([np.ascontiguousarray(res.results[b]["out"].T) for b in range(NCORE)], axis=0)
    return out.astype(np.float32)


class Cnt:
    def __init__(self, ctx, nc, name):
        self.h = ctx.enter_context(nc.semaphore(name))
        self.n = 0

    def inc(self, instr, k=1):
        self.n += k
        instr.then_inc(self.h, k)
        return self.n


class LanePair:

    def __init__(self, ctx, nc, name):
        self.lanes = [Cnt(ctx, nc, name + "a"), Cnt(ctx, nc, name + "b")]
        self.i = 0

    def issue(self, eng, out, in_, **kw):
        lane = self.lanes[self.i % 2]
        self.i += 1
        eng.wait_ge(lane.h, lane.n)
        tok = lane.inc(eng.dma_start(out=out, in_=in_, **kw), 16)
        return (lane, tok)


class P:
    pass


def build_nc():
    nc = bass.Bass("TRN2", target_bir_lowering=False)
    p = P()
    p.nc = nc
    dt = nc.dram_tensor
    p.d_xt = dt("xt", [D, T], F32, kind="ExternalInput").ap()
    p.d_wq = dt("wq", [D, D], F32, kind="ExternalInput").ap()
    p.d_wkv = dt("wkv", [D, 256], F32, kind="ExternalInput").ap()
    p.d_wo = dt("wo", [D, D], F32, kind="ExternalInput").ap()
    p.d_wg = dt("wg", [D, DC], F32, kind="ExternalInput").ap()
    p.d_wu = dt("wu", [D, DC], F32, kind="ExternalInput").ap()
    p.d_wd = dt("wd", [DC, D], F32, kind="ExternalInput").ap()
    p.d_wqf = dt("wqf", [D, 512], F32, kind="ExternalInput").ap()
    p.d_dt = dt("dte", [D, NE], F32, kind="ExternalInput").ap()
    p.d_ue = dt("ue", [NE, D], F32, kind="ExternalInput").ap()
    p.d_keyst = dt("keyst", [2, NH, 64, 64], F32, kind="ExternalInput").ap()
    p.d_bqc = dt("bqc", [128, 8], F32, kind="ExternalInput").ap()
    p.d_bkvc = dt("bkvc", [128, 2], F32, kind="ExternalInput").ap()
    p.d_bgc = dt("bgc", [128, 32], F32, kind="ExternalInput").ap()
    p.d_buc = dt("buc", [128, 32], F32, kind="ExternalInput").ap()
    p.d_bdr = dt("bdr", [1, D], F32, kind="ExternalInput").ap()
    p.d_pet = dt("pet", [128, T], F32, kind="ExternalInput").ap()
    p.d_stair = dt("stair", [128, 4 * 512], F32, kind="ExternalInput").ap()
    p.d_ident = dt("ident", [128, 128], F32, kind="ExternalInput").ap()
    p.d_gd = dt("gspill", [2, NTL, 128, NE], BF16, kind="Internal").ap()
    p.d_out = dt("out", [D, T], F32, kind="ExternalOutput").ap()
    p.taps = {name: dt("tap_" + name, list(shape), dtp, kind="ExternalOutput").ap()
              for name, (shape, dtp) in TAPS.items()}

    ctx = contextlib.ExitStack()

    def al(name, shape, dtp):
        return ctx.enter_context(nc.sbuf_tensor(name, shape, dtp))

    p.x = al("x", [128, CH, T], F32)
    p.big = al("big", [128, 32, TG], BF16)
    p.xn = al("xn", [128, CH, TG], BF16)
    p.scrG = al("scrG", [128, NE], BF16)
    p.scrP = al("scrP", [128, NE], BF16)
    p.gT = al("gT", [128, 32, 128], BF16)
    p.wr1 = al("wr1", [128, 4096], BF16)
    p.wr2 = al("wr2", [128, 2048], BF16)
    p.cmpb = al("cmpb", [128, 896], BF16)
    p.simbuf = al("simbuf", [128, 4096], F32)
    p.s1v8 = al("s1v8", [128, 8, 8, 8], F32)
    p.e8all = al("e8all", [128, 8, 64], BF16)
    p.bias = al("bias", [128, 80], F32)
    p.q8 = al("q8", [128, NH, 8], F32)
    p.q8S = al("q8S", [128, NH, 8], BF16)
    p.q9S = al("q9S", [128, NH, 8], BF16)
    p.zal = al("zal", [128, 3 * NH], F32)
    p.ub = al("ub", [128, 128], BF16)
    p.ident = p.ub[:, :]
    p.bdrt = al("bdr_sb", [1, 1024], BF16)
    p.bdr = p.bdrt[0:1, :]
    p.ones_r = al("ones_r", [1, 512], BF16)
    p.ones_rf = al("ones_rf", [1, 128], F32)
    p.ones_c = al("ones_c", [128, 1], F32)
    p.onesb_c = al("onesb_c", [128, 1], BF16)
    p.eps_c = al("eps_c", [128, 1], F32)
    p.keysbd = al("keysbd", [128, 4, 128], BF16)
    p.sil = p.scrP[:].bitcast(F32)
    p.silb = p.scrP[:].rearrange("p (s n) -> p s n", s=4)
    p.pe = p.simbuf[:].bitcast(BF16)[:, 0:2048]
    p.rows = p.simbuf[0:1, 1024:2048]
    p.q2 = p.big[:, 28:32, :]
    p.stair = p.wr2[:].rearrange("p (a b) -> p a b", a=4)

    p.psA = ctx.enter_context(nc.psum_tensor("psA", [128, 1024], F32))
    p.psB = ctx.enter_context(nc.psum_tensor("psB", [128, 1024], F32))
    p.psC = ctx.enter_context(nc.psum_tensor("psC", [128, 1024], F32))
    p.psD = ctx.enter_context(nc.psum_tensor("psD", [128, 1024], F32))

    p.TE = Cnt(ctx, nc, "qTE")
    p.DV = Cnt(ctx, nc, "qDV")
    p.AC = Cnt(ctx, nc, "qAC")
    p.PO = Cnt(ctx, nc, "qPO")
    p.DW1 = LanePair(ctx, nc, "qDW1")
    p.DW2 = LanePair(ctx, nc, "qDW2")
    p.DW3 = Cnt(ctx, nc, "qDW3")
    p.DS = Cnt(ctx, nc, "qDS")
    p.DT = Cnt(ctx, nc, "qDT")
    p.DG = LanePair(ctx, nc, "qDG")
    p.x_dv = 0
    p.xn_free = (p.DV, 0)
    p.psD_free = (p.DV, 0)
    p.ln_psD_free_dv = 0
    p.last_trp = 0
    p.tok_wt_last = 0
    p.scr_free = (p.DV, 0)
    p.t_c4 = 0
    p.big_free = (p.TE, 0)
    p.ring_free = (p.TE, 0)
    p.ring2_free = (p.TE, 0)
    p.ub_free = (p.TE, 0)
    p.psC_free = 0
    p.q2buf_free = []
    p.simbuf_free = []
    p.scrG_free_l = []
    p.gT_free_l = []
    p.tap_hold = []
    p.next_tok_xn = None

    with nc.Block() as blk:
        p.blk = blk
        emit_init(p)
        emit_ffn(p)
        emit_attn(p)
        emit_ffn(p)
        emit_store(p)
    ctx.close()
    return nc


def dma(eng, lane, out, in_, **kw):
    if isinstance(lane, LanePair):
        return lane.issue(eng, out, in_, **kw)
    eng.wait_ge(lane.h, lane.n)
    return lane.inc(eng.dma_start(out=out, in_=in_, **kw), 16)


def wdma(eng, token):
    lane, tok = token
    eng.wait_ge(lane.h, tok)


def wl(eng, lst):
    for h, v in lst:
        eng.wait_ge(h, v)


def tap(p, name, ap_src, cnt, tok):
    if name not in p.taps:
        return None
    blk = p.blk
    out = [None]

    @blk.sync
    def _(s):
        s.wait_ge(cnt.h, tok)
        out[0] = dma(s, p.DS, p.taps[name], ap_src)
    return out[0]


def emit_init(p):
    blk = p.blk

    @blk.sync
    def _(s):
        for c in range(CH):
            dma(s, p.DS, p.x[:, c, :], p.d_xt[128 * c:128 * (c + 1), :])
    p.tok_xin = p.DS.n

    @blk.gpsimd
    def _(g):
        dma(g, p.DW3, p.bias[:, 0:32], p.d_bgc)
        dma(g, p.DW3, p.bias[:, 32:64], p.d_buc)
        dma(g, p.DW3, p.bias[:, 64:72], p.d_bqc)
        dma(g, p.DW3, p.bias[:, 72:74], p.d_bkvc)
        tz = p.PO.inc(g.memset(p.keysbd[:], 0.0))
        g.wait_ge(p.PO.h, tz)
        for c in range(4):
            pp, hlo = c // 2, (c % 2) * 2
            dma(g, p.DW3, p.keysbd[0:64, c, 0:64], p.d_keyst[pp, hlo, :, :])
            dma(g, p.DW3, p.keysbd[64:128, c, 64:128], p.d_keyst[pp, hlo + 1, :, :])
    p.tok_const = p.DW3.n

    @blk.vector
    def _(v):
        p.DV.inc(v.memset(p.ones_r[:], 1.0))
        p.DV.inc(v.memset(p.ones_rf[:], 1.0))
        p.DV.inc(v.memset(p.ones_c[:], 1.0))
        p.DV.inc(v.memset(p.onesb_c[:], 1.0))
        p.DV.inc(v.memset(p.eps_c[:], float(EPS)))
    p.tok_ones = p.DV.n


def emit_ln(p, t0, scratch, scr_free, defer=False, bank=None):
    blk = p.blk
    psb = bank if bank is not None else p.psD
    xnf, psDf = p.xn_free, p.psD_free
    x_dv = p.x_dv
    hold0 = list(p.tap_hold)
    fins = []
    prev_stats = [0]
    tok_rm_l = [0, 0]
    norm_prev = [0]
    for b in range(TG // 512):
        tb = t0 + 512 * b
        base = 1024 + 1024 * b
        mb = p.simbuf[0:1, base:base + 512]
        varb = p.simbuf[0:1, base + 512:base + 1024]
        r = varb
        mr = mb
        te0, ac0, dv0 = p.TE.n, p.AC.n, p.DV.n
        tok_sq = [ac0 + c + 1 for c in range(CH)]
        tok_sd = ac0 + CH + 1
        tok_sumx = te0 + 1
        tok_smm = [te0 + 1 + c + 1 for c in range(CH)]
        tok_stats = dv0 + 3
        tok_rm = dv0 + 5
        tok_rm_l[b] = tok_rm

        @blk.scalar
        def _(s, tb=tb, b=b, r=r, varb=varb, tok_sq=tok_sq, tok_smm=tok_smm,
              tok_stats=tok_stats, tok_sd=tok_sd):
            s.wait_ge(p.DS.h, p.tok_xin)
            s.wait_ge(p.DV.h, x_dv)
            if b == 0:
                s.wait_ge(scr_free[0].h, scr_free[1])
                wl(s, hold0)
            for c in range(CH):
                if c >= 2:
                    s.wait_ge(p.TE.h, tok_smm[c - 2])
                a = p.AC.inc(s.activation(scratch[:, 512 * (c % 2):512 * (c % 2) + 512],
                                          p.x[:, c, tb:tb + 512], AF.Square))
                assert a == tok_sq[c]
            s.wait_ge(p.DV.h, tok_stats)
            assert p.AC.inc(s.activation(r, varb, AF.Sqrt, bias=p.eps_c[0:1, :])) == tok_sd

        @blk.tensor
        def _(t, tb=tb, b=b, tok_sq=tok_sq, tok_smm=tok_smm, tok_sumx=tok_sumx,
              pstats=prev_stats[0]):
            t.wait_ge(p.DS.h, p.tok_xin)
            t.wait_ge(p.DV.h, max(x_dv, p.tok_ones, p.ln_psD_free_dv, pstats))
            if b == 0:
                t.wait_ge(psDf[0].h, psDf[1])
            last = None
            for c in range(CH):
                last = t.matmul(psb[0:1, 0:512], p.ones_c[:], p.x[:, c, tb:tb + 512],
                                start=(c == 0), stop=(c == CH - 1))
            assert p.TE.inc(last) == tok_sumx
            for c in range(CH):
                t.wait_ge(p.AC.h, tok_sq[c])
                mm = t.matmul(psb[32:33, 0:512], p.ones_c[:],
                              scratch[:, 512 * (c % 2):512 * (c % 2) + 512],
                              start=(c == 0), stop=(c == CH - 1))
                assert p.TE.inc(mm) == tok_smm[c]

        @blk.vector
        def _(v, b=b, mb=mb, varb=varb, r=r, mr=mr, tok_smm=tok_smm,
              tok_stats=tok_stats, tok_sd=tok_sd, tok_rm=tok_rm):
            v.wait_ge(p.TE.h, tok_smm[CH - 1])
            t1 = p.DV.inc(v.tensor_scalar_mul(mb, psb[0:1, 0:512], 1.0 / D))
            v.wait_ge(p.DV.h, t1)
            t2 = p.DV.inc(v.tensor_mul(varb, mb, mb))
            v.wait_ge(p.DV.h, t2)
            t3 = p.DV.inc(v.scalar_tensor_tensor(varb, psb[32:33, 0:512], 1.0 / D, varb,
                                                 ALU.mult, ALU.subtract))
            assert t3 == tok_stats
            v.wait_ge(p.AC.h, tok_sd)
            t4 = p.DV.inc(v.reciprocal(r, r))
            v.wait_ge(p.DV.h, t4)
            t5 = p.DV.inc(v.tensor_mul(mr, mb, r))
            assert t5 == tok_rm
        prev_stats[0] = tok_stats

        def fin(b=b, tb=tb, r=r, mr=mr, tok_rm=tok_rm):
            t_bc = [0]

            @blk.tensor
            def _(t):
                t.wait_ge(p.DV.h, max(tok_rm, tok_rm_l[1], norm_prev[0]))
                t.matmul(psb[:, 0:512], p.ones_rf[:], r, start=True, stop=True)
                t_bc[0] = p.TE.inc(t.matmul(psb[:, 512:1024], p.ones_rf[:], mr,
                                            start=True, stop=True))

            @blk.vector
            def _(v):
                v.wait_ge(p.TE.h, t_bc[0])
                v.wait_ge(xnf[0].h, xnf[1])
                if b == 0:
                    wl(v, hold0)
                for c in range(CH):
                    ta = p.DV.inc(v.tensor_mul(scratch[:, 1024:1536], p.x[:, c, tb:tb + 512],
                                               psb[:, 0:512]))
                    v.wait_ge(p.DV.h, ta)
                    t6 = p.DV.inc(v.tensor_sub(p.xn[:, c, 512 * b:512 * (b + 1)],
                                               scratch[:, 1024:1536], psb[:, 512:1024]))
                    v.wait_ge(p.DV.h, t6)
            norm_prev[0] = p.DV.n
            return p.DV.n
        fins.append(fin)

    def finish():
        tok = 0
        for f in fins:
            tok = f()
        p.ln_psD_free_dv = tok
        p.psD_free = (p.DV, tok)
        return tok
    if defer:
        return finish
    return finish()


FFN_CALL = [0]


def emit_ffn(p):
    call = FFN_CALL[0]
    FFN_CALL[0] += 1
    for g in range(2):
        if p.next_tok_xn is not None:
            tok_xn = p.next_tok_xn
            p.next_tok_xn = None
        else:
            with p.nc.named_scope(f"f{call}g{g}_ln"):
                tok_xn = emit_ln(p, TG * g, p.sil, p.scr_free)
        tk = tap(p, f"xn{call}_{g}", p.xn[:].rearrange("p c t -> p (c t)"), p.DV, tok_xn)
        if tk:
            p.tap_hold.append((p.DS.h, tk))
        if g == 0:
            def nl():
                with p.nc.named_scope(f"f{call}g1_ln"):
                    return emit_ln(p, TG, p.sil, p.scr_free, defer=True)
        elif call == 0:
            def nl():
                with p.nc.named_scope("fa0_ln"):
                    return emit_ln(p, 0, p.sil, p.scr_free, defer=True)
        else:
            nl = None
        emit_group(p, g, tok_xn, nl)


def emit_group(p, g, tok_xn, next_ln=None):
    blk = p.blk
    scope = p.nc.named_scope(f"c{FFN_CALL[0]}g{g}_grp")
    scope.__enter__()
    hold0 = list(p.tap_hold)

    te0, ac0, dv0 = p.TE.n, p.AC.n, p.DV.n
    tok_q2mm = [te0 + oc + 1 for oc in range(4)]
    tok_sim = [te0 + 4 + tt + 1 for tt in range(NTL)]
    tok_Gmm = [te0 + 12 + 2 * c + 1 for c in range(32)]
    tok_Umm = [te0 + 12 + 2 * c + 2 for c in range(32)]
    tok_dmid = [te0 + 76 + 2 * do + 1 for do in range(8)]
    tok_dmm = [te0 + 76 + 2 * do + 2 for do in range(8)]
    tok_hdm = [te0 + 92 + ec + 1 for ec in range(32)]
    tok_q2cp = [ac0 + oc + 1 for oc in range(4)]
    tok_scp = [ac0 + 4 + tt + 1 for tt in range(NTL)]
    t_exn = ac0 + 14
    t_e8 = ac0 + 15
    tok_sil = [ac0 + 15 + 2 * c + 1 for c in range(32)]
    tok_ucp = [ac0 + 15 + 2 * c + 2 for c in range(32)]
    tok_hds = [ac0 + 79 + ec + 1 for ec in range(32)]
    po0 = p.PO.n
    tok_Ap = [po0 + j + 1 for j in range(16)]
    t_max_last = dv0 + 64
    t_mask = dv0 + 65
    t_fneg = dv0 + 66
    assert p.DG.i % 2 == 0
    dg0 = [p.DG.lanes[0].n, p.DG.lanes[1].n]
    tok_st = [(p.DG.lanes[tt % 2], dg0[tt % 2] + 16 * (tt // 2 + 1)) for tt in range(NTL)]

    wgv4 = p.wr1[:].rearrange("p (b d n) -> p b d n", b=4, d=CH)
    wqv = p.wr2[:].rearrange("p (d n) -> p d n", d=CH)
    wdv = p.wr1[:].rearrange("p (b c n) -> p b c n", b=2, c=16)
    dtv = p.wr1[:].rearrange("p (b d n) -> p b d n", b=4, d=CH)
    wg_r = p.d_wg.rearrange("(d p) n -> p d n", p=128)
    wu_r = p.d_wu.rearrange("(d p) n -> p d n", p=128)
    wqf_r = p.d_wqf.rearrange("(d p) n -> p d n", p=128)
    wd_r = p.d_wd.rearrange("(c p) n -> p c n", p=128)
    dt_r = p.d_dt.rearrange("(d p) n -> p d n", p=128)
    sbb = p.simbuf[:].bitcast(BF16)
    fnall = sbb[:, 0:4096]
    gacc2 = sbb[:, 4096:8192]
    tmp = p.gT[:].rearrange("p c t -> p (c t)")
    Z, alpha, thresh = p.zal[:, 0:4], p.zal[:, 4:8], p.zal[:, 8:12]
    qc = p.cmpb[:, 0:256].rearrange("p (h k) -> p h k", h=NH)
    qS = p.cmpb[:, 256:512].rearrange("p (h k) -> p h k", h=NH)
    qtmp = p.cmpb[:, 512:576]
    fxs = p.cmpb[:, 576:832].rearrange("p (h n) -> p h n", h=NH)
    e8xs = p.cmpb[:, 832:864].rearrange("p (h k) -> p h k", h=NH)

    t_wg = [None] * 32
    t_wu = [None] * 32
    t_wd = [None] * 16
    t_dts = [None] * 32
    t_wqf = [None, None]

    @blk.gpsimd
    def _(gg):
        gg.wait_ge(p.ring_free[0].h, p.ring_free[1])
        for c in range(2):
            t_wg[c] = dma(gg, p.DW1, wgv4[:, c % 2, :, :], wg_r[:, :, 128 * c:128 * (c + 1)])
            t_wu[c] = dma(gg, p.DW1, wgv4[:, 2 + c % 2, :, :],
                          wu_r[:, :, 128 * c:128 * (c + 1)])
        gg.wait_ge(p.ub_free[0].h, p.ub_free[1])
        p.t_bdr = dma(gg, p.DW3, p.bdr, p.d_bdr)
        gg.wait_ge(p.ring2_free[0].h, p.ring2_free[1])
        t_wqf[0] = dma(gg, p.DW2, wqv[:, :, :], wqf_r[:, :, 0:256])
        gg.wait_ge(p.TE.h, tok_q2mm[1])
        t_wqf[1] = dma(gg, p.DW2, wqv[:, :, :], wqf_r[:, :, 256:512])

        def a_pair(j):
            gg.wait_ge(p.AC.h, tok_ucp[2 * j + 1])
            a = p.PO.inc(gg.tensor_tensor(
                p.big[:, 2 * j:2 * j + 2, :], p.big[:, 2 * j:2 * j + 2, :],
                p.silb[:, (2 * j) % 4:(2 * j) % 4 + 2, :], ALU.mult))
            assert a == tok_Ap[j]

        for c in range(2, 32):
            gg.wait_ge(p.TE.h, tok_Umm[c - 2])
            t_wg[c] = dma(gg, p.DW1, wgv4[:, c % 2, :, :], wg_r[:, :, 128 * c:128 * (c + 1)])
            t_wu[c] = dma(gg, p.DW1, wgv4[:, 2 + c % 2, :, :],
                          wu_r[:, :, 128 * c:128 * (c + 1)])
            if c % 2 == 1:
                a_pair((c - 3) // 2)
        gg.wait_ge(p.TE.h, tok_Umm[31])
        t_wd[0] = dma(gg, p.DW1, wdv[:, 0, :, :], wd_r[:, 0:16, 0:128])
        t_wd[1] = dma(gg, p.DW1, wdv[:, 1, :, :], wd_r[:, 16:32, 0:128])
        a_pair(15)
        for sl in range(2, 16):
            do, hf = sl // 2, sl % 2
            gg.wait_ge(p.TE.h, tok_dmid[do - 1] if hf == 0 else tok_dmm[do - 1])
            t_wd[sl] = dma(gg, p.DW1, wdv[:, sl % 2, :, :],
                           wd_r[:, 16 * hf:16 * (hf + 1), 128 * do:128 * (do + 1)])
        for ec in range(32):
            gg.wait_ge(p.TE.h, tok_dmm[7] if ec < 4 else tok_hdm[ec - 4])
            t_dts[ec] = dma(gg, p.DW1, dtv[:, ec % 4, :, :],
                            dt_r[:, :, 128 * ec:128 * (ec + 1)])

    tok_ev = [None] * 8
    t_G = [None] * NTL
    sg = [p.scrG[:], gacc2]

    def emit_p1b_ev(v, do):
        v.wait_ge(p.TE.h, tok_dmm[do])
        psO = [p.psA, p.psB, p.psC, p.psD][do % 4]
        tok_ev[do] = p.DV.inc(v.scalar_tensor_tensor(
            p.x[:, do, TG * g:TG * (g + 1)], psO[:], 0.5,
            p.x[:, do, TG * g:TG * (g + 1)], ALU.mult, ALU.add))

    def tile_units(tt):
        acc = sg[tt % 2]
        fn_ = fnall[:, 512 * tt:512 * (tt + 1)]
        fnx = fn_[:, 0:256].rearrange("p (h n) -> p h n", h=NH)
        fny = fn_[:, 256:512].rearrange("p (h n) -> p h n", h=NH)
        e8x = p.e8all[:, tt, 0:32].rearrange("p (h k) -> p h k", h=NH)
        e8y = p.e8all[:, tt, 32:64].rearrange("p (h k) -> p h k", h=NH)

        def c1(v):
            v.wait_ge(p.AC.h, t_e8)
            t = p.DV.inc(v.tensor_tensor(qc, e8x.unsqueeze(3).to_broadcast([128, NH, 8, 8]),
                                         e8y.unsqueeze(2).to_broadcast([128, NH, 8, 8]),
                                         ALU.mult))
            v.wait_ge(p.DV.h, t)
            for h in range(NH):
                p.DV.inc(v.max(out=p.q8[:, h, :], in_=qc[:, h, :]))
            v.wait_ge(p.DV.h, p.DV.n)
            t = p.DV.inc(v.reduce_sum(Z, p.q8[:], axis=AX.X))
            v.wait_ge(p.DV.h, t)
            t = p.DV.inc(v.reciprocal(alpha, Z))
            v.wait_ge(p.DV.h, t)
            t = p.DV.inc(v.tensor_scalar_mul(alpha, alpha, 0.5))
            v.wait_ge(p.DV.h, t)

        def c2(v):
            p.DV.inc(v.tensor_tensor(fxs, fnx,
                                     alpha.unsqueeze(2).to_broadcast([128, NH, 64]), ALU.mult))
            t = p.DV.inc(v.tensor_tensor(e8xs, e8x,
                                         alpha.unsqueeze(2).to_broadcast([128, NH, 8]), ALU.mult))
            v.wait_ge(p.DV.h, t)
            t = p.DV.inc(v.tensor_tensor(qS, e8xs.unsqueeze(3).to_broadcast([128, NH, 8, 8]),
                                         e8y.unsqueeze(2).to_broadcast([128, NH, 8, 8]), ALU.mult))
            v.wait_ge(p.DV.h, t)

        def c3(v):
            for h in range(NH):
                p.DV.inc(v.max(out=p.q8S[:, h, :], in_=qS[:, h, :]))
                v.wait_ge(p.DV.h, p.DV.n)
                p.DV.inc(v.match_replace(out=qtmp, in_to_replace=p.q8S[:, h, :],
                                         in_values=qS[:, h, :], imm_value=-1.0))
                v.wait_ge(p.DV.h, p.DV.n)
                p.DV.inc(v.max(out=p.q9S[:, h, :], in_=qtmp))
            v.wait_ge(p.DV.h, p.DV.n)
            t = p.DV.inc(v.tensor_tensor(thresh, p.q8S[:, :, 7], p.q9S[:, :, 0], ALU.add))
            v.wait_ge(p.DV.h, t)
            t = p.DV.inc(v.tensor_scalar_mul(thresh, thresh, 0.5))
            v.wait_ge(p.DV.h, t)

        def mk_b(h):
            def b_(v):
                dst = acc if h == 0 else tmp
                if h == 0 and tt >= 2:
                    wdma(v, tok_st[tt - 2])
                t = p.DV.inc(v.tensor_tensor(
                    dst.rearrange("p (a b) -> p a b", a=64),
                    fxs[:, h, :].unsqueeze(2).to_broadcast([128, 64, 64]),
                    fny[:, h, :].unsqueeze(1).to_broadcast([128, 64, 64]), ALU.mult))
                v.wait_ge(p.DV.h, t)
            return b_

        def mk_s(h):
            def s_(v):
                dst = acc if h == 0 else tmp
                t = p.DV.inc(v.scalar_tensor_tensor(dst, dst, thresh[:, h:h + 1], dst,
                                                    ALU.is_ge, ALU.mult))
                v.wait_ge(p.DV.h, t)
            return s_

        def mk_a():
            def a_(v):
                t = p.DV.inc(v.tensor_add(acc, acc, tmp))
                v.wait_ge(p.DV.h, t)
            return a_

        def fin(v):
            t_G[tt] = p.DV.n

        us = [(2.2, c1), (1.1, c2), (2.8, c3)]
        for h in range(NH):
            us.append((4.4, mk_b(h)))
            us.append((4.5, mk_s(h)))
            if h > 0:
                us.append((2.3, mk_a()))
        us.append((0.0, fin))
        return us

    units = []
    for tt in range(NTL):
        units += tile_units(tt)
    inserts = [(225.0 + 15.7 * do, 1, do) for do in range(8)]

    @blk.vector
    def _(v):
        wl(v, hold0)
        for tt in range(NTL):
            v.wait_ge(p.AC.h, tok_scp[tt])
            for ph in range(8):
                p.DV.inc(v.max(out=p.s1v8[:, tt, ph, :],
                               in_=p.simbuf[:, 512 * tt + 64 * ph:512 * tt + 64 * (ph + 1)]))
        assert p.DV.n == t_max_last
        wl(v, p.scrG_free_l)
        v.wait_ge(p.DV.h, t_max_last)
        a = p.DV.inc(v.tensor_tensor(
            p.scrG[:].rearrange("p (t h n) -> p t h n", t=NTL, h=8),
            p.simbuf[:].rearrange("p (t h n) -> p t h n", t=NTL, h=8),
            p.s1v8[:, :, :, 7:8].to_broadcast([128, NTL, 8, 64]), ALU.is_ge))
        assert a == t_mask
        v.wait_ge(p.DV.h, t_mask)
        v.wait_ge(p.AC.h, t_exn)
        a = p.DV.inc(v.tensor_mul(fnall, p.scrG[:], tmp))
        assert a == t_fneg
        v.wait_ge(p.DV.h, t_fneg)
        clock = 0.0
        ii = 0
        for dur, fn in units:
            while ii < len(inserts) and inserts[ii][0] <= clock:
                emit_p1b_ev(v, inserts[ii][2])
                clock += 1.2
                ii += 1
            fn(v)
            clock += dur
        while ii < len(inserts):
            emit_p1b_ev(v, inserts[ii][2])
            ii += 1

    @blk.sync
    def _(s):
        for tt in range(NTL):
            s.wait_ge(p.DV.h, t_G[tt])
            got = dma(s, p.DG, p.d_gd[g % 2, tt], sg[tt % 2])
            assert got == tok_st[tt]

    for tt in range(NTL):
        gname = f"G{FFN_CALL[0]}_{g}_{tt}"
        if gname in p.taps:
            @blk.sync
            def _(s, tt=tt, gname=gname):
                wdma(s, tok_st[tt])
                tk = dma(s, p.DS, p.taps[gname], p.d_gd[g % 2, tt])
                p.tap_hold.append((p.DS.h, tk))
    tokA_tap = tap(p, f"A{FFN_CALL[0]}_{g}", p.big[:].rearrange("p c t -> p (c t)"),
                   p.PO, tok_Ap[15])

    @blk.tensor
    def _(t):
        t.wait_ge(p.DV.h, max(tok_xn, p.tok_ones, p.psC_free))
        t.wait_ge(p.DW3.h, p.tok_const)
        for oc in range(4):
            pair, k = oc // 2, oc % 2
            if k == 0:
                wdma(t, t_wqf[pair])
            psQ = p.psC if oc % 2 == 0 else p.psD
            if oc >= 2:
                t.wait_ge(p.AC.h, tok_q2cp[oc - 2])
            last = None
            for dd in range(CH):
                for hh in range(2):
                    last = t.matmul(psQ[:, 512 * hh:512 * (hh + 1)],
                                    wqv[:, dd, 128 * k:128 * (k + 1)],
                                    p.xn[:, dd, 512 * hh:512 * (hh + 1)],
                                    start=(dd == 0), stop=(dd == CH - 1))
            assert p.TE.inc(last) == tok_q2mm[oc]
        for tt in range(NTL):
            t.wait_ge(p.AC.h, tok_q2cp[3] if tt < 2 else tok_scp[tt - 2])
            off = 512 * (tt % 2)
            last = None
            for c in range(4):
                last = t.matmul(p.psD[:, off + 128 * c:off + 128 * (c + 1)],
                                p.q2[:, c, 128 * tt:128 * (tt + 1)],
                                p.keysbd[:, c, :], start=True, stop=True)
            assert p.TE.inc(last) == tok_sim[tt]
        for c in range(32):
            wdma(t, t_wu[c])
            wdma(t, t_wg[c])
            if c == 0:
                t.wait_ge(p.DV.h, p.x_dv)
            if c == 1:
                t.wait_ge(p.AC.h, tok_scp[7])
            psG = p.psA if c % 2 == 0 else p.psC
            psU = p.psB if c % 2 == 0 else p.psD
            if c >= 2:
                t.wait_ge(p.AC.h, tok_ucp[c - 2])
            last = None
            for dd in range(CH):
                for hh in range(2):
                    last = t.matmul(psG[:, 512 * hh:512 * (hh + 1)],
                                    wgv4[:, c % 2, dd, :],
                                    p.xn[:, dd, 512 * hh:512 * (hh + 1)],
                                    start=(dd == 0), stop=(dd == CH - 1))
            assert p.TE.inc(last) == tok_Gmm[c]
            for dd in range(CH):
                for hh in range(2):
                    last = t.matmul(psU[:, 512 * hh:512 * (hh + 1)],
                                    wgv4[:, 2 + c % 2, dd, :],
                                    p.xn[:, dd, 512 * hh:512 * (hh + 1)],
                                    start=(dd == 0), stop=(dd == CH - 1))
            assert p.TE.inc(last) == tok_Umm[c]
        t.wait_ge(p.PO.h, tok_Ap[15])
        t.wait_ge(p.DW3.h, p.t_bdr)
        psOs = [p.psA, p.psB, p.psC, p.psD]
        for do in range(8):
            psO = psOs[do % 4]
            wdma(t, t_wd[2 * do])
            if do >= 4:
                t.wait_ge(p.DV.h, tok_ev[do - 4])
            last = None
            for c in range(16):
                for hh in range(2):
                    last = t.matmul(psO[:, 512 * hh:512 * (hh + 1)],
                                    wdv[:, 0, c, :],
                                    p.big[:, c, 512 * hh:512 * (hh + 1)],
                                    start=(c == 0), stop=False)
            assert p.TE.inc(last) == tok_dmid[do]
            wdma(t, t_wd[2 * do + 1])
            for c in range(16, 32):
                for hh in range(2):
                    t.matmul(psO[:, 512 * hh:512 * (hh + 1)],
                             wdv[:, 1, c % 16, :],
                             p.big[:, c, 512 * hh:512 * (hh + 1)],
                             start=False, stop=False)
            last = None
            for hh in range(2):
                last = t.matmul(psO[:, 512 * hh:512 * (hh + 1)],
                                p.bdr[0:1, 128 * do:128 * (do + 1)], p.ones_r[:],
                                start=False, stop=True)
            assert p.TE.inc(last) == tok_dmm[do]
        t.wait_ge(p.DV.h, tok_ev[7])
        for ec in range(32):
            wdma(t, t_dts[ec])
            psH = p.psA if ec % 2 == 0 else p.psB
            if ec >= 2:
                t.wait_ge(p.AC.h, tok_hds[ec - 2])
            last = None
            for dd in range(CH):
                for hh in range(2):
                    last = t.matmul(psH[:, 512 * hh:512 * (hh + 1)], dtv[:, ec % 4, dd, :],
                                    p.xn[:, dd, 512 * hh:512 * (hh + 1)],
                                    start=(dd == 0), stop=(dd == CH - 1))
            assert p.TE.inc(last) == tok_hdm[ec]

    @blk.scalar
    def _(s):
        wl(s, p.q2buf_free)
        wl(s, p.simbuf_free)
        wl(s, hold0)
        s.wait_ge(p.DW3.h, p.tok_const)
        for oc in range(4):
            psQ = p.psC if oc % 2 == 0 else p.psD
            s.wait_ge(p.TE.h, tok_q2mm[oc])
            a = p.AC.inc(s.activation(p.q2[:, oc, :], psQ[:], AF.Copy))
            assert a == tok_q2cp[oc]
        for tt in range(NTL):
            off = 512 * (tt % 2)
            s.wait_ge(p.TE.h, tok_sim[tt])
            a = p.AC.inc(s.activation(p.simbuf[:, 512 * tt:512 * (tt + 1)],
                                      p.psD[:, off:off + 512], AF.Copy))
            assert a == tok_scp[tt]
        s.wait_ge(p.AC.h, tok_scp[7])
        wl(s, p.gT_free_l)
        p.AC.inc(s.activation(tmp[:, 0:2048], p.simbuf[:, 0:2048], AF.Exp))
        a = p.AC.inc(s.activation(tmp[:, 2048:4096], p.simbuf[:, 2048:4096], AF.Exp))
        assert a == t_exn
        s.wait_ge(p.DV.h, t_max_last)
        a = p.AC.inc(s.activation(p.e8all[:].rearrange("p a b -> p (a b)"),
                                  p.s1v8[:].rearrange("p a b c -> p (a b c)"), AF.Exp))
        assert a == t_e8
        for c in range(32):
            psG = p.psA if c % 2 == 0 else p.psC
            psU = p.psB if c % 2 == 0 else p.psD
            s.wait_ge(p.TE.h, tok_Gmm[c])
            if c >= 4:
                s.wait_ge(p.PO.h, tok_Ap[(c - 4) // 2])
            a = p.AC.inc(s.activation(p.silb[:, c % 4, :],
                                      psG[:], AF.Silu, bias=p.bias[:, c:c + 1]))
            assert a == tok_sil[c]
            s.wait_ge(p.TE.h, tok_Umm[c])
            a = p.AC.inc(s.activation(p.big[:, c, :], psU[:], AF.Identity,
                                      bias=p.bias[:, 32 + c:33 + c]))
            assert a == tok_ucp[c]
        if tokA_tap:
            s.wait_ge(p.DS.h, tokA_tap)
        for ec in range(32):
            psH = p.psA if ec % 2 == 0 else p.psB
            s.wait_ge(p.TE.h, tok_hdm[ec])
            a = p.AC.inc(s.activation(p.big[:, ec, :], psH[:], AF.Silu))
            assert a == tok_hds[ec]

    dests = [p.gT[:], p.scrG[:].rearrange("p (c t) -> p c t", c=32),
             gacc2.rearrange("p (c t) -> p c t", c=32)]
    trp_tok = [None] * NTL
    tok_wt = [None] * NTL
    for tt in range(NTL):
        @blk.sync
        def _(s, tt=tt):
            wdma(s, tok_st[tt])
            if tt == 0:
                s.wait_ge(p.DV.h, t_G[NTL - 1])
            elif tt == 1:
                wdma(s, tok_st[6])
            elif tt == 2:
                wdma(s, tok_st[7])
            else:
                s.wait_ge(p.DV.h, tok_wt[tt - 3])
            s.wait_ge(p.DT.h, p.DT.n)
            trp_tok[tt] = p.DT.inc(s.dma_start_transpose(dests[tt % 3], p.d_gd[g % 2, tt]), 16)

        @blk.vector
        def _(v, tt=tt):
            v.wait_ge(p.AC.h, tok_hds[31])
            v.wait_ge(p.DT.h, trp_tok[tt])
            sl = p.big[:, :, 128 * tt:128 * (tt + 1)]
            tok_wt[tt] = p.DV.inc(v.tensor_mul(sl, sl, dests[tt % 3]))

    tk = tap(p, f"WT{FFN_CALL[0]}_{g}", p.big[:].rearrange("p c t -> p (c t)"),
             p.DV, tok_wt[NTL - 1])
    if tk:
        p.tap_hold.append((p.DS.h, tk))

    p.psD_free = (p.DV, tok_ev[7])
    p.scr_free = (p.PO, tok_Ap[15])
    p.xn_free = (p.TE, tok_hdm[31])
    ln_fin = next_ln() if next_ln is not None else None

    t_p2b0 = p.TE.n
    assert t_p2b0 == te0 + 124 + (18 if next_ln is not None else 0)
    bump = 2 if next_ln is not None else 0
    tok_ueq = {(do, qq): t_p2b0 + 4 * do + qq + 1 + (bump if do >= 2 else 0)
               for do in range(8) for qq in range(4)}
    tok_umm = [tok_ueq[(do, 3)] for do in range(8)]

    uev = p.wr1[:].rearrange("p (b c n) -> p b c n", b=4, c=8)
    ue_r = p.d_ue.rearrange("(c p) n -> p c n", p=128)
    tok_ue = {}
    tok_uev = [0] * 8
    for do in range(8):
        psO = [p.psA, p.psB, p.psC][do % 3]
        for qq in range(4):
            j = 4 * do + qq

            @blk.gpsimd
            def _(gg, do=do, qq=qq, j=j):
                if j < 4:
                    gg.wait_ge(p.TE.h, tok_hdm[28 + j])
                else:
                    pj = j - 4
                    gg.wait_ge(p.TE.h, tok_ueq[(pj // 4, pj % 4)])
                tok_ue[(do, qq)] = dma(gg, p.DW2, uev[:, j % 4, :, :],
                                       ue_r[:, 8 * qq:8 * (qq + 1), 128 * do:128 * (do + 1)])

            @blk.tensor
            def _(t, do=do, qq=qq, j=j, psO=psO):
                wdma(t, tok_ue[(do, qq)])
                if qq == 0:
                    t.wait_ge(p.DV.h, tok_wt[NTL - 1])
                    if do >= 3:
                        t.wait_ge(p.DV.h, tok_uev[do - 3])
                last = None
                for cc in range(8):
                    c = 8 * qq + cc
                    for hh in range(2):
                        last = t.matmul(psO[:, 512 * hh:512 * (hh + 1)],
                                        uev[:, j % 4, cc, :],
                                        p.big[:, c, 512 * hh:512 * (hh + 1)],
                                        start=(c == 0), stop=(c == 31))
                a = p.TE.inc(last)
                assert a == tok_ueq[(do, qq)]

        @blk.vector
        def _(v, do=do, g=g, psO=psO):
            v.wait_ge(p.TE.h, tok_umm[do])
            tok_uev[do] = p.DV.inc(v.tensor_tensor(
                p.x[:, do, TG * g:TG * (g + 1)], psO[:],
                p.x[:, do, TG * g:TG * (g + 1)], ALU.add))

        if do == 1 and ln_fin is not None:
            p.next_tok_xn = ln_fin()
            ln_fin = None

    if ln_fin is not None:
        p.next_tok_xn = ln_fin()
    p.x_dv = tok_uev[7]
    p.psC_free = tok_uev[5]
    p.big_free = (p.TE, tok_umm[7])
    p.ring_free = (p.TE, tok_umm[7])
    p.ring2_free = (p.TE, tok_q2mm[3])
    p.ub_free = (p.TE, tok_dmm[7])
    p.p2b_last_te = tok_umm[7]
    p.tok_wt_last = tok_wt[NTL - 1]
    p.last_trp = trp_tok[NTL - 1]
    p.q2buf_free = [(p.TE.h, tok_umm[7]), (p.DV.h, tok_wt[NTL - 1])]
    p.simbuf_free = [(p.DV.h, tok_wt[NTL - 1]), (tok_st[7][0].h, tok_st[7][1])]
    p.scrG_free_l = [(p.DV.h, tok_wt[NTL - 1]), (tok_st[6][0].h, tok_st[6][1])]
    p.gT_free_l = [(p.DV.h, tok_wt[NTL - 1])]
    scope.__exit__(None, None, None)


def emit_attn(p):
    blk = p.blk
    scope = p.nc.named_scope("attn_proj")
    scope.__enter__()

    wkvv = p.q2.rearrange("p o t -> p (o t)")[:, 0:2048].rearrange("p (d n) -> p d n", d=CH)
    wkv_r = p.d_wkv.rearrange("(d p) n -> p d n", p=128)
    wq_r = p.d_wq.rearrange("(d p) n -> p d n", p=128)
    wqh = p.wr1[:, 0:2048].rearrange("p (b d n) -> p b d n", b=2, d=CH)
    t_wkv = [0]

    @blk.gpsimd
    def _(gg):
        gg.wait_ge(p.TE.h, p.p2b_last_te)
        gg.wait_ge(p.DV.h, p.x_dv)
        wl(gg, p.simbuf_free)
        wl(gg, p.tap_hold)
        p.t_pe = dma(gg, p.DW3, p.pe, p.d_pet)
        p.t_ident = dma(gg, p.DW3, p.ident, p.d_ident)
        p.t_stair = dma(gg, p.DW3, p.stair.rearrange("p a b -> p (a b)"), p.d_stair)
        t_wkv[0] = dma(gg, p.DW3, wkvv, wkv_r)

    kk = p.scrP[:, 0:2048]
    vv = p.scrP[:, 2048:4096]
    psT = p.psB[:].bitcast(BF16)
    tok_vt = [0] * 16

    def emit_vt(c):
        t_vtm = [0]

        @blk.tensor
        def _(t, c=c):
            t.wait_ge(p.DV.h, tok_kv[1])
            t.wait_ge(p.DW3.h, p.t_ident)
            if c > 0:
                t.wait_ge(p.DV.h, tok_vt[c - 1])
            t_vtm[0] = p.TE.inc(t.transpose(psT[:, 0:128], vv[:, 128 * c:128 * (c + 1)],
                                            p.ident[:]))

        @blk.vector
        def _(v, c=c):
            v.wait_ge(p.TE.h, t_vtm[0])
            if c == 0:
                v.wait_ge(p.DV.h, p.tok_wt_last)
            tok_vt[c] = p.DV.inc(v.tensor_copy(p.gT[:, c, :], psT[:, 0:128]))
    ln_scr = p.big[:, 24:27, :].rearrange("p a b -> p (a b)").bitcast(F32)
    tok_kv = [0, 0]
    t_kv_k = [0, 0]
    tok_qh = [[0] * 2 for _ in range(NHEAD)]
    t_qev = [0] * NHEAD
    t_kvev = [0]
    t_qmm = [0]
    p.t_wqh = [0] * 16
    p.t_qmm_j = [0] * 16
    for half in range(2):
        if half == 0 and p.next_tok_xn is not None:
            tok_xn = p.next_tok_xn
            p.next_tok_xn = None
        elif half == 0:
            tok_xn = emit_ln(p, TG * half, ln_scr, (p.TE, p.p2b_last_te))
        else:
            p.psD_free = (p.DV, t_kv_k[0])
            tok_xn = emit_ln(p, TG * half, ln_scr, (p.TE, p.p2b_last_te),
                             bank=p.psA)
        for j in range(2):
            psK = p.psA if j == 0 else p.psB

            @blk.tensor
            def _(t, j=j, half=half, psK=psK, tok_xn=tok_xn):
                t.wait_ge(p.DW3.h, t_wkv[0])
                t.wait_ge(p.DV.h, tok_xn)
                if half == 1:
                    t.wait_ge(p.DV.h, t_kvev[0])
                else:
                    t.wait_ge(p.DV.h, p.x_dv)
                last = None
                for dd in range(CH):
                    for hh in range(2):
                        last = t.matmul(psK[:, 512 * hh:512 * (hh + 1)],
                                        wkvv[:, dd, 128 * j:128 * (j + 1)],
                                        p.xn[:, dd, 512 * hh:512 * (hh + 1)],
                                        start=(dd == 0), stop=(dd == CH - 1))
                p.t_kvmm = p.TE.inc(last)

            @blk.vector
            def _(v, j=j, half=half, psK=psK):
                v.wait_ge(p.TE.h, p.t_kvmm)
                v.wait_ge(p.DW3.h, p.t_pe)
                dst = (kk if j == 0 else vv)[:, TG * half:TG * (half + 1)]
                if j == 0:
                    tok_kv[half] = p.DV.inc(v.scalar_tensor_tensor(
                        dst, psK[:], p.bias[:, 72:73], p.pe[:, TG * half:TG * (half + 1)],
                        ALU.add, ALU.add))
                    t_kv_k[half] = p.DV.n
                else:
                    tok_kv[half] = p.DV.inc(v.tensor_scalar_add(dst, psK[:], p.bias[:, 73:74]))
                t_kvev[0] = p.DV.n

        for h in range(NHEAD):
            psQ = p.psC if h % 2 == 0 else p.psD
            j = 8 * half + h

            @blk.gpsimd
            def _(gg, h=h, half=half, j=j):
                if j < 2:
                    gg.wait_ge(p.TE.h, p.p2b_last_te)
                else:
                    gg.wait_ge(p.TE.h, p.t_qmm_j[j - 2])
                p.t_wqh[j] = dma(gg, p.DW1, wqh[:, j % 2, :, :],
                                 wq_r[:, :, 128 * h:128 * (h + 1)])

            @blk.tensor
            def _(t, h=h, half=half, j=j, psQ=psQ, tok_xn=tok_xn):
                wdma(t, p.t_wqh[j])
                t.wait_ge(p.DV.h, tok_xn)
                prev = t_qev[h - 2] if h >= 2 else (t_qev[NHEAD - 2 + h] if half == 1 else 0)
                t.wait_ge(p.DV.h, prev)
                if half == 0 and h < 2:
                    t.wait_ge(p.DV.h, p.ln_psD_free_dv)
                last = None
                for dd in range(CH):
                    for hh in range(2):
                        last = t.matmul(psQ[:, 512 * hh:512 * (hh + 1)], wqh[:, j % 2, dd, :],
                                        p.xn[:, dd, 512 * hh:512 * (hh + 1)],
                                        start=(dd == 0), stop=(dd == CH - 1))
                t_qmm[0] = p.TE.inc(last)
                p.t_qmm_j[j] = t_qmm[0]

            @blk.vector
            def _(v, h=h, half=half, psQ=psQ):
                v.wait_ge(p.TE.h, t_qmm[0])
                v.wait_ge(p.DW3.h, p.t_pe)
                if half == 0 and h == 0:
                    v.wait_ge(p.TE.h, p.p2b_last_te)
                tok_qh[h][half] = p.DV.inc(v.scalar_tensor_tensor(
                    p.big[:, 2 * h + half, :], psQ[:], p.bias[:, 64 + h:65 + h],
                    p.pe[:, TG * half:TG * (half + 1)], ALU.add, ALU.add))
                t_qev[h] = p.DV.n

            if half == 1:
                emit_vt(2 * h)
                emit_vt(2 * h + 1)
        p.xn_free = (p.TE, t_qmm[0])

    scope.__exit__(None, None, None)
    scope = p.nc.named_scope("attn_heads")
    scope.__enter__()
    Ev = p.scrG[:].rearrange("p (s t) -> p s t", s=8)
    zr = p.rows[0:1, 0:512]
    zbs = p.gT[:, 16:20, :].rearrange("p a b -> p (a b)")
    tok_att = {-2: 0, -1: 0}
    t_zr = {-2: 0, -1: 0}
    t_zbs = {-2: 0, -1: 0}
    pv_tok = {}
    pv_last = {}
    exp_prev = [0, 0]
    t_zb = {}

    for h in range(NHEAD):
        for Q in range(4):
            nkc = 4 * Q + 4
            idx = 4 * h + Q
            npair = nkc // 2
            psOt = p.psA if idx % 2 == 0 else p.psB
            psO = psOt[:, 0:512]
            zrow = psOt[0:1, 512:1024]
            teI, acI, dvI = p.TE.n, p.AC.n, p.DV.n
            tok_exp = [acI + (kc // 2) + 1 for kc in range(nkc)]
            t_mask = dvI + 1
            t_sm_toks = []
            qblk = p.big[:, 2 * h + Q // 2, 512 * (Q % 2):512 * (Q % 2) + 512]

            def pv(t, kc, first, last_flag, idx=idx, psO=psO, zrow=zrow, nkc=nkc,
                   tok_exp=tok_exp, t_mask=t_mask):
                slot = kc % 8
                if kc >= nkc - 4:
                    t.wait_ge(p.DV.h, t_mask)
                else:
                    t.wait_ge(p.AC.h, tok_exp[kc])
                if first:
                    t.wait_ge(p.DV.h, tok_vt[15])
                    t.wait_ge(p.DV.h, t_zr[idx - 2])
                    t.wait_ge(p.DV.h, tok_att[idx - 2])
                    if idx < 2:
                        t.wait_ge(p.DV.h, t_qev[NHEAD - 1])
                t.matmul(zrow, p.onesb_c[:], Ev[:, slot, :], start=first, stop=last_flag)
                mm = t.matmul(psO, p.gT[:, kc, :], Ev[:, slot, :], start=first, stop=last_flag)
                pv_tok[slot] = p.TE.inc(mm)
                if last_flag:
                    pv_last[idx] = pv_tok[slot]

            @blk.tensor
            def _(t, h=h, Q=Q, idx=idx, nkc=nkc, npair=npair, qblk=qblk, tok_exp=tok_exp):
                t.wait_ge(p.DV.h, tok_kv[1])
                t.wait_ge(p.DV.h, tok_qh[h][Q // 2])
                pend_pv = []
                for pr in range(npair):
                    kc0 = 2 * pr
                    psS = p.psC if pr % 2 == 0 else p.psD
                    if kc0 >= 4:
                        t.wait_ge(p.AC.h, tok_exp[kc0 - 4])
                    else:
                        t.wait_ge(p.AC.h, exp_prev[pr % 2])
                        if idx == 0:
                            t.wait_ge(p.DV.h, t_qev[NHEAD - 1])
                    if pr % 2 == 1:
                        t.wait_ge(p.DV.h, t_zbs[idx - 2])
                    mm = None
                    for j in range(2):
                        kc = kc0 + j
                        mm = t.matmul(psS[:, 512 * j:512 * (j + 1)],
                                      kk[:, 128 * kc:128 * (kc + 1)], qblk,
                                      start=True, stop=True)
                    t_sm_toks.append(p.TE.inc(mm))
                    if kc0 + 1 < nkc - 4:
                        pend_pv.append(kc0)
                        if len(pend_pv) > 1:
                            k0 = pend_pv.pop(0)
                            pv(t, k0, first=(k0 == 0), last_flag=False)
                            pv(t, k0 + 1, first=False, last_flag=False)
                for k0 in pend_pv:
                    pv(t, k0, first=(k0 == 0), last_flag=False)
                    pv(t, k0 + 1, first=False, last_flag=False)
                for kc in range(nkc - 4, nkc):
                    pv(t, kc, first=(kc == 0), last_flag=(kc == nkc - 1))
                if idx >= 1:
                    t.wait_ge(p.DV.h, t_zr[idx - 1])
                    t.wait_ge(p.DV.h, t_zbs[idx - 2])
                    t.wait_ge(p.AC.h, tok_exp[2 * (npair - 1)])
                    t_zb[idx - 1] = p.TE.inc(t.matmul(p.psD[:, 0:512], p.ones_rf[:], zr,
                                                      start=True, stop=True))

            @blk.scalar
            def _(s, h=h, Q=Q, idx=idx, nkc=nkc, npair=npair, tok_exp=tok_exp,
                  t_sm_toks=t_sm_toks):
                for pr in range(npair):
                    kc0 = 2 * pr
                    slot = kc0 % 8
                    psS = p.psC if pr % 2 == 0 else p.psD
                    s.wait_ge(p.TE.h, t_sm_toks[pr])
                    for sl in (slot, slot + 1):
                        if sl in pv_tok and pv_tok[sl] < teI:
                            s.wait_ge(p.TE.h, pv_tok[sl])
                    if idx == 0 and kc0 == 0:
                        s.wait_ge(p.DT.h, p.last_trp)
                        s.wait_ge(p.DV.h, p.tok_wt_last)
                    a = p.AC.inc(s.activation(Ev[:, slot:slot + 2, :], psS[:, 0:1024],
                                              AF.Exp, scale=SCALE))
                    assert a == tok_exp[kc0]

            s0 = (nkc - 4) % 8

            @blk.vector
            def _(v, idx=idx, s0=s0, nkc=nkc, tok_exp=tok_exp, psO=psO, zrow=zrow):
                v.wait_ge(p.AC.h, tok_exp[nkc - 1])
                v.wait_ge(p.DW3.h, p.t_stair)
                a = p.DV.inc(v.tensor_mul(Ev[:, s0:s0 + 4, :], Ev[:, s0:s0 + 4, :],
                                          p.stair[:]))
                assert a == t_mask
                if idx >= 1:
                    v.wait_ge(p.TE.h, t_zb[idx - 1])
                    t1 = p.DV.inc(v.tensor_copy(zbs[:, 0:512], p.psD[:, 0:512]))
                    v.wait_ge(p.DV.h, t1)
                    t_zbs[idx - 1] = t1
                    pidx = idx - 1
                    ph, pQ = pidx // 4, pidx % 4
                    ppsO = (p.psA if pidx % 2 == 0 else p.psB)[:, 0:512]
                    tok_att[pidx] = p.DV.inc(v.tensor_mul(
                        p.big[:, 16 + 2 * ph + pQ // 2, 512 * (pQ % 2):512 * (pQ % 2) + 512],
                        ppsO, zbs[:, 0:512]))
                v.wait_ge(p.TE.h, pv_last[idx])
                t_zr[idx] = p.DV.inc(v.reciprocal(zr, zrow))

            exp_prev[0] = tok_exp[2 * (npair - 2)]
            exp_prev[1] = tok_exp[2 * (npair - 1)]

    lidx = 4 * NHEAD - 1

    @blk.tensor
    def _(t):
        t.wait_ge(p.DV.h, t_zr[lidx])
        t.wait_ge(p.DV.h, t_zbs[lidx - 1])
        t.wait_ge(p.AC.h, exp_prev[1])
        t_zb[lidx] = p.TE.inc(t.matmul(p.psD[:, 0:512], p.ones_rf[:], zr, start=True, stop=True))

    @blk.vector
    def _(v):
        v.wait_ge(p.TE.h, t_zb[lidx])
        t1 = p.DV.inc(v.tensor_copy(zbs[:, 0:512], p.psD[:, 0:512]))
        v.wait_ge(p.DV.h, t1)
        ph, pQ = lidx // 4, lidx % 4
        ppsO = (p.psA if lidx % 2 == 0 else p.psB)[:, 0:512]
        tok_att[lidx] = p.DV.inc(v.tensor_mul(
            p.big[:, 16 + 2 * ph + pQ // 2, 512 * (pQ % 2):512 * (pQ % 2) + 512],
            ppsO, zbs[:, 0:512]))
    scope.__exit__(None, None, None)
    scope = p.nc.named_scope("attn_wo")
    scope.__enter__()
    wov = p.wr1[:, 0:2048].rearrange("p (b d n) -> p b d n", b=2, d=CH)
    wo_r = p.d_wo.rearrange("(d p) n -> p d n", p=128)
    tok_wo = [0] * 8
    tok_omm = [0] * 8
    tok_oev = [0] * 16
    for oc in range(8):
        @blk.gpsimd
        def _(gg, oc=oc):
            if oc < 2:
                gg.wait_ge(p.TE.h, p.t_qmm_j[15])
            else:
                gg.wait_ge(p.TE.h, tok_omm[oc - 2])
            tok_wo[oc] = dma(gg, p.DW1, wov[:, oc % 2, :, :],
                             wo_r[:, :, 128 * oc:128 * (oc + 1)])
        for blkk in range(2):
            psX = p.psC if (2 * oc + blkk) % 2 == 0 else p.psD
            t_omm = [0]

            @blk.tensor
            def _(t, oc=oc, blkk=blkk, psX=psX):
                wdma(t, tok_wo[oc])
                t.wait_ge(p.DV.h, tok_att[4 * NHEAD - 1])
                j = 2 * oc + blkk
                if j >= 2:
                    t.wait_ge(p.DV.h, tok_oev[j - 2])
                last = None
                for hh in range(CH):
                    for q2_ in range(2):
                        last = t.matmul(psX[:, 512 * q2_:512 * (q2_ + 1)], wov[:, oc % 2, hh, :],
                                        p.big[:, 16 + 2 * hh + blkk, 512 * q2_:512 * (q2_ + 1)],
                                        start=(hh == 0), stop=(hh == CH - 1))
                t_omm[0] = p.TE.inc(last)
                tok_omm[oc] = t_omm[0]

            @blk.vector
            def _(v, oc=oc, blkk=blkk, psX=psX):
                v.wait_ge(p.TE.h, t_omm[0])
                if 2 * oc + blkk == 0:
                    v.wait_ge(p.DV.h, tok_att[4 * NHEAD - 1])
                tok_oev[2 * oc + blkk] = p.DV.inc(v.tensor_tensor(
                    p.x[:, oc, TG * blkk:TG * (blkk + 1)], psX[:],
                    p.x[:, oc, TG * blkk:TG * (blkk + 1)], ALU.add))
    scope.__exit__(None, None, None)
    p.x_dv = tok_oev[15]
    p.psC_free = tok_oev[14]
    p.psD_free = (p.DV, tok_oev[15])
    p.big_free = (p.TE, tok_omm[7])
    p.ring_free = (p.TE, tok_omm[7])
    p.ub_free = (p.TE, tok_omm[7])
    p.t_c4 = t_qev[NHEAD - 1]
    p.scr_free = (p.TE, tok_omm[7])
    p.q2buf_free = [(p.TE.h, tok_omm[7]), (p.DV.h, tok_oev[15])]
    p.scrG_free_l = [(p.TE.h, tok_omm[7])]
    p.gT_free_l = [(p.TE.h, tok_omm[7]), (p.DV.h, tok_oev[15])]
    p.simbuf_free = [(p.DV.h, tok_oev[15])]
    p.ring2_free = (p.DV, tok_oev[15])


def emit_store(p):
    blk = p.blk

    @blk.sync
    def _(s):
        s.wait_ge(p.DV.h, p.x_dv)
        for c in range(CH):
            dma(s, p.DS, p.d_out[128 * c:128 * (c + 1), :], p.x[:, c, :])
        s.wait_ge(p.DS.h, p.DS.n)
        s.wait_ge(p.DT.h, p.DT.n)


# version 24
# speedup vs baseline: 1.0047x; 1.0047x over previous
import math
import contextlib
import numpy as np
import concourse.bass as bass
import concourse.mybir as mybir
from concourse.bass_utils import run_bass_kernel_spmd
import concourse.dve_ops as _dve_ops
from concourse.dve_op import DveOp as _DveOp
from concourse.dve_spec import Spec as _Spec, Src0 as _Src0, Src1 as _Src1, C0 as _C0, \
    Zero as _Zero, select as _select
from concourse.dve_table_gen import dve_ver_for as _dve_ver_for

F32 = mybir.dt.float32
BF16 = mybir.dt.bfloat16
AF = mybir.ActivationFunctionType
ALU = mybir.AluOpType
AX = mybir.AxisListType

NCORE = 8
D = 1024
T = 2048
CH = 8
DC = 4096
NE = 4096
NH = 4
TG = 1024
NTL = 8
EPS = 1e-5
NHEAD = 8
SCALE = 1.0 / math.sqrt(128.0)

TAPS = {}


def _sel_add_ref(in0, in1, s0, s1, imm2):
    return np.where(in0 >= s0, in0, 0.0) + in1


def _register_dve_op(name, spec, ref):
    for o in _dve_ops.OPS:
        if o.name == name:
            return o
    ver = _dve_ver_for("TRN2")
    op = _DveOp(name, spec, subdim=False, uops_sha={}, reference=ref)
    try:
        op.compile(ver)
        sha = None
    except ValueError as e:
        import re as _re
        m = _re.search(r"\(%s: (\w+)" % ver, str(e))
        sha = m.group(1)
    if sha is not None:
        op = _DveOp(name, spec, subdim=False, uops_sha={ver: sha}, reference=ref)
        op.compile(ver)
    _dve_ops.OPS.append(op)
    _dve_ops._SUB_OPCODE_FOR_NAME[op.name] = (_dve_ops._CUSTOM_DVE_ROW_BASE
                                              + len(_dve_ops.OPS) - 1)
    _dve_ops.CUSTOM_DVE_SPECS[op.name] = spec
    _dve_ops.CUSTOM_DVE_REFS[op.name] = ref
    return op


XBAR_TRANSPOSE = True


def _host_consts():
    pos = np.arange(T, dtype=np.float64)[:, None]
    inv = np.exp(-math.log(10000.0) * np.arange(0, 128, 2, dtype=np.float64) / 128.0)
    ang = pos * inv[None, :]
    pe = np.zeros((T, 128), dtype=np.float64)
    pe[:, 0::2] = np.sin(ang)
    pe[:, 1::2] = np.cos(ang)
    pet = np.ascontiguousarray(pe.T.astype(np.float32))
    pp = np.arange(128)[:, None, None]
    cc = np.arange(4)[None, :, None]
    jj = np.arange(512)[None, None, :]
    stair = ((128 * cc + pp) <= jj).astype(np.float32).reshape(128, 4 * 512)
    ident = np.eye(128, dtype=np.float32)
    return pet, stair, ident


def kernel(**inp):
    pet, stair, ident = _host_consts()
    x = np.ascontiguousarray(inp["L0_attn_x"])
    com = {
        "wq": np.ascontiguousarray(inp["L0_attn_Wq"]),
        "wkv": np.ascontiguousarray(inp["L0_attn_Wkv"]),
        "wo": np.ascontiguousarray(inp["L0_attn_Wo"]),
        "wg": np.ascontiguousarray(inp["L0_ffn_Wg"]),
        "wu": np.ascontiguousarray(inp["L0_ffn_Wu"]),
        "wd": np.ascontiguousarray(inp["L0_ffn_Wd"]),
        "wqf": np.ascontiguousarray(inp["L0_ffn_Wq"]),
        "dte": np.ascontiguousarray(inp["L0_ffn_down_embed"].T),
        "ue": np.ascontiguousarray(inp["L0_ffn_up_embed"]),
        "keyst": np.ascontiguousarray(inp["L0_ffn_keys"].transpose(2, 0, 3, 1)),
        "bqc": np.ascontiguousarray(inp["L0_attn_bq"].reshape(8, 128).T),
        "bkvc": np.ascontiguousarray(inp["L0_attn_bkv"].reshape(2, 128).T),
        "bgc": np.ascontiguousarray(inp["L0_ffn_bg"].reshape(32, 128).T),
        "buc": np.ascontiguousarray(inp["L0_ffn_bu"].reshape(32, 128).T),
        "bdr": np.ascontiguousarray(inp["L0_ffn_bd"].reshape(1, 1024)),
        "pet": pet, "stair": stair, "ident": ident,
    }
    nc = build_nc()
    in_maps = []
    for b in range(NCORE):
        m = dict(com)
        m["xt"] = np.ascontiguousarray(x[b].T)
        in_maps.append(m)
    res = run_bass_kernel_spmd(nc, in_maps, core_ids=list(range(NCORE)))
    out = np.stack([np.ascontiguousarray(res.results[b]["out"].T) for b in range(NCORE)], axis=0)
    return out.astype(np.float32)


class Cnt:
    def __init__(self, ctx, nc, name):
        self.h = ctx.enter_context(nc.semaphore(name))
        self.n = 0

    def inc(self, instr, k=1):
        self.n += k
        instr.then_inc(self.h, k)
        return self.n


class LanePair:

    def __init__(self, ctx, nc, name):
        self.lanes = [Cnt(ctx, nc, name + "a"), Cnt(ctx, nc, name + "b")]
        self.i = 0

    def issue(self, eng, out, in_, **kw):
        lane = self.lanes[self.i % 2]
        self.i += 1
        eng.wait_ge(lane.h, lane.n)
        tok = lane.inc(eng.dma_start(out=out, in_=in_, **kw), 16)
        return (lane, tok)


class P:
    pass


def build_nc():
    nc = bass.Bass("TRN2", target_bir_lowering=False)
    p = P()
    p.nc = nc
    dt = nc.dram_tensor
    p.d_xt = dt("xt", [D, T], F32, kind="ExternalInput").ap()
    p.d_wq = dt("wq", [D, D], F32, kind="ExternalInput").ap()
    p.d_wkv = dt("wkv", [D, 256], F32, kind="ExternalInput").ap()
    p.d_wo = dt("wo", [D, D], F32, kind="ExternalInput").ap()
    p.d_wg = dt("wg", [D, DC], F32, kind="ExternalInput").ap()
    p.d_wu = dt("wu", [D, DC], F32, kind="ExternalInput").ap()
    p.d_wd = dt("wd", [DC, D], F32, kind="ExternalInput").ap()
    p.d_wqf = dt("wqf", [D, 512], F32, kind="ExternalInput").ap()
    p.d_dt = dt("dte", [D, NE], F32, kind="ExternalInput").ap()
    p.d_ue = dt("ue", [NE, D], F32, kind="ExternalInput").ap()
    p.d_keyst = dt("keyst", [2, NH, 64, 64], F32, kind="ExternalInput").ap()
    p.d_bqc = dt("bqc", [128, 8], F32, kind="ExternalInput").ap()
    p.d_bkvc = dt("bkvc", [128, 2], F32, kind="ExternalInput").ap()
    p.d_bgc = dt("bgc", [128, 32], F32, kind="ExternalInput").ap()
    p.d_buc = dt("buc", [128, 32], F32, kind="ExternalInput").ap()
    p.d_bdr = dt("bdr", [1, D], F32, kind="ExternalInput").ap()
    p.d_pet = dt("pet", [128, T], F32, kind="ExternalInput").ap()
    p.d_stair = dt("stair", [128, 4 * 512], F32, kind="ExternalInput").ap()
    p.d_ident = dt("ident", [128, 128], F32, kind="ExternalInput").ap()
    p.d_gd = dt("gspill", [2, NTL, 128, NE], BF16, kind="Internal").ap()
    p.d_out = dt("out", [D, T], F32, kind="ExternalOutput").ap()
    p.taps = {name: dt("tap_" + name, list(shape), dtp, kind="ExternalOutput").ap()
              for name, (shape, dtp) in TAPS.items()}

    ctx = contextlib.ExitStack()

    def al(name, shape, dtp):
        return ctx.enter_context(nc.sbuf_tensor(name, shape, dtp))

    p.x = al("x", [128, CH, T], F32)
    p.big = al("big", [128, 32, TG], BF16)
    p.xn = al("xn", [128, CH, TG], BF16)
    p.scrG = al("scrG", [128, NE], BF16)
    p.scrP = al("scrP", [128, NE], BF16)
    p.gT = al("gT", [128, 32, 128], BF16)
    p.wr1 = al("wr1", [128, 4096], BF16)
    p.wr2 = al("wr2", [128, 2048], BF16)
    p.cmpb = al("cmpb", [128, 896], BF16)
    p.simbuf = al("simbuf", [128, 4096], F32)
    p.s1v8 = al("s1v8", [128, 8, 8, 8], F32)
    p.e8all = al("e8all", [128, 8, 64], BF16)
    p.bias = al("bias", [128, 80], F32)
    p.q8 = al("q8", [128, NH, 8], F32)
    p.q8S = al("q8S", [128, NH, 8], BF16)
    p.q9S = al("q9S", [128, NH, 8], BF16)
    p.zal = al("zal", [128, 3 * NH], F32)
    p.ub = al("ub", [128, 128], BF16)
    p.ident = p.ub[:, :]
    p.bdrt = al("bdr_sb", [1, 1024], BF16)
    p.bdr = p.bdrt[0:1, :]
    p.ones_r = al("ones_r", [1, 512], BF16)
    p.ones_rf = al("ones_rf", [1, 128], F32)
    p.ones_c = al("ones_c", [128, 1], F32)
    p.onesb_c = al("onesb_c", [128, 1], BF16)
    p.eps_c = al("eps_c", [128, 1], F32)
    p.keysbd = al("keysbd", [128, 4, 128], BF16)
    p.sil = p.scrP[:].bitcast(F32)
    p.silb = p.scrP[:].rearrange("p (s n) -> p s n", s=4)
    p.pe = p.simbuf[:].bitcast(BF16)[:, 0:2048]
    p.rows = p.simbuf[0:1, 1024:2048]
    p.q2 = p.big[:, 28:32, :]
    p.stair = p.wr2[:].rearrange("p (a b) -> p a b", a=4)

    p.psA = ctx.enter_context(nc.psum_tensor("psA", [128, 1024], F32))
    p.psB = ctx.enter_context(nc.psum_tensor("psB", [128, 1024], F32))
    p.psC = ctx.enter_context(nc.psum_tensor("psC", [128, 1024], F32))
    p.psD = ctx.enter_context(nc.psum_tensor("psD", [128, 1024], F32))

    p.TE = Cnt(ctx, nc, "qTE")
    p.DV = Cnt(ctx, nc, "qDV")
    p.AC = Cnt(ctx, nc, "qAC")
    p.PO = Cnt(ctx, nc, "qPO")
    p.DW1 = LanePair(ctx, nc, "qDW1")
    p.DW2 = LanePair(ctx, nc, "qDW2")
    p.DW3 = Cnt(ctx, nc, "qDW3")
    p.DS = Cnt(ctx, nc, "qDS")
    p.DT = Cnt(ctx, nc, "qDT")
    p.DG = LanePair(ctx, nc, "qDG")
    p.x_dv = 0
    p.xn_free = (p.DV, 0)
    p.psD_free = (p.DV, 0)
    p.ln_psD_free_dv = 0
    p.last_trp = 0
    p.tok_wt_last = 0
    p.scr_free = (p.DV, 0)
    p.t_c4 = 0
    p.big_free = (p.TE, 0)
    p.ring_free = (p.TE, 0)
    p.ring2_free = (p.TE, 0)
    p.ub_free = (p.TE, 0)
    p.psC_free = 0
    p.q2buf_free = []
    p.simbuf_free = []
    p.scrG_free_l = []
    p.gT_free_l = []
    p.tap_hold = []
    p.next_tok_xn = None

    with nc.Block() as blk:
        p.blk = blk
        emit_init(p)
        emit_ffn(p)
        emit_attn(p)
        emit_ffn(p)
        emit_store(p)
    ctx.close()
    return nc


def dma(eng, lane, out, in_, **kw):
    if isinstance(lane, LanePair):
        return lane.issue(eng, out, in_, **kw)
    eng.wait_ge(lane.h, lane.n)
    return lane.inc(eng.dma_start(out=out, in_=in_, **kw), 16)


def wdma(eng, token):
    lane, tok = token
    eng.wait_ge(lane.h, tok)


def wl(eng, lst):
    for h, v in lst:
        eng.wait_ge(h, v)


def tap(p, name, ap_src, cnt, tok):
    if name not in p.taps:
        return None
    blk = p.blk
    out = [None]

    @blk.sync
    def _(s):
        s.wait_ge(cnt.h, tok)
        out[0] = dma(s, p.DS, p.taps[name], ap_src)
    return out[0]


def emit_init(p):
    blk = p.blk

    @blk.sync
    def _(s):
        for c in range(CH):
            dma(s, p.DS, p.x[:, c, :], p.d_xt[128 * c:128 * (c + 1), :])
    p.tok_xin = p.DS.n

    @blk.gpsimd
    def _(g):
        dma(g, p.DW3, p.bias[:, 0:32], p.d_bgc)
        dma(g, p.DW3, p.bias[:, 32:64], p.d_buc)
        dma(g, p.DW3, p.bias[:, 64:72], p.d_bqc)
        dma(g, p.DW3, p.bias[:, 72:74], p.d_bkvc)
        tz = p.PO.inc(g.memset(p.keysbd[:], 0.0))
        g.wait_ge(p.PO.h, tz)
        for c in range(4):
            pp, hlo = c // 2, (c % 2) * 2
            dma(g, p.DW3, p.keysbd[0:64, c, 0:64], p.d_keyst[pp, hlo, :, :])
            dma(g, p.DW3, p.keysbd[64:128, c, 64:128], p.d_keyst[pp, hlo + 1, :, :])
    p.tok_const = p.DW3.n

    @blk.vector
    def _(v):
        p.DV.inc(v.memset(p.ones_r[:], 1.0))
        p.DV.inc(v.memset(p.ones_rf[:], 1.0))
        p.DV.inc(v.memset(p.ones_c[:], 1.0))
        p.DV.inc(v.memset(p.onesb_c[:], 1.0))
        p.DV.inc(v.memset(p.eps_c[:], float(EPS)))
    p.tok_ones = p.DV.n


def emit_ln(p, t0, scratch, scr_free, defer=False, bank=None, xwait=None):
    blk = p.blk
    psb = bank if bank is not None else p.psD
    xnf, psDf = p.xn_free, p.psD_free
    x_dv = p.x_dv
    hold0 = list(p.tap_hold)
    fins = []
    prev_stats = [0]
    tok_rm_l = [0, 0]
    norm_prev = [0]
    for b in range(TG // 512):
        tb = t0 + 512 * b
        base = 1024 + 1024 * b
        mb = p.simbuf[0:1, base:base + 512]
        varb = p.simbuf[0:1, base + 512:base + 1024]
        r = varb
        mr = mb
        te0, ac0, dv0 = p.TE.n, p.AC.n, p.DV.n
        tok_sq = [ac0 + c + 1 for c in range(CH)]
        tok_sd = ac0 + CH + 1
        tok_sumx = te0 + 1
        tok_smm = [te0 + 1 + c + 1 for c in range(CH)]
        tok_stats = dv0 + 3
        tok_rm = dv0 + 5
        tok_rm_l[b] = tok_rm

        @blk.scalar
        def _(s, tb=tb, b=b, r=r, varb=varb, tok_sq=tok_sq, tok_smm=tok_smm,
              tok_stats=tok_stats, tok_sd=tok_sd):
            s.wait_ge(p.DS.h, p.tok_xin)
            if xwait is None:
                s.wait_ge(p.DV.h, x_dv)
            if b == 0:
                s.wait_ge(scr_free[0].h, scr_free[1])
                wl(s, hold0)
            for c in range(CH):
                if xwait is not None:
                    s.wait_ge(p.DV.h, xwait[c])
                if c >= 2:
                    s.wait_ge(p.TE.h, tok_smm[c - 2])
                a = p.AC.inc(s.activation(scratch[:, 512 * (c % 2):512 * (c % 2) + 512],
                                          p.x[:, c, tb:tb + 512], AF.Square))
                assert a == tok_sq[c]
            s.wait_ge(p.DV.h, tok_stats)
            assert p.AC.inc(s.activation(r, varb, AF.Sqrt, bias=p.eps_c[0:1, :])) == tok_sd

        @blk.tensor
        def _(t, tb=tb, b=b, tok_sq=tok_sq, tok_smm=tok_smm, tok_sumx=tok_sumx,
              pstats=prev_stats[0]):
            t.wait_ge(p.DS.h, p.tok_xin)
            t.wait_ge(p.DV.h, max(0 if xwait is not None else x_dv,
                                  p.tok_ones, p.ln_psD_free_dv, pstats))
            if b == 0:
                t.wait_ge(psDf[0].h, psDf[1])
            last = None
            for c in range(CH):
                if xwait is not None:
                    t.wait_ge(p.DV.h, xwait[c])
                last = t.matmul(psb[0:1, 0:512], p.ones_c[:], p.x[:, c, tb:tb + 512],
                                start=(c == 0), stop=(c == CH - 1))
            assert p.TE.inc(last) == tok_sumx
            for c in range(CH):
                t.wait_ge(p.AC.h, tok_sq[c])
                mm = t.matmul(psb[32:33, 0:512], p.ones_c[:],
                              scratch[:, 512 * (c % 2):512 * (c % 2) + 512],
                              start=(c == 0), stop=(c == CH - 1))
                assert p.TE.inc(mm) == tok_smm[c]

        @blk.vector
        def _(v, b=b, mb=mb, varb=varb, r=r, mr=mr, tok_smm=tok_smm,
              tok_stats=tok_stats, tok_sd=tok_sd, tok_rm=tok_rm):
            v.wait_ge(p.TE.h, tok_smm[CH - 1])
            t1 = p.DV.inc(v.tensor_scalar_mul(mb, psb[0:1, 0:512], 1.0 / D))
            v.wait_ge(p.DV.h, t1)
            t2 = p.DV.inc(v.tensor_mul(varb, mb, mb))
            v.wait_ge(p.DV.h, t2)
            t3 = p.DV.inc(v.scalar_tensor_tensor(varb, psb[32:33, 0:512], 1.0 / D, varb,
                                                 ALU.mult, ALU.subtract))
            assert t3 == tok_stats
            v.wait_ge(p.AC.h, tok_sd)
            t4 = p.DV.inc(v.reciprocal(r, r))
            v.wait_ge(p.DV.h, t4)
            t5 = p.DV.inc(v.tensor_mul(mr, mb, r))
            assert t5 == tok_rm
        prev_stats[0] = tok_stats

        def fin(b=b, tb=tb, r=r, mr=mr, tok_rm=tok_rm):
            t_bc = [0]

            @blk.tensor
            def _(t):
                t.wait_ge(p.DV.h, max(tok_rm, tok_rm_l[1], norm_prev[0]))
                t.matmul(psb[:, 0:512], p.ones_rf[:], r, start=True, stop=True)
                t_bc[0] = p.TE.inc(t.matmul(psb[:, 512:1024], p.ones_rf[:], mr,
                                            start=True, stop=True))

            @blk.vector
            def _(v):
                v.wait_ge(p.TE.h, t_bc[0])
                v.wait_ge(xnf[0].h, xnf[1])
                if b == 0:
                    wl(v, hold0)
                for c in range(CH):
                    ta = p.DV.inc(v.tensor_mul(scratch[:, 1024:1536], p.x[:, c, tb:tb + 512],
                                               psb[:, 0:512]))
                    v.wait_ge(p.DV.h, ta)
                    t6 = p.DV.inc(v.tensor_sub(p.xn[:, c, 512 * b:512 * (b + 1)],
                                               scratch[:, 1024:1536], psb[:, 512:1024]))
                    v.wait_ge(p.DV.h, t6)
            norm_prev[0] = p.DV.n
            return p.DV.n
        fins.append(fin)

    def finish():
        tok = 0
        for f in fins:
            tok = f()
        p.ln_psD_free_dv = tok
        p.psD_free = (p.DV, tok)
        return tok
    if defer:
        return finish
    return finish()


FFN_CALL = [0]


def emit_ffn(p):
    call = FFN_CALL[0]
    FFN_CALL[0] += 1
    for g in range(2):
        if p.next_tok_xn is not None:
            tok_xn = p.next_tok_xn
            p.next_tok_xn = None
        else:
            xw = None
            if call == 1 and g == 0 and hasattr(p, "attn_oev_l"):
                xw = [p.attn_oev_l[2 * c] for c in range(CH)]
            with p.nc.named_scope(f"f{call}g{g}_ln"):
                tok_xn = emit_ln(p, TG * g, p.sil, p.scr_free, xwait=xw)
        tk = tap(p, f"xn{call}_{g}", p.xn[:].rearrange("p c t -> p (c t)"), p.DV, tok_xn)
        if tk:
            p.tap_hold.append((p.DS.h, tk))
        if g == 0:
            def nl():
                with p.nc.named_scope(f"f{call}g1_ln"):
                    return emit_ln(p, TG, p.sil, p.scr_free, defer=True)
        elif call == 0:
            def nl():
                with p.nc.named_scope("fa0_ln"):
                    return emit_ln(p, 0, p.sil, p.scr_free, defer=True)
        else:
            nl = None
        emit_group(p, g, tok_xn, nl)


def emit_group(p, g, tok_xn, next_ln=None):
    blk = p.blk
    scope = p.nc.named_scope(f"c{FFN_CALL[0]}g{g}_grp")
    scope.__enter__()
    hold0 = list(p.tap_hold)

    te0, ac0, dv0 = p.TE.n, p.AC.n, p.DV.n
    tok_q2mm = [te0 + oc + 1 for oc in range(4)]
    tok_sim = [te0 + 4 + tt + 1 for tt in range(NTL)]
    tok_Gmm = [te0 + 12 + 2 * c + 1 for c in range(32)]
    tok_Umm = [te0 + 12 + 2 * c + 2 for c in range(32)]
    tok_dmid = [te0 + 76 + 2 * do + 1 for do in range(8)]
    tok_dmm = [te0 + 76 + 2 * do + 2 for do in range(8)]
    tok_hdm = [te0 + 92 + ec + 1 for ec in range(32)]
    tok_q2cp = [ac0 + oc + 1 for oc in range(4)]
    tok_scp = [ac0 + 4 + tt + 1 for tt in range(NTL)]
    t_exn = ac0 + 14
    t_e8 = ac0 + 15
    tok_sil = [ac0 + 15 + 2 * c + 1 for c in range(32)]
    tok_ucp = [ac0 + 15 + 2 * c + 2 for c in range(32)]
    tok_hds = [ac0 + 79 + ec + 1 for ec in range(32)]
    po0 = p.PO.n
    tok_Ap = [po0 + j + 1 for j in range(16)]
    t_max_last = dv0 + 64
    t_mask = dv0 + 65
    t_fneg = dv0 + 66
    assert p.DG.i % 2 == 0
    dg0 = [p.DG.lanes[0].n, p.DG.lanes[1].n]
    tok_st = [(p.DG.lanes[tt % 2], dg0[tt % 2] + 16 * (tt // 2 + 1)) for tt in range(NTL)]

    wgv4 = p.wr1[:].rearrange("p (b d n) -> p b d n", b=4, d=CH)
    wqv = p.wr2[:].rearrange("p (d n) -> p d n", d=CH)
    wdv = p.wr1[:].rearrange("p (b c n) -> p b c n", b=2, c=16)
    dtv = p.wr1[:].rearrange("p (b d n) -> p b d n", b=4, d=CH)
    wg_r = p.d_wg.rearrange("(d p) n -> p d n", p=128)
    wu_r = p.d_wu.rearrange("(d p) n -> p d n", p=128)
    wqf_r = p.d_wqf.rearrange("(d p) n -> p d n", p=128)
    wd_r = p.d_wd.rearrange("(c p) n -> p c n", p=128)
    dt_r = p.d_dt.rearrange("(d p) n -> p d n", p=128)
    sbb = p.simbuf[:].bitcast(BF16)
    fnall = sbb[:, 0:4096]
    gacc2 = sbb[:, 4096:8192]
    tmp = p.gT[:].rearrange("p c t -> p (c t)")
    Z, alpha, thresh = p.zal[:, 0:4], p.zal[:, 4:8], p.zal[:, 8:12]
    qc = p.cmpb[:, 0:256].rearrange("p (h k) -> p h k", h=NH)
    qS = p.cmpb[:, 256:512].rearrange("p (h k) -> p h k", h=NH)
    qtmp = p.cmpb[:, 512:576]
    fxs = p.cmpb[:, 576:832].rearrange("p (h n) -> p h n", h=NH)
    e8xs = p.cmpb[:, 832:864].rearrange("p (h k) -> p h k", h=NH)

    t_wg = [None] * 32
    t_wu = [None] * 32
    t_wd = [None] * 16
    t_dts = [None] * 32
    t_wqf = [None, None]

    @blk.gpsimd
    def _(gg):
        gg.wait_ge(p.ring_free[0].h, p.ring_free[1])
        for c in range(2):
            t_wg[c] = dma(gg, p.DW1, wgv4[:, c % 2, :, :], wg_r[:, :, 128 * c:128 * (c + 1)])
            t_wu[c] = dma(gg, p.DW1, wgv4[:, 2 + c % 2, :, :],
                          wu_r[:, :, 128 * c:128 * (c + 1)])
        gg.wait_ge(p.ub_free[0].h, p.ub_free[1])
        p.t_bdr = dma(gg, p.DW3, p.bdr, p.d_bdr)
        gg.wait_ge(p.ring2_free[0].h, p.ring2_free[1])
        t_wqf[0] = dma(gg, p.DW2, wqv[:, :, :], wqf_r[:, :, 0:256])
        gg.wait_ge(p.TE.h, tok_q2mm[1])
        t_wqf[1] = dma(gg, p.DW2, wqv[:, :, :], wqf_r[:, :, 256:512])

        def a_pair(j):
            gg.wait_ge(p.AC.h, tok_ucp[2 * j + 1])
            a = p.PO.inc(gg.tensor_tensor(
                p.big[:, 2 * j:2 * j + 2, :], p.big[:, 2 * j:2 * j + 2, :],
                p.silb[:, (2 * j) % 4:(2 * j) % 4 + 2, :], ALU.mult))
            assert a == tok_Ap[j]

        for c in range(2, 32):
            gg.wait_ge(p.TE.h, tok_Umm[c - 2])
            t_wg[c] = dma(gg, p.DW1, wgv4[:, c % 2, :, :], wg_r[:, :, 128 * c:128 * (c + 1)])
            t_wu[c] = dma(gg, p.DW1, wgv4[:, 2 + c % 2, :, :],
                          wu_r[:, :, 128 * c:128 * (c + 1)])
            if c % 2 == 1:
                a_pair((c - 3) // 2)
        gg.wait_ge(p.TE.h, tok_Umm[31])
        t_wd[0] = dma(gg, p.DW1, wdv[:, 0, :, :], wd_r[:, 0:16, 0:128])
        t_wd[1] = dma(gg, p.DW1, wdv[:, 1, :, :], wd_r[:, 16:32, 0:128])
        a_pair(15)
        for sl in range(2, 16):
            do, hf = sl // 2, sl % 2
            gg.wait_ge(p.TE.h, tok_dmid[do - 1] if hf == 0 else tok_dmm[do - 1])
            t_wd[sl] = dma(gg, p.DW1, wdv[:, sl % 2, :, :],
                           wd_r[:, 16 * hf:16 * (hf + 1), 128 * do:128 * (do + 1)])
        for ec in range(32):
            gg.wait_ge(p.TE.h, tok_dmm[7] if ec < 4 else tok_hdm[ec - 4])
            t_dts[ec] = dma(gg, p.DW1, dtv[:, ec % 4, :, :],
                            dt_r[:, :, 128 * ec:128 * (ec + 1)])

    tok_ev = [None] * 8
    t_G = [None] * NTL
    sg = [p.scrG[:], gacc2]

    def emit_p1b_ev(v, do):
        v.wait_ge(p.TE.h, tok_dmm[do])
        psO = [p.psA, p.psB, p.psC, p.psD][do % 4]
        tok_ev[do] = p.DV.inc(v.scalar_tensor_tensor(
            p.x[:, do, TG * g:TG * (g + 1)], psO[:], 0.5,
            p.x[:, do, TG * g:TG * (g + 1)], ALU.mult, ALU.add))

    def tile_units(tt):
        acc = sg[tt % 2]
        fn_ = fnall[:, 512 * tt:512 * (tt + 1)]
        fnx = fn_[:, 0:256].rearrange("p (h n) -> p h n", h=NH)
        fny = fn_[:, 256:512].rearrange("p (h n) -> p h n", h=NH)
        e8x = p.e8all[:, tt, 0:32].rearrange("p (h k) -> p h k", h=NH)
        e8y = p.e8all[:, tt, 32:64].rearrange("p (h k) -> p h k", h=NH)

        def c1(v):
            v.wait_ge(p.AC.h, t_e8)
            t = p.DV.inc(v.tensor_tensor(qc, e8x.unsqueeze(3).to_broadcast([128, NH, 8, 8]),
                                         e8y.unsqueeze(2).to_broadcast([128, NH, 8, 8]),
                                         ALU.mult))
            v.wait_ge(p.DV.h, t)
            for h in range(NH):
                p.DV.inc(v.max(out=p.q8[:, h, :], in_=qc[:, h, :]))
            v.wait_ge(p.DV.h, p.DV.n)
            t = p.DV.inc(v.reduce_sum(Z, p.q8[:], axis=AX.X))
            v.wait_ge(p.DV.h, t)
            t = p.DV.inc(v.reciprocal(alpha, Z))
            v.wait_ge(p.DV.h, t)
            t = p.DV.inc(v.tensor_scalar_mul(alpha, alpha, 0.5))
            v.wait_ge(p.DV.h, t)

        def c2(v):
            p.DV.inc(v.tensor_tensor(fxs, fnx,
                                     alpha.unsqueeze(2).to_broadcast([128, NH, 64]), ALU.mult))
            t = p.DV.inc(v.tensor_tensor(e8xs, e8x,
                                         alpha.unsqueeze(2).to_broadcast([128, NH, 8]), ALU.mult))
            v.wait_ge(p.DV.h, t)
            t = p.DV.inc(v.tensor_tensor(qS, e8xs.unsqueeze(3).to_broadcast([128, NH, 8, 8]),
                                         e8y.unsqueeze(2).to_broadcast([128, NH, 8, 8]), ALU.mult))
            v.wait_ge(p.DV.h, t)

        def c3(v):
            for h in range(NH):
                p.DV.inc(v.max(out=p.q8S[:, h, :], in_=qS[:, h, :]))
                v.wait_ge(p.DV.h, p.DV.n)
                p.DV.inc(v.match_replace(out=qtmp, in_to_replace=p.q8S[:, h, :],
                                         in_values=qS[:, h, :], imm_value=-1.0))
                v.wait_ge(p.DV.h, p.DV.n)
                p.DV.inc(v.max(out=p.q9S[:, h, :], in_=qtmp))
            v.wait_ge(p.DV.h, p.DV.n)
            t = p.DV.inc(v.tensor_tensor(thresh, p.q8S[:, :, 7], p.q9S[:, :, 0], ALU.add))
            v.wait_ge(p.DV.h, t)
            t = p.DV.inc(v.tensor_scalar_mul(thresh, thresh, 0.5))
            v.wait_ge(p.DV.h, t)

        def mk_b(h):
            def b_(v):
                dst = acc if h == 0 else tmp
                if h == 0 and tt >= 2:
                    wdma(v, tok_st[tt - 2])
                t = p.DV.inc(v.tensor_tensor(
                    dst.rearrange("p (a b) -> p a b", a=64),
                    fxs[:, h, :].unsqueeze(2).to_broadcast([128, 64, 64]),
                    fny[:, h, :].unsqueeze(1).to_broadcast([128, 64, 64]), ALU.mult))
                v.wait_ge(p.DV.h, t)
            return b_

        def mk_s(h):
            def s_(v):
                dst = acc if h == 0 else tmp
                t = p.DV.inc(v.scalar_tensor_tensor(dst, dst, thresh[:, h:h + 1], dst,
                                                    ALU.is_ge, ALU.mult))
                v.wait_ge(p.DV.h, t)
            return s_

        def mk_a():
            def a_(v):
                t = p.DV.inc(v.tensor_add(acc, acc, tmp))
                v.wait_ge(p.DV.h, t)
            return a_

        def fin(v):
            t_G[tt] = p.DV.n

        us = [(2.2, c1), (1.1, c2), (2.8, c3)]
        for h in range(NH):
            us.append((4.4, mk_b(h)))
            us.append((4.5, mk_s(h)))
            if h > 0:
                us.append((2.3, mk_a()))
        us.append((0.0, fin))
        return us

    units = []
    for tt in range(NTL):
        units += tile_units(tt)
    inserts = [(225.0 + 15.7 * do, 1, do) for do in range(8)]

    @blk.vector
    def _(v):
        wl(v, hold0)
        for tt in range(NTL):
            v.wait_ge(p.AC.h, tok_scp[tt])
            for ph in range(8):
                p.DV.inc(v.max(out=p.s1v8[:, tt, ph, :],
                               in_=p.simbuf[:, 512 * tt + 64 * ph:512 * tt + 64 * (ph + 1)]))
        assert p.DV.n == t_max_last
        wl(v, p.scrG_free_l)
        v.wait_ge(p.DV.h, t_max_last)
        a = p.DV.inc(v.tensor_tensor(
            p.scrG[:].rearrange("p (t h n) -> p t h n", t=NTL, h=8),
            p.simbuf[:].rearrange("p (t h n) -> p t h n", t=NTL, h=8),
            p.s1v8[:, :, :, 7:8].to_broadcast([128, NTL, 8, 64]), ALU.is_ge))
        assert a == t_mask
        v.wait_ge(p.DV.h, t_mask)
        v.wait_ge(p.AC.h, t_exn)
        a = p.DV.inc(v.tensor_mul(fnall, p.scrG[:], tmp))
        assert a == t_fneg
        v.wait_ge(p.DV.h, t_fneg)
        clock = 0.0
        ii = 0
        for dur, fn in units:
            while ii < len(inserts) and inserts[ii][0] <= clock:
                emit_p1b_ev(v, inserts[ii][2])
                clock += 1.2
                ii += 1
            fn(v)
            clock += dur
        while ii < len(inserts):
            emit_p1b_ev(v, inserts[ii][2])
            ii += 1

    @blk.sync
    def _(s):
        for tt in range(NTL):
            s.wait_ge(p.DV.h, t_G[tt])
            got = dma(s, p.DG, p.d_gd[g % 2, tt], sg[tt % 2])
            assert got == tok_st[tt]

    for tt in range(NTL):
        gname = f"G{FFN_CALL[0]}_{g}_{tt}"
        if gname in p.taps:
            @blk.sync
            def _(s, tt=tt, gname=gname):
                wdma(s, tok_st[tt])
                tk = dma(s, p.DS, p.taps[gname], p.d_gd[g % 2, tt])
                p.tap_hold.append((p.DS.h, tk))
    tokA_tap = tap(p, f"A{FFN_CALL[0]}_{g}", p.big[:].rearrange("p c t -> p (c t)"),
                   p.PO, tok_Ap[15])

    @blk.tensor
    def _(t):
        t.wait_ge(p.DV.h, max(tok_xn, p.tok_ones, p.psC_free))
        t.wait_ge(p.DW3.h, p.tok_const)
        for oc in range(4):
            pair, k = oc // 2, oc % 2
            if k == 0:
                wdma(t, t_wqf[pair])
            psQ = p.psC if oc % 2 == 0 else p.psD
            if oc >= 2:
                t.wait_ge(p.AC.h, tok_q2cp[oc - 2])
            last = None
            for dd in range(CH):
                for hh in range(2):
                    last = t.matmul(psQ[:, 512 * hh:512 * (hh + 1)],
                                    wqv[:, dd, 128 * k:128 * (k + 1)],
                                    p.xn[:, dd, 512 * hh:512 * (hh + 1)],
                                    start=(dd == 0), stop=(dd == CH - 1))
            assert p.TE.inc(last) == tok_q2mm[oc]
        for tt in range(NTL):
            t.wait_ge(p.AC.h, tok_q2cp[3] if tt < 2 else tok_scp[tt - 2])
            off = 512 * (tt % 2)
            last = None
            for c in range(4):
                last = t.matmul(p.psD[:, off + 128 * c:off + 128 * (c + 1)],
                                p.q2[:, c, 128 * tt:128 * (tt + 1)],
                                p.keysbd[:, c, :], start=True, stop=True)
            assert p.TE.inc(last) == tok_sim[tt]
        for c in range(32):
            wdma(t, t_wu[c])
            wdma(t, t_wg[c])
            if c == 0:
                t.wait_ge(p.DV.h, p.x_dv)
            if c == 1:
                t.wait_ge(p.AC.h, tok_scp[7])
            psG = p.psA if c % 2 == 0 else p.psC
            psU = p.psB if c % 2 == 0 else p.psD
            if c >= 2:
                t.wait_ge(p.AC.h, tok_ucp[c - 2])
            last = None
            for dd in range(CH):
                for hh in range(2):
                    last = t.matmul(psG[:, 512 * hh:512 * (hh + 1)],
                                    wgv4[:, c % 2, dd, :],
                                    p.xn[:, dd, 512 * hh:512 * (hh + 1)],
                                    start=(dd == 0), stop=(dd == CH - 1))
            assert p.TE.inc(last) == tok_Gmm[c]
            for dd in range(CH):
                for hh in range(2):
                    last = t.matmul(psU[:, 512 * hh:512 * (hh + 1)],
                                    wgv4[:, 2 + c % 2, dd, :],
                                    p.xn[:, dd, 512 * hh:512 * (hh + 1)],
                                    start=(dd == 0), stop=(dd == CH - 1))
            assert p.TE.inc(last) == tok_Umm[c]
        t.wait_ge(p.PO.h, tok_Ap[15])
        t.wait_ge(p.DW3.h, p.t_bdr)
        psOs = [p.psA, p.psB, p.psC, p.psD]
        for do in range(8):
            psO = psOs[do % 4]
            wdma(t, t_wd[2 * do])
            if do >= 4:
                t.wait_ge(p.DV.h, tok_ev[do - 4])
            last = None
            for c in range(16):
                for hh in range(2):
                    last = t.matmul(psO[:, 512 * hh:512 * (hh + 1)],
                                    wdv[:, 0, c, :],
                                    p.big[:, c, 512 * hh:512 * (hh + 1)],
                                    start=(c == 0), stop=False)
            assert p.TE.inc(last) == tok_dmid[do]
            wdma(t, t_wd[2 * do + 1])
            for c in range(16, 32):
                for hh in range(2):
                    t.matmul(psO[:, 512 * hh:512 * (hh + 1)],
                             wdv[:, 1, c % 16, :],
                             p.big[:, c, 512 * hh:512 * (hh + 1)],
                             start=False, stop=False)
            last = None
            for hh in range(2):
                last = t.matmul(psO[:, 512 * hh:512 * (hh + 1)],
                                p.bdr[0:1, 128 * do:128 * (do + 1)], p.ones_r[:],
                                start=False, stop=True)
            assert p.TE.inc(last) == tok_dmm[do]
        t.wait_ge(p.DV.h, tok_ev[7])
        for ec in range(32):
            wdma(t, t_dts[ec])
            psH = p.psA if ec % 2 == 0 else p.psB
            if ec >= 2:
                t.wait_ge(p.AC.h, tok_hds[ec - 2])
            last = None
            for dd in range(CH):
                for hh in range(2):
                    last = t.matmul(psH[:, 512 * hh:512 * (hh + 1)], dtv[:, ec % 4, dd, :],
                                    p.xn[:, dd, 512 * hh:512 * (hh + 1)],
                                    start=(dd == 0), stop=(dd == CH - 1))
            assert p.TE.inc(last) == tok_hdm[ec]

    @blk.scalar
    def _(s):
        wl(s, p.q2buf_free)
        wl(s, p.simbuf_free)
        wl(s, hold0)
        s.wait_ge(p.DW3.h, p.tok_const)
        for oc in range(4):
            psQ = p.psC if oc % 2 == 0 else p.psD
            s.wait_ge(p.TE.h, tok_q2mm[oc])
            a = p.AC.inc(s.activation(p.q2[:, oc, :], psQ[:], AF.Copy))
            assert a == tok_q2cp[oc]
        for tt in range(NTL):
            off = 512 * (tt % 2)
            s.wait_ge(p.TE.h, tok_sim[tt])
            a = p.AC.inc(s.activation(p.simbuf[:, 512 * tt:512 * (tt + 1)],
                                      p.psD[:, off:off + 512], AF.Copy))
            assert a == tok_scp[tt]
        s.wait_ge(p.AC.h, tok_scp[7])
        wl(s, p.gT_free_l)
        p.AC.inc(s.activation(tmp[:, 0:2048], p.simbuf[:, 0:2048], AF.Exp))
        a = p.AC.inc(s.activation(tmp[:, 2048:4096], p.simbuf[:, 2048:4096], AF.Exp))
        assert a == t_exn
        s.wait_ge(p.DV.h, t_max_last)
        a = p.AC.inc(s.activation(p.e8all[:].rearrange("p a b -> p (a b)"),
                                  p.s1v8[:].rearrange("p a b c -> p (a b c)"), AF.Exp))
        assert a == t_e8
        for c in range(32):
            psG = p.psA if c % 2 == 0 else p.psC
            psU = p.psB if c % 2 == 0 else p.psD
            s.wait_ge(p.TE.h, tok_Gmm[c])
            if c >= 4:
                s.wait_ge(p.PO.h, tok_Ap[(c - 4) // 2])
            a = p.AC.inc(s.activation(p.silb[:, c % 4, :],
                                      psG[:], AF.Silu, bias=p.bias[:, c:c + 1]))
            assert a == tok_sil[c]
            s.wait_ge(p.TE.h, tok_Umm[c])
            a = p.AC.inc(s.activation(p.big[:, c, :], psU[:], AF.Identity,
                                      bias=p.bias[:, 32 + c:33 + c]))
            assert a == tok_ucp[c]
        if tokA_tap:
            s.wait_ge(p.DS.h, tokA_tap)
        for ec in range(32):
            psH = p.psA if ec % 2 == 0 else p.psB
            s.wait_ge(p.TE.h, tok_hdm[ec])
            a = p.AC.inc(s.activation(p.big[:, ec, :], psH[:], AF.Silu))
            assert a == tok_hds[ec]

    dests = [p.gT[:], p.scrG[:].rearrange("p (c t) -> p c t", c=32),
             gacc2.rearrange("p (c t) -> p c t", c=32)]
    trp_tok = [None] * NTL
    tok_wt = [None] * NTL
    for tt in range(NTL):
        @blk.sync
        def _(s, tt=tt):
            wdma(s, tok_st[tt])
            if tt == 0:
                s.wait_ge(p.DV.h, t_G[NTL - 1])
            elif tt == 1:
                wdma(s, tok_st[6])
            elif tt == 2:
                wdma(s, tok_st[7])
            else:
                s.wait_ge(p.DV.h, tok_wt[tt - 3])
            s.wait_ge(p.DT.h, p.DT.n)
            trp_tok[tt] = p.DT.inc(s.dma_start_transpose(dests[tt % 3], p.d_gd[g % 2, tt]), 16)

        @blk.vector
        def _(v, tt=tt):
            v.wait_ge(p.AC.h, tok_hds[31])
            v.wait_ge(p.DT.h, trp_tok[tt])
            sl = p.big[:, :, 128 * tt:128 * (tt + 1)]
            tok_wt[tt] = p.DV.inc(v.tensor_mul(sl, sl, dests[tt % 3]))

    tk = tap(p, f"WT{FFN_CALL[0]}_{g}", p.big[:].rearrange("p c t -> p (c t)"),
             p.DV, tok_wt[NTL - 1])
    if tk:
        p.tap_hold.append((p.DS.h, tk))

    p.psD_free = (p.DV, tok_ev[7])
    p.scr_free = (p.PO, tok_Ap[15])
    p.xn_free = (p.TE, tok_hdm[31])
    ln_fin = next_ln() if next_ln is not None else None

    t_p2b0 = p.TE.n
    assert t_p2b0 == te0 + 124 + (18 if next_ln is not None else 0)
    bump = 2 if next_ln is not None else 0
    tok_ueq = {(do, qq): t_p2b0 + 4 * do + qq + 1 + (bump if do >= 2 else 0)
               for do in range(8) for qq in range(4)}
    tok_umm = [tok_ueq[(do, 3)] for do in range(8)]

    uev = p.wr1[:].rearrange("p (b c n) -> p b c n", b=4, c=8)
    ue_r = p.d_ue.rearrange("(c p) n -> p c n", p=128)
    tok_ue = {}
    tok_uev = [0] * 8
    for do in range(8):
        psO = [p.psA, p.psB, p.psC][do % 3]
        for qq in range(4):
            j = 4 * do + qq

            @blk.gpsimd
            def _(gg, do=do, qq=qq, j=j):
                if j < 4:
                    gg.wait_ge(p.TE.h, tok_hdm[28 + j])
                else:
                    pj = j - 4
                    gg.wait_ge(p.TE.h, tok_ueq[(pj // 4, pj % 4)])
                tok_ue[(do, qq)] = dma(gg, p.DW2, uev[:, j % 4, :, :],
                                       ue_r[:, 8 * qq:8 * (qq + 1), 128 * do:128 * (do + 1)])

            @blk.tensor
            def _(t, do=do, qq=qq, j=j, psO=psO):
                wdma(t, tok_ue[(do, qq)])
                if qq == 0:
                    t.wait_ge(p.DV.h, tok_wt[NTL - 1])
                    if do >= 3:
                        t.wait_ge(p.DV.h, tok_uev[do - 3])
                last = None
                for cc in range(8):
                    c = 8 * qq + cc
                    for hh in range(2):
                        last = t.matmul(psO[:, 512 * hh:512 * (hh + 1)],
                                        uev[:, j % 4, cc, :],
                                        p.big[:, c, 512 * hh:512 * (hh + 1)],
                                        start=(c == 0), stop=(c == 31))
                a = p.TE.inc(last)
                assert a == tok_ueq[(do, qq)]

        @blk.vector
        def _(v, do=do, g=g, psO=psO):
            v.wait_ge(p.TE.h, tok_umm[do])
            tok_uev[do] = p.DV.inc(v.tensor_tensor(
                p.x[:, do, TG * g:TG * (g + 1)], psO[:],
                p.x[:, do, TG * g:TG * (g + 1)], ALU.add))

        if do == 1 and ln_fin is not None:
            p.next_tok_xn = ln_fin()
            ln_fin = None

    if ln_fin is not None:
        p.next_tok_xn = ln_fin()
    p.x_dv = tok_uev[7]
    p.psC_free = tok_uev[5]
    p.big_free = (p.TE, tok_umm[7])
    p.ring_free = (p.TE, tok_umm[7])
    p.ring2_free = (p.TE, tok_q2mm[3])
    p.ub_free = (p.TE, tok_dmm[7])
    p.p2b_last_te = tok_umm[7]
    p.tok_wt_last = tok_wt[NTL - 1]
    p.last_trp = trp_tok[NTL - 1]
    p.q2buf_free = [(p.TE.h, tok_umm[7]), (p.DV.h, tok_wt[NTL - 1])]
    p.simbuf_free = [(p.DV.h, tok_wt[NTL - 1]), (tok_st[7][0].h, tok_st[7][1])]
    p.scrG_free_l = [(p.DV.h, tok_wt[NTL - 1]), (tok_st[6][0].h, tok_st[6][1])]
    p.gT_free_l = [(p.DV.h, tok_wt[NTL - 1])]
    scope.__exit__(None, None, None)


def emit_attn(p):
    blk = p.blk
    scope = p.nc.named_scope("attn_proj")
    scope.__enter__()

    wkvv = p.q2.rearrange("p o t -> p (o t)")[:, 0:2048].rearrange("p (d n) -> p d n", d=CH)
    wkv_r = p.d_wkv.rearrange("(d p) n -> p d n", p=128)
    wq_r = p.d_wq.rearrange("(d p) n -> p d n", p=128)
    wqh = p.wr1[:, 0:2048].rearrange("p (b d n) -> p b d n", b=2, d=CH)
    t_wkv = [0]

    @blk.gpsimd
    def _(gg):
        gg.wait_ge(p.TE.h, p.p2b_last_te)
        gg.wait_ge(p.DV.h, p.x_dv)
        wl(gg, p.simbuf_free)
        wl(gg, p.tap_hold)
        p.t_pe = dma(gg, p.DW3, p.pe, p.d_pet)
        p.t_ident = dma(gg, p.DW3, p.ident, p.d_ident)
        p.t_stair = dma(gg, p.DW3, p.stair.rearrange("p a b -> p (a b)"), p.d_stair)
        t_wkv[0] = dma(gg, p.DW3, wkvv, wkv_r)

    kk = p.scrP[:, 0:2048]
    vv = p.scrP[:, 2048:4096]
    psT = p.psB[:].bitcast(BF16)
    tok_vt = [0] * 16

    def emit_vt(c):
        t_vtm = [0]

        @blk.tensor
        def _(t, c=c):
            t.wait_ge(p.DV.h, tok_kv[1])
            t.wait_ge(p.DW3.h, p.t_ident)
            if c > 0:
                t.wait_ge(p.DV.h, tok_vt[c - 1])
            t_vtm[0] = p.TE.inc(t.transpose(psT[:, 0:128], vv[:, 128 * c:128 * (c + 1)],
                                            p.ident[:]))

        @blk.vector
        def _(v, c=c):
            v.wait_ge(p.TE.h, t_vtm[0])
            if c == 0:
                v.wait_ge(p.DV.h, p.tok_wt_last)
            tok_vt[c] = p.DV.inc(v.tensor_copy(p.gT[:, c, :], psT[:, 0:128]))
    ln_scr = p.big[:, 24:27, :].rearrange("p a b -> p (a b)").bitcast(F32)
    tok_kv = [0, 0]
    t_kv_k = [0, 0]
    tok_qh = [[0] * 2 for _ in range(NHEAD)]
    t_qev = [0] * NHEAD
    t_kvev = [0]
    t_qmm = [0]
    p.t_wqh = [0] * 16
    p.t_qmm_j = [0] * 16
    for half in range(2):
        if half == 0 and p.next_tok_xn is not None:
            tok_xn = p.next_tok_xn
            p.next_tok_xn = None
        elif half == 0:
            tok_xn = emit_ln(p, TG * half, ln_scr, (p.TE, p.p2b_last_te))
        else:
            p.psD_free = (p.DV, t_kv_k[0])
            tok_xn = emit_ln(p, TG * half, ln_scr, (p.TE, p.p2b_last_te),
                             bank=p.psA)
        for j in range(2):
            psK = p.psA if j == 0 else p.psB

            @blk.tensor
            def _(t, j=j, half=half, psK=psK, tok_xn=tok_xn):
                t.wait_ge(p.DW3.h, t_wkv[0])
                t.wait_ge(p.DV.h, tok_xn)
                if half == 1:
                    t.wait_ge(p.DV.h, t_kvev[0])
                else:
                    t.wait_ge(p.DV.h, p.x_dv)
                last = None
                for dd in range(CH):
                    for hh in range(2):
                        last = t.matmul(psK[:, 512 * hh:512 * (hh + 1)],
                                        wkvv[:, dd, 128 * j:128 * (j + 1)],
                                        p.xn[:, dd, 512 * hh:512 * (hh + 1)],
                                        start=(dd == 0), stop=(dd == CH - 1))
                p.t_kvmm = p.TE.inc(last)

            @blk.vector
            def _(v, j=j, half=half, psK=psK):
                v.wait_ge(p.TE.h, p.t_kvmm)
                v.wait_ge(p.DW3.h, p.t_pe)
                dst = (kk if j == 0 else vv)[:, TG * half:TG * (half + 1)]
                if j == 0:
                    tok_kv[half] = p.DV.inc(v.scalar_tensor_tensor(
                        dst, psK[:], p.bias[:, 72:73], p.pe[:, TG * half:TG * (half + 1)],
                        ALU.add, ALU.add))
                    t_kv_k[half] = p.DV.n
                else:
                    tok_kv[half] = p.DV.inc(v.tensor_scalar_add(dst, psK[:], p.bias[:, 73:74]))
                t_kvev[0] = p.DV.n

        for h in range(NHEAD):
            psQ = p.psC if h % 2 == 0 else p.psD
            j = 8 * half + h

            @blk.gpsimd
            def _(gg, h=h, half=half, j=j):
                if j < 2:
                    gg.wait_ge(p.TE.h, p.p2b_last_te)
                else:
                    gg.wait_ge(p.TE.h, p.t_qmm_j[j - 2])
                p.t_wqh[j] = dma(gg, p.DW1, wqh[:, j % 2, :, :],
                                 wq_r[:, :, 128 * h:128 * (h + 1)])

            @blk.tensor
            def _(t, h=h, half=half, j=j, psQ=psQ, tok_xn=tok_xn):
                wdma(t, p.t_wqh[j])
                t.wait_ge(p.DV.h, tok_xn)
                prev = t_qev[h - 2] if h >= 2 else (t_qev[NHEAD - 2 + h] if half == 1 else 0)
                t.wait_ge(p.DV.h, prev)
                if half == 0 and h < 2:
                    t.wait_ge(p.DV.h, p.ln_psD_free_dv)
                last = None
                for dd in range(CH):
                    for hh in range(2):
                        last = t.matmul(psQ[:, 512 * hh:512 * (hh + 1)], wqh[:, j % 2, dd, :],
                                        p.xn[:, dd, 512 * hh:512 * (hh + 1)],
                                        start=(dd == 0), stop=(dd == CH - 1))
                t_qmm[0] = p.TE.inc(last)
                p.t_qmm_j[j] = t_qmm[0]

            @blk.vector
            def _(v, h=h, half=half, psQ=psQ):
                v.wait_ge(p.TE.h, t_qmm[0])
                v.wait_ge(p.DW3.h, p.t_pe)
                if half == 0 and h == 0:
                    v.wait_ge(p.TE.h, p.p2b_last_te)
                tok_qh[h][half] = p.DV.inc(v.scalar_tensor_tensor(
                    p.big[:, 2 * h + half, :], psQ[:], p.bias[:, 64 + h:65 + h],
                    p.pe[:, TG * half:TG * (half + 1)], ALU.add, ALU.add))
                t_qev[h] = p.DV.n

            if half == 1:
                emit_vt(2 * h)
                emit_vt(2 * h + 1)
        p.xn_free = (p.TE, t_qmm[0])

    scope.__exit__(None, None, None)
    scope = p.nc.named_scope("attn_heads")
    scope.__enter__()
    Ev = p.scrG[:].rearrange("p (s t) -> p s t", s=8)
    zr = p.rows[0:1, 0:512]
    zbs = p.gT[:, 16:20, :].rearrange("p a b -> p (a b)")
    tok_att = {-2: 0, -1: 0}
    t_zr = {-2: 0, -1: 0}
    t_zbs = {-2: 0, -1: 0}
    pv_tok = {}
    pv_last = {}
    exp_prev = [0, 0]
    t_zb = {}

    for h in range(NHEAD):
        for Q in range(4):
            nkc = 4 * Q + 4
            idx = 4 * h + Q
            npair = nkc // 2
            psOt = p.psA if idx % 2 == 0 else p.psB
            psO = psOt[:, 0:512]
            zrow = psOt[0:1, 512:1024]
            teI, acI, dvI = p.TE.n, p.AC.n, p.DV.n
            tok_exp = [acI + (kc // 2) + 1 for kc in range(nkc)]
            t_mask = dvI + 1
            t_sm_toks = []
            qblk = p.big[:, 2 * h + Q // 2, 512 * (Q % 2):512 * (Q % 2) + 512]

            def pv(t, kc, first, last_flag, idx=idx, psO=psO, zrow=zrow, nkc=nkc,
                   tok_exp=tok_exp, t_mask=t_mask):
                slot = kc % 8
                if kc >= nkc - 4:
                    t.wait_ge(p.DV.h, t_mask)
                else:
                    t.wait_ge(p.AC.h, tok_exp[kc])
                if first:
                    t.wait_ge(p.DV.h, tok_vt[15])
                    t.wait_ge(p.DV.h, t_zr[idx - 2])
                    t.wait_ge(p.DV.h, tok_att[idx - 2])
                    if idx < 2:
                        t.wait_ge(p.DV.h, t_qev[NHEAD - 1])
                t.matmul(zrow, p.onesb_c[:], Ev[:, slot, :], start=first, stop=last_flag)
                mm = t.matmul(psO, p.gT[:, kc, :], Ev[:, slot, :], start=first, stop=last_flag)
                pv_tok[slot] = p.TE.inc(mm)
                if last_flag:
                    pv_last[idx] = pv_tok[slot]

            @blk.tensor
            def _(t, h=h, Q=Q, idx=idx, nkc=nkc, npair=npair, qblk=qblk, tok_exp=tok_exp):
                t.wait_ge(p.DV.h, tok_kv[1])
                t.wait_ge(p.DV.h, tok_qh[h][Q // 2])
                pend_pv = []
                for pr in range(npair):
                    kc0 = 2 * pr
                    psS = p.psC if pr % 2 == 0 else p.psD
                    if kc0 >= 4:
                        t.wait_ge(p.AC.h, tok_exp[kc0 - 4])
                    else:
                        t.wait_ge(p.AC.h, exp_prev[pr % 2])
                        if idx == 0:
                            t.wait_ge(p.DV.h, t_qev[NHEAD - 1])
                    if pr % 2 == 1:
                        t.wait_ge(p.DV.h, t_zbs[idx - 2])
                    mm = None
                    for j in range(2):
                        kc = kc0 + j
                        mm = t.matmul(psS[:, 512 * j:512 * (j + 1)],
                                      kk[:, 128 * kc:128 * (kc + 1)], qblk,
                                      start=True, stop=True)
                    t_sm_toks.append(p.TE.inc(mm))
                    if kc0 + 1 < nkc - 4:
                        pend_pv.append(kc0)
                        if len(pend_pv) > 1:
                            k0 = pend_pv.pop(0)
                            pv(t, k0, first=(k0 == 0), last_flag=False)
                            pv(t, k0 + 1, first=False, last_flag=False)
                for k0 in pend_pv:
                    pv(t, k0, first=(k0 == 0), last_flag=False)
                    pv(t, k0 + 1, first=False, last_flag=False)
                for kc in range(nkc - 4, nkc):
                    pv(t, kc, first=(kc == 0), last_flag=(kc == nkc - 1))
                if idx >= 1:
                    t.wait_ge(p.DV.h, t_zr[idx - 1])
                    t.wait_ge(p.DV.h, t_zbs[idx - 2])
                    t.wait_ge(p.AC.h, tok_exp[2 * (npair - 1)])
                    t_zb[idx - 1] = p.TE.inc(t.matmul(p.psD[:, 0:512], p.ones_rf[:], zr,
                                                      start=True, stop=True))

            @blk.scalar
            def _(s, h=h, Q=Q, idx=idx, nkc=nkc, npair=npair, tok_exp=tok_exp,
                  t_sm_toks=t_sm_toks):
                for pr in range(npair):
                    kc0 = 2 * pr
                    slot = kc0 % 8
                    psS = p.psC if pr % 2 == 0 else p.psD
                    s.wait_ge(p.TE.h, t_sm_toks[pr])
                    for sl in (slot, slot + 1):
                        if sl in pv_tok and pv_tok[sl] < teI:
                            s.wait_ge(p.TE.h, pv_tok[sl])
                    if idx == 0 and kc0 == 0:
                        s.wait_ge(p.DT.h, p.last_trp)
                        s.wait_ge(p.DV.h, p.tok_wt_last)
                    a = p.AC.inc(s.activation(Ev[:, slot:slot + 2, :], psS[:, 0:1024],
                                              AF.Exp, scale=SCALE))
                    assert a == tok_exp[kc0]

            s0 = (nkc - 4) % 8

            @blk.vector
            def _(v, idx=idx, s0=s0, nkc=nkc, tok_exp=tok_exp, psO=psO, zrow=zrow):
                v.wait_ge(p.AC.h, tok_exp[nkc - 1])
                v.wait_ge(p.DW3.h, p.t_stair)
                a = p.DV.inc(v.tensor_mul(Ev[:, s0:s0 + 4, :], Ev[:, s0:s0 + 4, :],
                                          p.stair[:]))
                assert a == t_mask
                if idx >= 1:
                    v.wait_ge(p.TE.h, t_zb[idx - 1])
                    t1 = p.DV.inc(v.tensor_copy(zbs[:, 0:512], p.psD[:, 0:512]))
                    v.wait_ge(p.DV.h, t1)
                    t_zbs[idx - 1] = t1
                    pidx = idx - 1
                    ph, pQ = pidx // 4, pidx % 4
                    ppsO = (p.psA if pidx % 2 == 0 else p.psB)[:, 0:512]
                    tok_att[pidx] = p.DV.inc(v.tensor_mul(
                        p.big[:, 16 + 2 * ph + pQ // 2, 512 * (pQ % 2):512 * (pQ % 2) + 512],
                        ppsO, zbs[:, 0:512]))
                v.wait_ge(p.TE.h, pv_last[idx])
                t_zr[idx] = p.DV.inc(v.reciprocal(zr, zrow))

            exp_prev[0] = tok_exp[2 * (npair - 2)]
            exp_prev[1] = tok_exp[2 * (npair - 1)]

    lidx = 4 * NHEAD - 1

    @blk.tensor
    def _(t):
        t.wait_ge(p.DV.h, t_zr[lidx])
        t.wait_ge(p.DV.h, t_zbs[lidx - 1])
        t.wait_ge(p.AC.h, exp_prev[1])
        t_zb[lidx] = p.TE.inc(t.matmul(p.psD[:, 0:512], p.ones_rf[:], zr, start=True, stop=True))

    @blk.vector
    def _(v):
        v.wait_ge(p.TE.h, t_zb[lidx])
        t1 = p.DV.inc(v.tensor_copy(zbs[:, 0:512], p.psD[:, 0:512]))
        v.wait_ge(p.DV.h, t1)
        ph, pQ = lidx // 4, lidx % 4
        ppsO = (p.psA if lidx % 2 == 0 else p.psB)[:, 0:512]
        tok_att[lidx] = p.DV.inc(v.tensor_mul(
            p.big[:, 16 + 2 * ph + pQ // 2, 512 * (pQ % 2):512 * (pQ % 2) + 512],
            ppsO, zbs[:, 0:512]))
    scope.__exit__(None, None, None)
    scope = p.nc.named_scope("attn_wo")
    scope.__enter__()
    wov = p.wr1[:, 0:2048].rearrange("p (b d n) -> p b d n", b=2, d=CH)
    wo_r = p.d_wo.rearrange("(d p) n -> p d n", p=128)
    tok_wo = [0] * 8
    tok_omm = [0] * 8
    tok_oev = [0] * 16
    for oc in range(8):
        @blk.gpsimd
        def _(gg, oc=oc):
            if oc < 2:
                gg.wait_ge(p.TE.h, p.t_qmm_j[15])
            else:
                gg.wait_ge(p.TE.h, tok_omm[oc - 2])
            tok_wo[oc] = dma(gg, p.DW1, wov[:, oc % 2, :, :],
                             wo_r[:, :, 128 * oc:128 * (oc + 1)])
        for blkk in range(2):
            psX = p.psC if (2 * oc + blkk) % 2 == 0 else p.psD
            t_omm = [0]

            @blk.tensor
            def _(t, oc=oc, blkk=blkk, psX=psX):
                wdma(t, tok_wo[oc])
                t.wait_ge(p.DV.h, tok_att[4 * NHEAD - 1])
                j = 2 * oc + blkk
                if j >= 2:
                    t.wait_ge(p.DV.h, tok_oev[j - 2])
                last = None
                for hh in range(CH):
                    for q2_ in range(2):
                        last = t.matmul(psX[:, 512 * q2_:512 * (q2_ + 1)], wov[:, oc % 2, hh, :],
                                        p.big[:, 16 + 2 * hh + blkk, 512 * q2_:512 * (q2_ + 1)],
                                        start=(hh == 0), stop=(hh == CH - 1))
                t_omm[0] = p.TE.inc(last)
                tok_omm[oc] = t_omm[0]

            @blk.vector
            def _(v, oc=oc, blkk=blkk, psX=psX):
                v.wait_ge(p.TE.h, t_omm[0])
                if 2 * oc + blkk == 0:
                    v.wait_ge(p.DV.h, tok_att[4 * NHEAD - 1])
                tok_oev[2 * oc + blkk] = p.DV.inc(v.tensor_tensor(
                    p.x[:, oc, TG * blkk:TG * (blkk + 1)], psX[:],
                    p.x[:, oc, TG * blkk:TG * (blkk + 1)], ALU.add))
    scope.__exit__(None, None, None)
    p.x_dv = tok_oev[15]
    p.attn_oev_l = list(tok_oev)
    p.psC_free = tok_oev[14]
    p.psD_free = (p.DV, tok_oev[15])
    p.big_free = (p.TE, tok_omm[7])
    p.ring_free = (p.TE, tok_omm[7])
    p.ub_free = (p.TE, tok_omm[7])
    p.t_c4 = t_qev[NHEAD - 1]
    p.scr_free = (p.TE, tok_omm[7])
    p.q2buf_free = [(p.TE.h, tok_omm[7]), (p.DV.h, tok_oev[15])]
    p.scrG_free_l = [(p.TE.h, tok_omm[7])]
    p.gT_free_l = [(p.TE.h, tok_omm[7]), (p.DV.h, tok_oev[15])]
    p.simbuf_free = [(p.DV.h, tok_oev[15])]
    p.ring2_free = (p.DV, tok_oev[15])


def emit_store(p):
    blk = p.blk

    @blk.sync
    def _(s):
        s.wait_ge(p.DV.h, p.x_dv)
        for c in range(CH):
            dma(s, p.DS, p.d_out[128 * c:128 * (c + 1), :], p.x[:, c, :])
        s.wait_ge(p.DS.h, p.DS.n)
        s.wait_ge(p.DT.h, p.DT.n)


# version 25
# speedup vs baseline: 1.0105x; 1.0057x over previous
import math
import contextlib
import numpy as np
import concourse.bass as bass
import concourse.mybir as mybir
from concourse.bass_utils import run_bass_kernel_spmd
import concourse.dve_ops as _dve_ops
from concourse.dve_op import DveOp as _DveOp
from concourse.dve_spec import Spec as _Spec, Src0 as _Src0, Src1 as _Src1, C0 as _C0, \
    Zero as _Zero, select as _select
from concourse.dve_table_gen import dve_ver_for as _dve_ver_for

F32 = mybir.dt.float32
BF16 = mybir.dt.bfloat16
AF = mybir.ActivationFunctionType
ALU = mybir.AluOpType
AX = mybir.AxisListType

NCORE = 8
D = 1024
T = 2048
CH = 8
DC = 4096
NE = 4096
NH = 4
TG = 1024
NTL = 8
EPS = 1e-5
NHEAD = 8
SCALE = 1.0 / math.sqrt(128.0)

TAPS = {}


def _sel_add_ref(in0, in1, s0, s1, imm2):
    return np.where(in0 >= s0, in0, 0.0) + in1


def _register_dve_op(name, spec, ref):
    for o in _dve_ops.OPS:
        if o.name == name:
            return o
    ver = _dve_ver_for("TRN2")
    op = _DveOp(name, spec, subdim=False, uops_sha={}, reference=ref)
    try:
        op.compile(ver)
        sha = None
    except ValueError as e:
        import re as _re
        m = _re.search(r"\(%s: (\w+)" % ver, str(e))
        sha = m.group(1)
    if sha is not None:
        op = _DveOp(name, spec, subdim=False, uops_sha={ver: sha}, reference=ref)
        op.compile(ver)
    _dve_ops.OPS.append(op)
    _dve_ops._SUB_OPCODE_FOR_NAME[op.name] = (_dve_ops._CUSTOM_DVE_ROW_BASE
                                              + len(_dve_ops.OPS) - 1)
    _dve_ops.CUSTOM_DVE_SPECS[op.name] = spec
    _dve_ops.CUSTOM_DVE_REFS[op.name] = ref
    return op


XBAR_TRANSPOSE = True


def _host_consts():
    pos = np.arange(T, dtype=np.float64)[:, None]
    inv = np.exp(-math.log(10000.0) * np.arange(0, 128, 2, dtype=np.float64) / 128.0)
    ang = pos * inv[None, :]
    pe = np.zeros((T, 128), dtype=np.float64)
    pe[:, 0::2] = np.sin(ang)
    pe[:, 1::2] = np.cos(ang)
    pet = np.ascontiguousarray(pe.T.astype(np.float32))
    pp = np.arange(128)[:, None, None]
    cc = np.arange(4)[None, :, None]
    jj = np.arange(512)[None, None, :]
    stair = ((128 * cc + pp) <= jj).astype(np.float32).reshape(128, 4 * 512)
    ident = np.eye(128, dtype=np.float32)
    return pet, stair, ident


def kernel(**inp):
    pet, stair, ident = _host_consts()
    x = np.ascontiguousarray(inp["L0_attn_x"])
    com = {
        "wq": np.ascontiguousarray(inp["L0_attn_Wq"]),
        "wkv": np.ascontiguousarray(inp["L0_attn_Wkv"]),
        "wo": np.ascontiguousarray(inp["L0_attn_Wo"]),
        "wg": np.ascontiguousarray(inp["L0_ffn_Wg"]),
        "wu": np.ascontiguousarray(inp["L0_ffn_Wu"]),
        "wd": np.ascontiguousarray(inp["L0_ffn_Wd"]),
        "wqf": np.ascontiguousarray(inp["L0_ffn_Wq"]),
        "dte": np.ascontiguousarray(inp["L0_ffn_down_embed"].T),
        "ue": np.ascontiguousarray(inp["L0_ffn_up_embed"]),
        "keyst": np.ascontiguousarray(inp["L0_ffn_keys"].transpose(2, 0, 3, 1)),
        "bqc": np.ascontiguousarray(inp["L0_attn_bq"].reshape(8, 128).T),
        "bkvc": np.ascontiguousarray(inp["L0_attn_bkv"].reshape(2, 128).T),
        "bgc": np.ascontiguousarray(inp["L0_ffn_bg"].reshape(32, 128).T),
        "buc": np.ascontiguousarray(inp["L0_ffn_bu"].reshape(32, 128).T),
        "bdr": np.ascontiguousarray(inp["L0_ffn_bd"].reshape(1, 1024)),
        "pet": pet, "stair": stair, "ident": ident,
    }
    nc = build_nc()
    in_maps = []
    for b in range(NCORE):
        m = dict(com)
        m["xt"] = np.ascontiguousarray(x[b].T)
        in_maps.append(m)
    res = run_bass_kernel_spmd(nc, in_maps, core_ids=list(range(NCORE)))
    out = np.stack([np.ascontiguousarray(res.results[b]["out"].T) for b in range(NCORE)], axis=0)
    return out.astype(np.float32)


class Cnt:
    def __init__(self, ctx, nc, name):
        self.h = ctx.enter_context(nc.semaphore(name))
        self.n = 0

    def inc(self, instr, k=1):
        self.n += k
        instr.then_inc(self.h, k)
        return self.n


class LanePair:

    def __init__(self, ctx, nc, name):
        self.lanes = [Cnt(ctx, nc, name + "a"), Cnt(ctx, nc, name + "b")]
        self.i = 0

    def issue(self, eng, out, in_, **kw):
        lane = self.lanes[self.i % 2]
        self.i += 1
        eng.wait_ge(lane.h, lane.n)
        tok = lane.inc(eng.dma_start(out=out, in_=in_, **kw), 16)
        return (lane, tok)


class P:
    pass


def build_nc():
    nc = bass.Bass("TRN2", target_bir_lowering=False)
    p = P()
    p.nc = nc
    dt = nc.dram_tensor
    p.d_xt = dt("xt", [D, T], F32, kind="ExternalInput").ap()
    p.d_wq = dt("wq", [D, D], F32, kind="ExternalInput").ap()
    p.d_wkv = dt("wkv", [D, 256], F32, kind="ExternalInput").ap()
    p.d_wo = dt("wo", [D, D], F32, kind="ExternalInput").ap()
    p.d_wg = dt("wg", [D, DC], F32, kind="ExternalInput").ap()
    p.d_wu = dt("wu", [D, DC], F32, kind="ExternalInput").ap()
    p.d_wd = dt("wd", [DC, D], F32, kind="ExternalInput").ap()
    p.d_wqf = dt("wqf", [D, 512], F32, kind="ExternalInput").ap()
    p.d_dt = dt("dte", [D, NE], F32, kind="ExternalInput").ap()
    p.d_ue = dt("ue", [NE, D], F32, kind="ExternalInput").ap()
    p.d_keyst = dt("keyst", [2, NH, 64, 64], F32, kind="ExternalInput").ap()
    p.d_bqc = dt("bqc", [128, 8], F32, kind="ExternalInput").ap()
    p.d_bkvc = dt("bkvc", [128, 2], F32, kind="ExternalInput").ap()
    p.d_bgc = dt("bgc", [128, 32], F32, kind="ExternalInput").ap()
    p.d_buc = dt("buc", [128, 32], F32, kind="ExternalInput").ap()
    p.d_bdr = dt("bdr", [1, D], F32, kind="ExternalInput").ap()
    p.d_pet = dt("pet", [128, T], F32, kind="ExternalInput").ap()
    p.d_stair = dt("stair", [128, 4 * 512], F32, kind="ExternalInput").ap()
    p.d_ident = dt("ident", [128, 128], F32, kind="ExternalInput").ap()
    p.d_gd = dt("gspill", [2, NTL, 128, NE], BF16, kind="Internal").ap()
    p.d_out = dt("out", [D, T], F32, kind="ExternalOutput").ap()
    p.taps = {name: dt("tap_" + name, list(shape), dtp, kind="ExternalOutput").ap()
              for name, (shape, dtp) in TAPS.items()}

    ctx = contextlib.ExitStack()

    def al(name, shape, dtp):
        return ctx.enter_context(nc.sbuf_tensor(name, shape, dtp))

    p.x = al("x", [128, CH, T], F32)
    p.big = al("big", [128, 32, TG], BF16)
    p.xn = al("xn", [128, CH, TG], BF16)
    p.scrG = al("scrG", [128, NE], BF16)
    p.scrP = al("scrP", [128, NE], BF16)
    p.gT = al("gT", [128, 32, 128], BF16)
    p.wr1 = al("wr1", [128, 4096], BF16)
    p.wr2 = al("wr2", [128, 2048], BF16)
    p.cmpb = al("cmpb", [128, 896], BF16)
    p.simbuf = al("simbuf", [128, 4096], F32)
    p.s1v8 = al("s1v8", [128, 8, 8, 8], F32)
    p.e8all = al("e8all", [128, 8, 64], BF16)
    p.bias = al("bias", [128, 80], F32)
    p.q8a = al("q8a", [128, 8, NH, 8], F32)
    p.q8S = al("q8S", [128, NH, 8], BF16)
    p.q9S = al("q9S", [128, NH, 8], BF16)
    p.zal = al("zal", [128, 3 * NH], F32)
    p.zala = al("zala", [128, 64], F32)
    p.ub = al("ub", [128, 128], BF16)
    p.ident = p.ub[:, :]
    p.bdrt = al("bdr_sb", [1, 1024], BF16)
    p.bdr = p.bdrt[0:1, :]
    p.ones_r = al("ones_r", [1, 512], BF16)
    p.ones_rf = al("ones_rf", [1, 128], F32)
    p.ones_c = al("ones_c", [128, 1], F32)
    p.onesb_c = al("onesb_c", [128, 1], BF16)
    p.eps_c = al("eps_c", [128, 1], F32)
    p.keysbd = al("keysbd", [128, 4, 128], BF16)
    p.sil = p.scrP[:].bitcast(F32)
    p.silb = p.scrP[:].rearrange("p (s n) -> p s n", s=4)
    p.pe = p.simbuf[:].bitcast(BF16)[:, 0:2048]
    p.rows = p.simbuf[0:1, 1024:2048]
    p.q2 = p.big[:, 28:32, :]
    p.stair = p.wr2[:].rearrange("p (a b) -> p a b", a=4)

    p.psA = ctx.enter_context(nc.psum_tensor("psA", [128, 1024], F32))
    p.psB = ctx.enter_context(nc.psum_tensor("psB", [128, 1024], F32))
    p.psC = ctx.enter_context(nc.psum_tensor("psC", [128, 1024], F32))
    p.psD = ctx.enter_context(nc.psum_tensor("psD", [128, 1024], F32))

    p.TE = Cnt(ctx, nc, "qTE")
    p.DV = Cnt(ctx, nc, "qDV")
    p.AC = Cnt(ctx, nc, "qAC")
    p.PO = Cnt(ctx, nc, "qPO")
    p.DW1 = LanePair(ctx, nc, "qDW1")
    p.DW2 = LanePair(ctx, nc, "qDW2")
    p.DW3 = Cnt(ctx, nc, "qDW3")
    p.DS = Cnt(ctx, nc, "qDS")
    p.DT = Cnt(ctx, nc, "qDT")
    p.DG = LanePair(ctx, nc, "qDG")
    p.x_dv = 0
    p.xn_free = (p.DV, 0)
    p.psD_free = (p.DV, 0)
    p.ln_psD_free_dv = 0
    p.last_trp = 0
    p.tok_wt_last = 0
    p.scr_free = (p.DV, 0)
    p.t_c4 = 0
    p.big_free = (p.TE, 0)
    p.ring_free = (p.TE, 0)
    p.ring2_free = (p.TE, 0)
    p.ub_free = (p.TE, 0)
    p.psC_free = 0
    p.q2buf_free = []
    p.simbuf_free = []
    p.scrG_free_l = []
    p.gT_free_l = []
    p.tap_hold = []
    p.next_tok_xn = None

    with nc.Block() as blk:
        p.blk = blk
        emit_init(p)
        emit_ffn(p)
        emit_attn(p)
        emit_ffn(p)
        emit_store(p)
    ctx.close()
    return nc


def dma(eng, lane, out, in_, **kw):
    if isinstance(lane, LanePair):
        return lane.issue(eng, out, in_, **kw)
    eng.wait_ge(lane.h, lane.n)
    return lane.inc(eng.dma_start(out=out, in_=in_, **kw), 16)


def wdma(eng, token):
    lane, tok = token
    eng.wait_ge(lane.h, tok)


def wl(eng, lst):
    for h, v in lst:
        eng.wait_ge(h, v)


def tap(p, name, ap_src, cnt, tok):
    if name not in p.taps:
        return None
    blk = p.blk
    out = [None]

    @blk.sync
    def _(s):
        s.wait_ge(cnt.h, tok)
        out[0] = dma(s, p.DS, p.taps[name], ap_src)
    return out[0]


def emit_init(p):
    blk = p.blk

    @blk.sync
    def _(s):
        for c in range(CH):
            dma(s, p.DS, p.x[:, c, :], p.d_xt[128 * c:128 * (c + 1), :])
    p.tok_xin = p.DS.n

    @blk.gpsimd
    def _(g):
        dma(g, p.DW3, p.bias[:, 0:32], p.d_bgc)
        dma(g, p.DW3, p.bias[:, 32:64], p.d_buc)
        dma(g, p.DW3, p.bias[:, 64:72], p.d_bqc)
        dma(g, p.DW3, p.bias[:, 72:74], p.d_bkvc)
        tz = p.PO.inc(g.memset(p.keysbd[:], 0.0))
        g.wait_ge(p.PO.h, tz)
        for c in range(4):
            pp, hlo = c // 2, (c % 2) * 2
            dma(g, p.DW3, p.keysbd[0:64, c, 0:64], p.d_keyst[pp, hlo, :, :])
            dma(g, p.DW3, p.keysbd[64:128, c, 64:128], p.d_keyst[pp, hlo + 1, :, :])
    p.tok_const = p.DW3.n

    @blk.vector
    def _(v):
        p.DV.inc(v.memset(p.ones_r[:], 1.0))
        p.DV.inc(v.memset(p.ones_rf[:], 1.0))
        p.DV.inc(v.memset(p.ones_c[:], 1.0))
        p.DV.inc(v.memset(p.onesb_c[:], 1.0))
        p.DV.inc(v.memset(p.eps_c[:], float(EPS)))
    p.tok_ones = p.DV.n


def emit_ln(p, t0, scratch, scr_free, defer=False, bank=None):
    blk = p.blk
    psb = bank if bank is not None else p.psD
    xnf, psDf = p.xn_free, p.psD_free
    x_dv = p.x_dv
    hold0 = list(p.tap_hold)
    fins = []
    prev_stats = [0]
    tok_rm_l = [0, 0]
    norm_prev = [0]
    for b in range(TG // 512):
        tb = t0 + 512 * b
        base = 1024 + 1024 * b
        mb = p.simbuf[0:1, base:base + 512]
        varb = p.simbuf[0:1, base + 512:base + 1024]
        r = varb
        mr = mb
        te0, ac0, dv0 = p.TE.n, p.AC.n, p.DV.n
        tok_sq = [ac0 + c + 1 for c in range(CH)]
        tok_sd = ac0 + CH + 1
        tok_sumx = te0 + 1
        tok_smm = [te0 + 1 + c + 1 for c in range(CH)]
        tok_stats = dv0 + 3
        tok_rm = dv0 + 5
        tok_rm_l[b] = tok_rm

        @blk.scalar
        def _(s, tb=tb, b=b, r=r, varb=varb, tok_sq=tok_sq, tok_smm=tok_smm,
              tok_stats=tok_stats, tok_sd=tok_sd):
            s.wait_ge(p.DS.h, p.tok_xin)
            s.wait_ge(p.DV.h, x_dv)
            if b == 0:
                s.wait_ge(scr_free[0].h, scr_free[1])
                wl(s, hold0)
            for c in range(CH):
                if c >= 2:
                    s.wait_ge(p.TE.h, tok_smm[c - 2])
                a = p.AC.inc(s.activation(scratch[:, 512 * (c % 2):512 * (c % 2) + 512],
                                          p.x[:, c, tb:tb + 512], AF.Square))
                assert a == tok_sq[c]
            s.wait_ge(p.DV.h, tok_stats)
            assert p.AC.inc(s.activation(r, varb, AF.Sqrt, bias=p.eps_c[0:1, :])) == tok_sd

        @blk.tensor
        def _(t, tb=tb, b=b, tok_sq=tok_sq, tok_smm=tok_smm, tok_sumx=tok_sumx,
              pstats=prev_stats[0]):
            t.wait_ge(p.DS.h, p.tok_xin)
            t.wait_ge(p.DV.h, max(x_dv, p.tok_ones, p.ln_psD_free_dv, pstats))
            if b == 0:
                t.wait_ge(psDf[0].h, psDf[1])
            last = None
            for c in range(CH):
                last = t.matmul(psb[0:1, 0:512], p.ones_c[:], p.x[:, c, tb:tb + 512],
                                start=(c == 0), stop=(c == CH - 1))
            assert p.TE.inc(last) == tok_sumx
            for c in range(CH):
                t.wait_ge(p.AC.h, tok_sq[c])
                mm = t.matmul(psb[32:33, 0:512], p.ones_c[:],
                              scratch[:, 512 * (c % 2):512 * (c % 2) + 512],
                              start=(c == 0), stop=(c == CH - 1))
                assert p.TE.inc(mm) == tok_smm[c]

        @blk.vector
        def _(v, b=b, mb=mb, varb=varb, r=r, mr=mr, tok_smm=tok_smm,
              tok_stats=tok_stats, tok_sd=tok_sd, tok_rm=tok_rm):
            v.wait_ge(p.TE.h, tok_smm[CH - 1])
            t1 = p.DV.inc(v.tensor_scalar_mul(mb, psb[0:1, 0:512], 1.0 / D))
            v.wait_ge(p.DV.h, t1)
            t2 = p.DV.inc(v.tensor_mul(varb, mb, mb))
            v.wait_ge(p.DV.h, t2)
            t3 = p.DV.inc(v.scalar_tensor_tensor(varb, psb[32:33, 0:512], 1.0 / D, varb,
                                                 ALU.mult, ALU.subtract))
            assert t3 == tok_stats
            v.wait_ge(p.AC.h, tok_sd)
            t4 = p.DV.inc(v.reciprocal(r, r))
            v.wait_ge(p.DV.h, t4)
            t5 = p.DV.inc(v.tensor_mul(mr, mb, r))
            assert t5 == tok_rm
        prev_stats[0] = tok_stats

        def fin(b=b, tb=tb, r=r, mr=mr, tok_rm=tok_rm):
            t_bc = [0]

            @blk.tensor
            def _(t):
                t.wait_ge(p.DV.h, max(tok_rm, tok_rm_l[1], norm_prev[0]))
                t.matmul(psb[:, 0:512], p.ones_rf[:], r, start=True, stop=True)
                t_bc[0] = p.TE.inc(t.matmul(psb[:, 512:1024], p.ones_rf[:], mr,
                                            start=True, stop=True))

            @blk.vector
            def _(v):
                v.wait_ge(p.TE.h, t_bc[0])
                v.wait_ge(xnf[0].h, xnf[1])
                if b == 0:
                    wl(v, hold0)
                for c in range(CH):
                    ta = p.DV.inc(v.tensor_mul(scratch[:, 1024:1536], p.x[:, c, tb:tb + 512],
                                               psb[:, 0:512]))
                    v.wait_ge(p.DV.h, ta)
                    t6 = p.DV.inc(v.tensor_sub(p.xn[:, c, 512 * b:512 * (b + 1)],
                                               scratch[:, 1024:1536], psb[:, 512:1024]))
                    v.wait_ge(p.DV.h, t6)
            norm_prev[0] = p.DV.n
            return p.DV.n
        fins.append(fin)

    def finish():
        tok = 0
        for f in fins:
            tok = f()
        p.ln_psD_free_dv = tok
        p.psD_free = (p.DV, tok)
        return tok
    if defer:
        return finish
    return finish()


FFN_CALL = [0]


def emit_ffn(p):
    call = FFN_CALL[0]
    FFN_CALL[0] += 1
    for g in range(2):
        if p.next_tok_xn is not None:
            tok_xn = p.next_tok_xn
            p.next_tok_xn = None
        else:
            with p.nc.named_scope(f"f{call}g{g}_ln"):
                tok_xn = emit_ln(p, TG * g, p.sil, p.scr_free)
        tk = tap(p, f"xn{call}_{g}", p.xn[:].rearrange("p c t -> p (c t)"), p.DV, tok_xn)
        if tk:
            p.tap_hold.append((p.DS.h, tk))
        if g == 0:
            def nl():
                with p.nc.named_scope(f"f{call}g1_ln"):
                    return emit_ln(p, TG, p.sil, p.scr_free, defer=True)
        elif call == 0:
            def nl():
                with p.nc.named_scope("fa0_ln"):
                    return emit_ln(p, 0, p.sil, p.scr_free, defer=True)
        else:
            nl = None
        emit_group(p, g, tok_xn, nl)


def emit_group(p, g, tok_xn, next_ln=None):
    blk = p.blk
    scope = p.nc.named_scope(f"c{FFN_CALL[0]}g{g}_grp")
    scope.__enter__()
    hold0 = list(p.tap_hold)

    te0, ac0, dv0 = p.TE.n, p.AC.n, p.DV.n
    tok_q2mm = [te0 + oc + 1 for oc in range(4)]
    tok_sim = [te0 + 4 + tt + 1 for tt in range(NTL)]
    tok_Gmm = [te0 + 12 + 2 * c + 1 for c in range(32)]
    tok_Umm = [te0 + 12 + 2 * c + 2 for c in range(32)]
    tok_dmid = [te0 + 76 + 2 * do + 1 for do in range(8)]
    tok_dmm = [te0 + 76 + 2 * do + 2 for do in range(8)]
    tok_hdm = [te0 + 92 + ec + 1 for ec in range(32)]
    tok_q2cp = [ac0 + oc + 1 for oc in range(4)]
    tok_scp = [ac0 + 4 + tt + 1 for tt in range(NTL)]
    t_exn = ac0 + 14
    t_e8 = ac0 + 16
    tok_sil = [ac0 + 16 + 2 * c + 1 for c in range(32)]
    tok_ucp = [ac0 + 16 + 2 * c + 2 for c in range(32)]
    tok_hds = [ac0 + 80 + ec + 1 for ec in range(32)]
    po0 = p.PO.n
    tok_Ap = [po0 + j + 1 for j in range(16)]
    t_max_last = dv0 + 64
    t_mask = dv0 + 65
    t_fneg = dv0 + 66
    assert p.DG.i % 2 == 0
    dg0 = [p.DG.lanes[0].n, p.DG.lanes[1].n]
    tok_st = [(p.DG.lanes[tt % 2], dg0[tt % 2] + 16 * (tt // 2 + 1)) for tt in range(NTL)]

    wgv4 = p.wr1[:].rearrange("p (b d n) -> p b d n", b=4, d=CH)
    wqv = p.wr2[:].rearrange("p (d n) -> p d n", d=CH)
    wdv = p.wr1[:].rearrange("p (b c n) -> p b c n", b=2, c=16)
    dtv = p.wr1[:].rearrange("p (b d n) -> p b d n", b=4, d=CH)
    wg_r = p.d_wg.rearrange("(d p) n -> p d n", p=128)
    wu_r = p.d_wu.rearrange("(d p) n -> p d n", p=128)
    wqf_r = p.d_wqf.rearrange("(d p) n -> p d n", p=128)
    wd_r = p.d_wd.rearrange("(c p) n -> p c n", p=128)
    dt_r = p.d_dt.rearrange("(d p) n -> p d n", p=128)
    sbb = p.simbuf[:].bitcast(BF16)
    fnall = sbb[:, 0:4096]
    gacc2 = sbb[:, 4096:8192]
    tmp = p.gT[:].rearrange("p c t -> p (c t)")
    thresh = p.zal[:, 8:12]
    qS = p.cmpb[:, 256:512].rearrange("p (h k) -> p h k", h=NH)
    qtmp = p.cmpb[:, 512:576]
    e8f = p.e8all[:].rearrange("p a b -> p (a b)")
    qc_all = tmp[:, 0:2048].rearrange("p (j a b) -> p j a b", j=32, a=8)
    e8x_all = e8f[:, 0:256].rearrange("p (j k) -> p j k", j=32)
    e8y_all = e8f[:, 256:512].rearrange("p (j k) -> p j k", j=32)
    fnx_all = fnall.rearrange("p (t s n) -> p t s n", t=NTL, s=2)[:, :, 0, :].rearrange(
        "p t (h n) -> p t h n", h=NH)
    Z_all = p.zala[:, 0:32].rearrange("p (t h) -> p t h", t=NTL)
    al_all = p.zala[:, 32:64].rearrange("p (t h) -> p t h", t=NTL)

    t_wg = [None] * 32
    t_wu = [None] * 32
    t_wd = [None] * 16
    t_dts = [None] * 32
    t_wqf = [None, None]

    @blk.gpsimd
    def _(gg):
        gg.wait_ge(p.ring_free[0].h, p.ring_free[1])
        for c in range(2):
            t_wg[c] = dma(gg, p.DW1, wgv4[:, c % 2, :, :], wg_r[:, :, 128 * c:128 * (c + 1)])
            t_wu[c] = dma(gg, p.DW1, wgv4[:, 2 + c % 2, :, :],
                          wu_r[:, :, 128 * c:128 * (c + 1)])
        gg.wait_ge(p.ub_free[0].h, p.ub_free[1])
        p.t_bdr = dma(gg, p.DW3, p.bdr, p.d_bdr)
        gg.wait_ge(p.ring2_free[0].h, p.ring2_free[1])
        t_wqf[0] = dma(gg, p.DW2, wqv[:, :, :], wqf_r[:, :, 0:256])
        gg.wait_ge(p.TE.h, tok_q2mm[1])
        t_wqf[1] = dma(gg, p.DW2, wqv[:, :, :], wqf_r[:, :, 256:512])

        def a_pair(j):
            gg.wait_ge(p.AC.h, tok_ucp[2 * j + 1])
            a = p.PO.inc(gg.tensor_tensor(
                p.big[:, 2 * j:2 * j + 2, :], p.big[:, 2 * j:2 * j + 2, :],
                p.silb[:, (2 * j) % 4:(2 * j) % 4 + 2, :], ALU.mult))
            assert a == tok_Ap[j]

        for c in range(2, 32):
            gg.wait_ge(p.TE.h, tok_Umm[c - 2])
            t_wg[c] = dma(gg, p.DW1, wgv4[:, c % 2, :, :], wg_r[:, :, 128 * c:128 * (c + 1)])
            t_wu[c] = dma(gg, p.DW1, wgv4[:, 2 + c % 2, :, :],
                          wu_r[:, :, 128 * c:128 * (c + 1)])
            if c % 2 == 1:
                a_pair((c - 3) // 2)
        gg.wait_ge(p.TE.h, tok_Umm[31])
        t_wd[0] = dma(gg, p.DW1, wdv[:, 0, :, :], wd_r[:, 0:16, 0:128])
        t_wd[1] = dma(gg, p.DW1, wdv[:, 1, :, :], wd_r[:, 16:32, 0:128])
        a_pair(15)
        for sl in range(2, 16):
            do, hf = sl // 2, sl % 2
            gg.wait_ge(p.TE.h, tok_dmid[do - 1] if hf == 0 else tok_dmm[do - 1])
            t_wd[sl] = dma(gg, p.DW1, wdv[:, sl % 2, :, :],
                           wd_r[:, 16 * hf:16 * (hf + 1), 128 * do:128 * (do + 1)])
        for ec in range(32):
            gg.wait_ge(p.TE.h, tok_dmm[7] if ec < 4 else tok_hdm[ec - 4])
            t_dts[ec] = dma(gg, p.DW1, dtv[:, ec % 4, :, :],
                            dt_r[:, :, 128 * ec:128 * (ec + 1)])

    tok_ev = [None] * 8
    t_G = [None] * NTL
    sg = [p.scrG[:], gacc2]

    def emit_p1b_ev(v, do):
        v.wait_ge(p.TE.h, tok_dmm[do])
        psO = [p.psA, p.psB, p.psC, p.psD][do % 4]
        tok_ev[do] = p.DV.inc(v.scalar_tensor_tensor(
            p.x[:, do, TG * g:TG * (g + 1)], psO[:], 0.5,
            p.x[:, do, TG * g:TG * (g + 1)], ALU.mult, ALU.add))

    def tile_units(tt):
        acc = sg[tt % 2]
        fn_ = fnall[:, 512 * tt:512 * (tt + 1)]
        fnx = fn_[:, 0:256].rearrange("p (h n) -> p h n", h=NH)
        fny = fn_[:, 256:512].rearrange("p (h n) -> p h n", h=NH)
        e8x = e8f[:, 32 * tt:32 * (tt + 1)].rearrange("p (h k) -> p h k", h=NH)
        e8y = e8f[:, 256 + 32 * tt:256 + 32 * (tt + 1)].rearrange("p (h k) -> p h k", h=NH)

        def c2(v):
            t = p.DV.inc(v.tensor_tensor(qS, e8x.unsqueeze(3).to_broadcast([128, NH, 8, 8]),
                                         e8y.unsqueeze(2).to_broadcast([128, NH, 8, 8]), ALU.mult))
            v.wait_ge(p.DV.h, t)

        def c3(v):
            for h in range(NH):
                p.DV.inc(v.max(out=p.q8S[:, h, :], in_=qS[:, h, :]))
                v.wait_ge(p.DV.h, p.DV.n)
                p.DV.inc(v.match_replace(out=qtmp, in_to_replace=p.q8S[:, h, :],
                                         in_values=qS[:, h, :], imm_value=-1.0))
                v.wait_ge(p.DV.h, p.DV.n)
                p.DV.inc(v.max(out=p.q9S[:, h, :], in_=qtmp))
            v.wait_ge(p.DV.h, p.DV.n)
            t = p.DV.inc(v.tensor_tensor(thresh, p.q8S[:, :, 7], p.q9S[:, :, 0], ALU.add))
            v.wait_ge(p.DV.h, t)
            t = p.DV.inc(v.tensor_scalar_mul(thresh, thresh, 0.5))
            v.wait_ge(p.DV.h, t)

        def mk_b(h):
            def b_(v):
                dst = acc if h == 0 else tmp
                if h == 0 and tt >= 2:
                    wdma(v, tok_st[tt - 2])
                t = p.DV.inc(v.tensor_tensor(
                    dst.rearrange("p (a b) -> p a b", a=64),
                    fnx[:, h, :].unsqueeze(2).to_broadcast([128, 64, 64]),
                    fny[:, h, :].unsqueeze(1).to_broadcast([128, 64, 64]), ALU.mult))
                v.wait_ge(p.DV.h, t)
            return b_

        def mk_s(h):
            def s_(v):
                dst = acc if h == 0 else tmp
                t = p.DV.inc(v.scalar_tensor_tensor(dst, dst, thresh[:, h:h + 1], dst,
                                                    ALU.is_ge, ALU.mult))
                v.wait_ge(p.DV.h, t)
            return s_

        def mk_a():
            def a_(v):
                t = p.DV.inc(v.tensor_add(acc, acc, tmp))
                v.wait_ge(p.DV.h, t)
            return a_

        def fin(v):
            t_G[tt] = p.DV.n

        us = [(0.5, c2), (2.8, c3)]
        for h in range(NH):
            us.append((4.4, mk_b(h)))
            us.append((4.5, mk_s(h)))
            if h > 0:
                us.append((2.3, mk_a()))
        us.append((0.0, fin))
        return us

    units = []
    for tt in range(NTL):
        units += tile_units(tt)
    inserts = [(225.0 + 15.7 * do, 1, do) for do in range(8)]

    @blk.vector
    def _(v):
        wl(v, hold0)
        for tt in range(NTL):
            v.wait_ge(p.AC.h, tok_scp[tt])
            for ph in range(8):
                p.DV.inc(v.max(out=p.s1v8[:, tt, ph, :],
                               in_=p.simbuf[:, 512 * tt + 64 * ph:512 * tt + 64 * (ph + 1)]))
        assert p.DV.n == t_max_last
        wl(v, p.scrG_free_l)
        v.wait_ge(p.DV.h, t_max_last)
        a = p.DV.inc(v.tensor_tensor(
            p.scrG[:].rearrange("p (t h n) -> p t h n", t=NTL, h=8),
            p.simbuf[:].rearrange("p (t h n) -> p t h n", t=NTL, h=8),
            p.s1v8[:, :, :, 7:8].to_broadcast([128, NTL, 8, 64]), ALU.is_ge))
        assert a == t_mask
        v.wait_ge(p.DV.h, t_mask)
        v.wait_ge(p.AC.h, t_exn)
        a = p.DV.inc(v.tensor_mul(fnall, p.scrG[:], tmp))
        assert a == t_fneg
        v.wait_ge(p.DV.h, t_fneg)
        v.wait_ge(p.AC.h, t_e8)
        q8f = p.q8a[:].rearrange("p t h k -> p (t h) k")
        e8sc = e8f[:, 0:256].rearrange("p (t h k) -> p t h k", t=NTL, h=NH)
        t = p.DV.inc(v.tensor_tensor(qc_all,
                                     e8x_all.unsqueeze(3).to_broadcast([128, 32, 8, 8]),
                                     e8y_all.unsqueeze(2).to_broadcast([128, 32, 8, 8]),
                                     ALU.mult))
        v.wait_ge(p.DV.h, t)
        for j in range(32):
            p.DV.inc(v.max(out=q8f[:, j, :], in_=qc_all[:, j, :, :]))
        v.wait_ge(p.DV.h, p.DV.n)
        t = p.DV.inc(v.reduce_sum(p.zala[:, 0:32], q8f, axis=AX.X))
        v.wait_ge(p.DV.h, t)
        t = p.DV.inc(v.reciprocal(p.zala[:, 32:64], p.zala[:, 0:32]))
        v.wait_ge(p.DV.h, t)
        t = p.DV.inc(v.tensor_scalar_mul(p.zala[:, 32:64], p.zala[:, 32:64], 0.5))
        v.wait_ge(p.DV.h, t)
        p.DV.inc(v.tensor_tensor(
            fnx_all, fnx_all,
            al_all.unsqueeze(3).to_broadcast([128, NTL, NH, 64]), ALU.mult))
        t = p.DV.inc(v.tensor_tensor(
            e8sc, e8sc,
            al_all.unsqueeze(3).to_broadcast([128, NTL, NH, 8]), ALU.mult))
        v.wait_ge(p.DV.h, t)
        clock = 0.0
        ii = 0
        for dur, fn in units:
            while ii < len(inserts) and inserts[ii][0] <= clock:
                emit_p1b_ev(v, inserts[ii][2])
                clock += 1.2
                ii += 1
            fn(v)
            clock += dur
        while ii < len(inserts):
            emit_p1b_ev(v, inserts[ii][2])
            ii += 1

    @blk.sync
    def _(s):
        for tt in range(NTL):
            s.wait_ge(p.DV.h, t_G[tt])
            got = dma(s, p.DG, p.d_gd[g % 2, tt], sg[tt % 2])
            assert got == tok_st[tt]

    for tt in range(NTL):
        gname = f"G{FFN_CALL[0]}_{g}_{tt}"
        if gname in p.taps:
            @blk.sync
            def _(s, tt=tt, gname=gname):
                wdma(s, tok_st[tt])
                tk = dma(s, p.DS, p.taps[gname], p.d_gd[g % 2, tt])
                p.tap_hold.append((p.DS.h, tk))
    tokA_tap = tap(p, f"A{FFN_CALL[0]}_{g}", p.big[:].rearrange("p c t -> p (c t)"),
                   p.PO, tok_Ap[15])

    @blk.tensor
    def _(t):
        t.wait_ge(p.DV.h, max(tok_xn, p.tok_ones, p.psC_free))
        t.wait_ge(p.DW3.h, p.tok_const)
        for oc in range(4):
            pair, k = oc // 2, oc % 2
            if k == 0:
                wdma(t, t_wqf[pair])
            psQ = p.psC if oc % 2 == 0 else p.psD
            if oc >= 2:
                t.wait_ge(p.AC.h, tok_q2cp[oc - 2])
            last = None
            for dd in range(CH):
                for hh in range(2):
                    last = t.matmul(psQ[:, 512 * hh:512 * (hh + 1)],
                                    wqv[:, dd, 128 * k:128 * (k + 1)],
                                    p.xn[:, dd, 512 * hh:512 * (hh + 1)],
                                    start=(dd == 0), stop=(dd == CH - 1))
            assert p.TE.inc(last) == tok_q2mm[oc]
        for tt in range(NTL):
            t.wait_ge(p.AC.h, tok_q2cp[3] if tt < 2 else tok_scp[tt - 2])
            off = 512 * (tt % 2)
            last = None
            for c in range(4):
                last = t.matmul(p.psD[:, off + 128 * c:off + 128 * (c + 1)],
                                p.q2[:, c, 128 * tt:128 * (tt + 1)],
                                p.keysbd[:, c, :], start=True, stop=True)
            assert p.TE.inc(last) == tok_sim[tt]
        for c in range(32):
            wdma(t, t_wu[c])
            wdma(t, t_wg[c])
            if c == 0:
                t.wait_ge(p.DV.h, p.x_dv)
            if c == 1:
                t.wait_ge(p.AC.h, tok_scp[7])
            psG = p.psA if c % 2 == 0 else p.psC
            psU = p.psB if c % 2 == 0 else p.psD
            if c >= 2:
                t.wait_ge(p.AC.h, tok_ucp[c - 2])
            last = None
            for dd in range(CH):
                for hh in range(2):
                    last = t.matmul(psG[:, 512 * hh:512 * (hh + 1)],
                                    wgv4[:, c % 2, dd, :],
                                    p.xn[:, dd, 512 * hh:512 * (hh + 1)],
                                    start=(dd == 0), stop=(dd == CH - 1))
            assert p.TE.inc(last) == tok_Gmm[c]
            for dd in range(CH):
                for hh in range(2):
                    last = t.matmul(psU[:, 512 * hh:512 * (hh + 1)],
                                    wgv4[:, 2 + c % 2, dd, :],
                                    p.xn[:, dd, 512 * hh:512 * (hh + 1)],
                                    start=(dd == 0), stop=(dd == CH - 1))
            assert p.TE.inc(last) == tok_Umm[c]
        t.wait_ge(p.PO.h, tok_Ap[15])
        t.wait_ge(p.DW3.h, p.t_bdr)
        psOs = [p.psA, p.psB, p.psC, p.psD]
        for do in range(8):
            psO = psOs[do % 4]
            wdma(t, t_wd[2 * do])
            if do >= 4:
                t.wait_ge(p.DV.h, tok_ev[do - 4])
            last = None
            for c in range(16):
                for hh in range(2):
                    last = t.matmul(psO[:, 512 * hh:512 * (hh + 1)],
                                    wdv[:, 0, c, :],
                                    p.big[:, c, 512 * hh:512 * (hh + 1)],
                                    start=(c == 0), stop=False)
            assert p.TE.inc(last) == tok_dmid[do]
            wdma(t, t_wd[2 * do + 1])
            for c in range(16, 32):
                for hh in range(2):
                    t.matmul(psO[:, 512 * hh:512 * (hh + 1)],
                             wdv[:, 1, c % 16, :],
                             p.big[:, c, 512 * hh:512 * (hh + 1)],
                             start=False, stop=False)
            last = None
            for hh in range(2):
                last = t.matmul(psO[:, 512 * hh:512 * (hh + 1)],
                                p.bdr[0:1, 128 * do:128 * (do + 1)], p.ones_r[:],
                                start=False, stop=True)
            assert p.TE.inc(last) == tok_dmm[do]
        t.wait_ge(p.DV.h, tok_ev[7])
        for ec in range(32):
            wdma(t, t_dts[ec])
            psH = p.psA if ec % 2 == 0 else p.psB
            if ec >= 2:
                t.wait_ge(p.AC.h, tok_hds[ec - 2])
            last = None
            for dd in range(CH):
                for hh in range(2):
                    last = t.matmul(psH[:, 512 * hh:512 * (hh + 1)], dtv[:, ec % 4, dd, :],
                                    p.xn[:, dd, 512 * hh:512 * (hh + 1)],
                                    start=(dd == 0), stop=(dd == CH - 1))
            assert p.TE.inc(last) == tok_hdm[ec]

    @blk.scalar
    def _(s):
        wl(s, p.q2buf_free)
        wl(s, p.simbuf_free)
        wl(s, hold0)
        s.wait_ge(p.DW3.h, p.tok_const)
        for oc in range(4):
            psQ = p.psC if oc % 2 == 0 else p.psD
            s.wait_ge(p.TE.h, tok_q2mm[oc])
            a = p.AC.inc(s.activation(p.q2[:, oc, :], psQ[:], AF.Copy))
            assert a == tok_q2cp[oc]
        for tt in range(NTL):
            off = 512 * (tt % 2)
            s.wait_ge(p.TE.h, tok_sim[tt])
            a = p.AC.inc(s.activation(p.simbuf[:, 512 * tt:512 * (tt + 1)],
                                      p.psD[:, off:off + 512], AF.Copy))
            assert a == tok_scp[tt]
        s.wait_ge(p.AC.h, tok_scp[7])
        wl(s, p.gT_free_l)
        p.AC.inc(s.activation(tmp[:, 0:2048], p.simbuf[:, 0:2048], AF.Exp))
        a = p.AC.inc(s.activation(tmp[:, 2048:4096], p.simbuf[:, 2048:4096], AF.Exp))
        assert a == t_exn
        s.wait_ge(p.DV.h, t_max_last)
        p.AC.inc(s.activation(e8f[:, 0:256].rearrange("p (t h k) -> p t h k",
                                                      t=NTL, h=NH),
                              p.s1v8[:, :, 0:4, :], AF.Exp))
        a = p.AC.inc(s.activation(e8f[:, 256:512].rearrange("p (t h k) -> p t h k",
                                                            t=NTL, h=NH),
                                  p.s1v8[:, :, 4:8, :], AF.Exp))
        assert a == t_e8
        for c in range(32):
            psG = p.psA if c % 2 == 0 else p.psC
            psU = p.psB if c % 2 == 0 else p.psD
            s.wait_ge(p.TE.h, tok_Gmm[c])
            if c >= 4:
                s.wait_ge(p.PO.h, tok_Ap[(c - 4) // 2])
            a = p.AC.inc(s.activation(p.silb[:, c % 4, :],
                                      psG[:], AF.Silu, bias=p.bias[:, c:c + 1]))
            assert a == tok_sil[c]
            s.wait_ge(p.TE.h, tok_Umm[c])
            a = p.AC.inc(s.activation(p.big[:, c, :], psU[:], AF.Identity,
                                      bias=p.bias[:, 32 + c:33 + c]))
            assert a == tok_ucp[c]
        if tokA_tap:
            s.wait_ge(p.DS.h, tokA_tap)
        for ec in range(32):
            psH = p.psA if ec % 2 == 0 else p.psB
            s.wait_ge(p.TE.h, tok_hdm[ec])
            a = p.AC.inc(s.activation(p.big[:, ec, :], psH[:], AF.Silu))
            assert a == tok_hds[ec]

    dests = [p.gT[:], p.scrG[:].rearrange("p (c t) -> p c t", c=32),
             gacc2.rearrange("p (c t) -> p c t", c=32)]
    trp_tok = [None] * NTL
    tok_wt = [None] * NTL
    for tt in range(NTL):
        @blk.sync
        def _(s, tt=tt):
            wdma(s, tok_st[tt])
            if tt == 0:
                s.wait_ge(p.DV.h, t_G[NTL - 1])
            elif tt == 1:
                wdma(s, tok_st[6])
            elif tt == 2:
                wdma(s, tok_st[7])
            else:
                s.wait_ge(p.DV.h, tok_wt[tt - 3])
            s.wait_ge(p.DT.h, p.DT.n)
            trp_tok[tt] = p.DT.inc(s.dma_start_transpose(dests[tt % 3], p.d_gd[g % 2, tt]), 16)

        @blk.vector
        def _(v, tt=tt):
            v.wait_ge(p.AC.h, tok_hds[31])
            v.wait_ge(p.DT.h, trp_tok[tt])
            sl = p.big[:, :, 128 * tt:128 * (tt + 1)]
            tok_wt[tt] = p.DV.inc(v.tensor_mul(sl, sl, dests[tt % 3]))

    tk = tap(p, f"WT{FFN_CALL[0]}_{g}", p.big[:].rearrange("p c t -> p (c t)"),
             p.DV, tok_wt[NTL - 1])
    if tk:
        p.tap_hold.append((p.DS.h, tk))

    p.psD_free = (p.DV, tok_ev[7])
    p.scr_free = (p.PO, tok_Ap[15])
    p.xn_free = (p.TE, tok_hdm[31])
    ln_fin = next_ln() if next_ln is not None else None

    t_p2b0 = p.TE.n
    assert t_p2b0 == te0 + 124 + (18 if next_ln is not None else 0)
    bump = 2 if next_ln is not None else 0
    tok_ueq = {(do, qq): t_p2b0 + 4 * do + qq + 1 + (bump if do >= 2 else 0)
               for do in range(8) for qq in range(4)}
    tok_umm = [tok_ueq[(do, 3)] for do in range(8)]

    uev = p.wr1[:].rearrange("p (b c n) -> p b c n", b=4, c=8)
    ue_r = p.d_ue.rearrange("(c p) n -> p c n", p=128)
    tok_ue = {}
    tok_uev = [0] * 8
    for do in range(8):
        psO = [p.psA, p.psB, p.psC][do % 3]
        for qq in range(4):
            j = 4 * do + qq

            @blk.gpsimd
            def _(gg, do=do, qq=qq, j=j):
                if j < 4:
                    gg.wait_ge(p.TE.h, tok_hdm[28 + j])
                else:
                    pj = j - 4
                    gg.wait_ge(p.TE.h, tok_ueq[(pj // 4, pj % 4)])
                tok_ue[(do, qq)] = dma(gg, p.DW2, uev[:, j % 4, :, :],
                                       ue_r[:, 8 * qq:8 * (qq + 1), 128 * do:128 * (do + 1)])

            @blk.tensor
            def _(t, do=do, qq=qq, j=j, psO=psO):
                wdma(t, tok_ue[(do, qq)])
                if qq == 0:
                    t.wait_ge(p.DV.h, tok_wt[NTL - 1])
                    if do >= 3:
                        t.wait_ge(p.DV.h, tok_uev[do - 3])
                last = None
                for cc in range(8):
                    c = 8 * qq + cc
                    for hh in range(2):
                        last = t.matmul(psO[:, 512 * hh:512 * (hh + 1)],
                                        uev[:, j % 4, cc, :],
                                        p.big[:, c, 512 * hh:512 * (hh + 1)],
                                        start=(c == 0), stop=(c == 31))
                a = p.TE.inc(last)
                assert a == tok_ueq[(do, qq)]

        @blk.vector
        def _(v, do=do, g=g, psO=psO):
            v.wait_ge(p.TE.h, tok_umm[do])
            tok_uev[do] = p.DV.inc(v.tensor_tensor(
                p.x[:, do, TG * g:TG * (g + 1)], psO[:],
                p.x[:, do, TG * g:TG * (g + 1)], ALU.add))

        if do == 1 and ln_fin is not None:
            p.next_tok_xn = ln_fin()
            ln_fin = None

    if ln_fin is not None:
        p.next_tok_xn = ln_fin()
    p.x_dv = tok_uev[7]
    p.psC_free = tok_uev[5]
    p.big_free = (p.TE, tok_umm[7])
    p.ring_free = (p.TE, tok_umm[7])
    p.ring2_free = (p.TE, tok_q2mm[3])
    p.ub_free = (p.TE, tok_dmm[7])
    p.p2b_last_te = tok_umm[7]
    p.tok_wt_last = tok_wt[NTL - 1]
    p.last_trp = trp_tok[NTL - 1]
    p.q2buf_free = [(p.TE.h, tok_umm[7]), (p.DV.h, tok_wt[NTL - 1])]
    p.simbuf_free = [(p.DV.h, tok_wt[NTL - 1]), (tok_st[7][0].h, tok_st[7][1])]
    p.scrG_free_l = [(p.DV.h, tok_wt[NTL - 1]), (tok_st[6][0].h, tok_st[6][1])]
    p.gT_free_l = [(p.DV.h, tok_wt[NTL - 1])]
    scope.__exit__(None, None, None)


def emit_attn(p):
    blk = p.blk
    scope = p.nc.named_scope("attn_proj")
    scope.__enter__()

    wkvv = p.q2.rearrange("p o t -> p (o t)")[:, 0:2048].rearrange("p (d n) -> p d n", d=CH)
    wkv_r = p.d_wkv.rearrange("(d p) n -> p d n", p=128)
    wq_r = p.d_wq.rearrange("(d p) n -> p d n", p=128)
    wqh = p.wr1[:, 0:2048].rearrange("p (b d n) -> p b d n", b=2, d=CH)
    t_wkv = [0]

    @blk.gpsimd
    def _(gg):
        gg.wait_ge(p.TE.h, p.p2b_last_te)
        gg.wait_ge(p.DV.h, p.x_dv)
        wl(gg, p.simbuf_free)
        wl(gg, p.tap_hold)
        p.t_pe = dma(gg, p.DW3, p.pe, p.d_pet)
        p.t_ident = dma(gg, p.DW3, p.ident, p.d_ident)
        p.t_stair = dma(gg, p.DW3, p.stair.rearrange("p a b -> p (a b)"), p.d_stair)
        t_wkv[0] = dma(gg, p.DW3, wkvv, wkv_r)

    kk = p.scrP[:, 0:2048]
    vv = p.scrP[:, 2048:4096]
    psT = p.psB[:].bitcast(BF16)
    tok_vt = [0] * 16

    def emit_vt(c):
        t_vtm = [0]

        @blk.tensor
        def _(t, c=c):
            t.wait_ge(p.DV.h, tok_kv[1])
            t.wait_ge(p.DW3.h, p.t_ident)
            if c > 0:
                t.wait_ge(p.DV.h, tok_vt[c - 1])
            t_vtm[0] = p.TE.inc(t.transpose(psT[:, 0:128], vv[:, 128 * c:128 * (c + 1)],
                                            p.ident[:]))

        @blk.vector
        def _(v, c=c):
            v.wait_ge(p.TE.h, t_vtm[0])
            if c == 0:
                v.wait_ge(p.DV.h, p.tok_wt_last)
            tok_vt[c] = p.DV.inc(v.tensor_copy(p.gT[:, c, :], psT[:, 0:128]))
    ln_scr = p.big[:, 24:27, :].rearrange("p a b -> p (a b)").bitcast(F32)
    tok_kv = [0, 0]
    t_kv_k = [0, 0]
    tok_qh = [[0] * 2 for _ in range(NHEAD)]
    t_qev = [0] * NHEAD
    t_kvev = [0]
    t_qmm = [0]
    p.t_wqh = [0] * 16
    p.t_qmm_j = [0] * 16
    for half in range(2):
        if half == 0 and p.next_tok_xn is not None:
            tok_xn = p.next_tok_xn
            p.next_tok_xn = None
        elif half == 0:
            tok_xn = emit_ln(p, TG * half, ln_scr, (p.TE, p.p2b_last_te))
        else:
            p.psD_free = (p.DV, t_kv_k[0])
            tok_xn = emit_ln(p, TG * half, ln_scr, (p.TE, p.p2b_last_te),
                             bank=p.psA)
        for j in range(2):
            psK = p.psA if j == 0 else p.psB

            @blk.tensor
            def _(t, j=j, half=half, psK=psK, tok_xn=tok_xn):
                t.wait_ge(p.DW3.h, t_wkv[0])
                t.wait_ge(p.DV.h, tok_xn)
                if half == 1:
                    t.wait_ge(p.DV.h, t_kvev[0])
                else:
                    t.wait_ge(p.DV.h, p.x_dv)
                last = None
                for dd in range(CH):
                    for hh in range(2):
                        last = t.matmul(psK[:, 512 * hh:512 * (hh + 1)],
                                        wkvv[:, dd, 128 * j:128 * (j + 1)],
                                        p.xn[:, dd, 512 * hh:512 * (hh + 1)],
                                        start=(dd == 0), stop=(dd == CH - 1))
                p.t_kvmm = p.TE.inc(last)

            @blk.vector
            def _(v, j=j, half=half, psK=psK):
                v.wait_ge(p.TE.h, p.t_kvmm)
                v.wait_ge(p.DW3.h, p.t_pe)
                dst = (kk if j == 0 else vv)[:, TG * half:TG * (half + 1)]
                if j == 0:
                    tok_kv[half] = p.DV.inc(v.scalar_tensor_tensor(
                        dst, psK[:], p.bias[:, 72:73], p.pe[:, TG * half:TG * (half + 1)],
                        ALU.add, ALU.add))
                    t_kv_k[half] = p.DV.n
                else:
                    tok_kv[half] = p.DV.inc(v.tensor_scalar_add(dst, psK[:], p.bias[:, 73:74]))
                t_kvev[0] = p.DV.n

        for h in range(NHEAD):
            psQ = p.psC if h % 2 == 0 else p.psD
            j = 8 * half + h

            @blk.gpsimd
            def _(gg, h=h, half=half, j=j):
                if j < 2:
                    gg.wait_ge(p.TE.h, p.p2b_last_te)
                else:
                    gg.wait_ge(p.TE.h, p.t_qmm_j[j - 2])
                p.t_wqh[j] = dma(gg, p.DW1, wqh[:, j % 2, :, :],
                                 wq_r[:, :, 128 * h:128 * (h + 1)])

            @blk.tensor
            def _(t, h=h, half=half, j=j, psQ=psQ, tok_xn=tok_xn):
                wdma(t, p.t_wqh[j])
                t.wait_ge(p.DV.h, tok_xn)
                prev = t_qev[h - 2] if h >= 2 else (t_qev[NHEAD - 2 + h] if half == 1 else 0)
                t.wait_ge(p.DV.h, prev)
                if half == 0 and h < 2:
                    t.wait_ge(p.DV.h, p.ln_psD_free_dv)
                last = None
                for dd in range(CH):
                    for hh in range(2):
                        last = t.matmul(psQ[:, 512 * hh:512 * (hh + 1)], wqh[:, j % 2, dd, :],
                                        p.xn[:, dd, 512 * hh:512 * (hh + 1)],
                                        start=(dd == 0), stop=(dd == CH - 1))
                t_qmm[0] = p.TE.inc(last)
                p.t_qmm_j[j] = t_qmm[0]

            @blk.vector
            def _(v, h=h, half=half, psQ=psQ):
                v.wait_ge(p.TE.h, t_qmm[0])
                v.wait_ge(p.DW3.h, p.t_pe)
                if half == 0 and h == 0:
                    v.wait_ge(p.TE.h, p.p2b_last_te)
                tok_qh[h][half] = p.DV.inc(v.scalar_tensor_tensor(
                    p.big[:, 2 * h + half, :], psQ[:], p.bias[:, 64 + h:65 + h],
                    p.pe[:, TG * half:TG * (half + 1)], ALU.add, ALU.add))
                t_qev[h] = p.DV.n

            if half == 1:
                emit_vt(2 * h)
                emit_vt(2 * h + 1)
        p.xn_free = (p.TE, t_qmm[0])

    scope.__exit__(None, None, None)
    scope = p.nc.named_scope("attn_heads")
    scope.__enter__()
    Ev = p.scrG[:].rearrange("p (s t) -> p s t", s=8)
    zr = p.rows[0:1, 0:512]
    zbs = p.gT[:, 16:20, :].rearrange("p a b -> p (a b)")
    tok_att = {-2: 0, -1: 0}
    t_zr = {-2: 0, -1: 0}
    t_zbs = {-2: 0, -1: 0}
    pv_tok = {}
    pv_last = {}
    exp_prev = [0, 0]
    t_zb = {}

    for h in range(NHEAD):
        for Q in range(4):
            nkc = 4 * Q + 4
            idx = 4 * h + Q
            npair = nkc // 2
            psOt = p.psA if idx % 2 == 0 else p.psB
            psO = psOt[:, 0:512]
            zrow = psOt[0:1, 512:1024]
            teI, acI, dvI = p.TE.n, p.AC.n, p.DV.n
            tok_exp = [acI + (kc // 2) + 1 for kc in range(nkc)]
            t_mask = dvI + 1
            t_sm_toks = []
            qblk = p.big[:, 2 * h + Q // 2, 512 * (Q % 2):512 * (Q % 2) + 512]

            def pv(t, kc, first, last_flag, idx=idx, psO=psO, zrow=zrow, nkc=nkc,
                   tok_exp=tok_exp, t_mask=t_mask):
                slot = kc % 8
                if kc >= nkc - 4:
                    t.wait_ge(p.DV.h, t_mask)
                else:
                    t.wait_ge(p.AC.h, tok_exp[kc])
                if first:
                    t.wait_ge(p.DV.h, tok_vt[15])
                    t.wait_ge(p.DV.h, t_zr[idx - 2])
                    t.wait_ge(p.DV.h, tok_att[idx - 2])
                    if idx < 2:
                        t.wait_ge(p.DV.h, t_qev[NHEAD - 1])
                t.matmul(zrow, p.onesb_c[:], Ev[:, slot, :], start=first, stop=last_flag)
                mm = t.matmul(psO, p.gT[:, kc, :], Ev[:, slot, :], start=first, stop=last_flag)
                pv_tok[slot] = p.TE.inc(mm)
                if last_flag:
                    pv_last[idx] = pv_tok[slot]

            @blk.tensor
            def _(t, h=h, Q=Q, idx=idx, nkc=nkc, npair=npair, qblk=qblk, tok_exp=tok_exp):
                t.wait_ge(p.DV.h, tok_kv[1])
                t.wait_ge(p.DV.h, tok_qh[h][Q // 2])
                pend_pv = []
                for pr in range(npair):
                    kc0 = 2 * pr
                    psS = p.psC if pr % 2 == 0 else p.psD
                    if kc0 >= 4:
                        t.wait_ge(p.AC.h, tok_exp[kc0 - 4])
                    else:
                        t.wait_ge(p.AC.h, exp_prev[pr % 2])
                        if idx == 0:
                            t.wait_ge(p.DV.h, t_qev[NHEAD - 1])
                    if pr % 2 == 1:
                        t.wait_ge(p.DV.h, t_zbs[idx - 2])
                    mm = None
                    for j in range(2):
                        kc = kc0 + j
                        mm = t.matmul(psS[:, 512 * j:512 * (j + 1)],
                                      kk[:, 128 * kc:128 * (kc + 1)], qblk,
                                      start=True, stop=True)
                    t_sm_toks.append(p.TE.inc(mm))
                    if kc0 + 1 < nkc - 4:
                        pend_pv.append(kc0)
                        if len(pend_pv) > 1:
                            k0 = pend_pv.pop(0)
                            pv(t, k0, first=(k0 == 0), last_flag=False)
                            pv(t, k0 + 1, first=False, last_flag=False)
                for k0 in pend_pv:
                    pv(t, k0, first=(k0 == 0), last_flag=False)
                    pv(t, k0 + 1, first=False, last_flag=False)
                for kc in range(nkc - 4, nkc):
                    pv(t, kc, first=(kc == 0), last_flag=(kc == nkc - 1))
                if idx >= 1:
                    t.wait_ge(p.DV.h, t_zr[idx - 1])
                    t.wait_ge(p.DV.h, t_zbs[idx - 2])
                    t.wait_ge(p.AC.h, tok_exp[2 * (npair - 1)])
                    t_zb[idx - 1] = p.TE.inc(t.matmul(p.psD[:, 0:512], p.ones_rf[:], zr,
                                                      start=True, stop=True))

            @blk.scalar
            def _(s, h=h, Q=Q, idx=idx, nkc=nkc, npair=npair, tok_exp=tok_exp,
                  t_sm_toks=t_sm_toks):
                for pr in range(npair):
                    kc0 = 2 * pr
                    slot = kc0 % 8
                    psS = p.psC if pr % 2 == 0 else p.psD
                    s.wait_ge(p.TE.h, t_sm_toks[pr])
                    for sl in (slot, slot + 1):
                        if sl in pv_tok and pv_tok[sl] < teI:
                            s.wait_ge(p.TE.h, pv_tok[sl])
                    if idx == 0 and kc0 == 0:
                        s.wait_ge(p.DT.h, p.last_trp)
                        s.wait_ge(p.DV.h, p.tok_wt_last)
                    a = p.AC.inc(s.activation(Ev[:, slot:slot + 2, :], psS[:, 0:1024],
                                              AF.Exp, scale=SCALE))
                    assert a == tok_exp[kc0]

            s0 = (nkc - 4) % 8

            @blk.vector
            def _(v, idx=idx, s0=s0, nkc=nkc, tok_exp=tok_exp, psO=psO, zrow=zrow):
                v.wait_ge(p.AC.h, tok_exp[nkc - 1])
                v.wait_ge(p.DW3.h, p.t_stair)
                a = p.DV.inc(v.tensor_mul(Ev[:, s0:s0 + 4, :], Ev[:, s0:s0 + 4, :],
                                          p.stair[:]))
                assert a == t_mask
                if idx >= 1:
                    v.wait_ge(p.TE.h, t_zb[idx - 1])
                    t1 = p.DV.inc(v.tensor_copy(zbs[:, 0:512], p.psD[:, 0:512]))
                    v.wait_ge(p.DV.h, t1)
                    t_zbs[idx - 1] = t1
                    pidx = idx - 1
                    ph, pQ = pidx // 4, pidx % 4
                    ppsO = (p.psA if pidx % 2 == 0 else p.psB)[:, 0:512]
                    tok_att[pidx] = p.DV.inc(v.tensor_mul(
                        p.big[:, 16 + 2 * ph + pQ // 2, 512 * (pQ % 2):512 * (pQ % 2) + 512],
                        ppsO, zbs[:, 0:512]))
                v.wait_ge(p.TE.h, pv_last[idx])
                t_zr[idx] = p.DV.inc(v.reciprocal(zr, zrow))

            exp_prev[0] = tok_exp[2 * (npair - 2)]
            exp_prev[1] = tok_exp[2 * (npair - 1)]

    lidx = 4 * NHEAD - 1

    @blk.tensor
    def _(t):
        t.wait_ge(p.DV.h, t_zr[lidx])
        t.wait_ge(p.DV.h, t_zbs[lidx - 1])
        t.wait_ge(p.AC.h, exp_prev[1])
        t_zb[lidx] = p.TE.inc(t.matmul(p.psD[:, 0:512], p.ones_rf[:], zr, start=True, stop=True))

    @blk.vector
    def _(v):
        v.wait_ge(p.TE.h, t_zb[lidx])
        t1 = p.DV.inc(v.tensor_copy(zbs[:, 0:512], p.psD[:, 0:512]))
        v.wait_ge(p.DV.h, t1)
        ph, pQ = lidx // 4, lidx % 4
        ppsO = (p.psA if lidx % 2 == 0 else p.psB)[:, 0:512]
        tok_att[lidx] = p.DV.inc(v.tensor_mul(
            p.big[:, 16 + 2 * ph + pQ // 2, 512 * (pQ % 2):512 * (pQ % 2) + 512],
            ppsO, zbs[:, 0:512]))
    scope.__exit__(None, None, None)
    scope = p.nc.named_scope("attn_wo")
    scope.__enter__()
    wov = p.wr1[:, 0:2048].rearrange("p (b d n) -> p b d n", b=2, d=CH)
    wo_r = p.d_wo.rearrange("(d p) n -> p d n", p=128)
    tok_wo = [0] * 8
    tok_omm = [0] * 8
    tok_oev = [0] * 16
    for oc in range(8):
        @blk.gpsimd
        def _(gg, oc=oc):
            if oc < 2:
                gg.wait_ge(p.TE.h, p.t_qmm_j[15])
            else:
                gg.wait_ge(p.TE.h, tok_omm[oc - 2])
            tok_wo[oc] = dma(gg, p.DW1, wov[:, oc % 2, :, :],
                             wo_r[:, :, 128 * oc:128 * (oc + 1)])
        for blkk in range(2):
            psX = p.psC if (2 * oc + blkk) % 2 == 0 else p.psD
            t_omm = [0]

            @blk.tensor
            def _(t, oc=oc, blkk=blkk, psX=psX):
                wdma(t, tok_wo[oc])
                t.wait_ge(p.DV.h, tok_att[4 * NHEAD - 1])
                j = 2 * oc + blkk
                if j >= 2:
                    t.wait_ge(p.DV.h, tok_oev[j - 2])
                last = None
                for hh in range(CH):
                    for q2_ in range(2):
                        last = t.matmul(psX[:, 512 * q2_:512 * (q2_ + 1)], wov[:, oc % 2, hh, :],
                                        p.big[:, 16 + 2 * hh + blkk, 512 * q2_:512 * (q2_ + 1)],
                                        start=(hh == 0), stop=(hh == CH - 1))
                t_omm[0] = p.TE.inc(last)
                tok_omm[oc] = t_omm[0]

            @blk.vector
            def _(v, oc=oc, blkk=blkk, psX=psX):
                v.wait_ge(p.TE.h, t_omm[0])
                if 2 * oc + blkk == 0:
                    v.wait_ge(p.DV.h, tok_att[4 * NHEAD - 1])
                tok_oev[2 * oc + blkk] = p.DV.inc(v.tensor_tensor(
                    p.x[:, oc, TG * blkk:TG * (blkk + 1)], psX[:],
                    p.x[:, oc, TG * blkk:TG * (blkk + 1)], ALU.add))
    scope.__exit__(None, None, None)
    p.x_dv = tok_oev[15]
    p.psC_free = tok_oev[14]
    p.psD_free = (p.DV, tok_oev[15])
    p.big_free = (p.TE, tok_omm[7])
    p.ring_free = (p.TE, tok_omm[7])
    p.ub_free = (p.TE, tok_omm[7])
    p.t_c4 = t_qev[NHEAD - 1]
    p.scr_free = (p.TE, tok_omm[7])
    p.q2buf_free = [(p.TE.h, tok_omm[7]), (p.DV.h, tok_oev[15])]
    p.scrG_free_l = [(p.TE.h, tok_omm[7])]
    p.gT_free_l = [(p.TE.h, tok_omm[7]), (p.DV.h, tok_oev[15])]
    p.simbuf_free = [(p.DV.h, tok_oev[15])]
    p.ring2_free = (p.DV, tok_oev[15])


def emit_store(p):
    blk = p.blk

    @blk.sync
    def _(s):
        s.wait_ge(p.DV.h, p.x_dv)
        for c in range(CH):
            dma(s, p.DS, p.d_out[128 * c:128 * (c + 1), :], p.x[:, c, :])
        s.wait_ge(p.DS.h, p.DS.n)
        s.wait_ge(p.DT.h, p.DT.n)


# version 26
# speedup vs baseline: 1.0433x; 1.0325x over previous
import math
import contextlib
import numpy as np
import concourse.bass as bass
import concourse.mybir as mybir
from concourse.bass_utils import run_bass_kernel_spmd
import concourse.dve_ops as _dve_ops
from concourse.dve_op import DveOp as _DveOp
from concourse.dve_spec import Spec as _Spec, Src0 as _Src0, Src1 as _Src1, C0 as _C0, \
    Zero as _Zero, select as _select
from concourse.dve_table_gen import dve_ver_for as _dve_ver_for

F32 = mybir.dt.float32
BF16 = mybir.dt.bfloat16
AF = mybir.ActivationFunctionType
ALU = mybir.AluOpType
AX = mybir.AxisListType

NCORE = 8
D = 1024
T = 2048
CH = 8
DC = 4096
NE = 4096
NH = 4
TG = 1024
NTL = 8
EPS = 1e-5
NHEAD = 8
SCALE = 1.0 / math.sqrt(128.0)

TAPS = {}


def _sel_add_ref(in0, in1, s0, s1, imm2):
    return np.where(in0 >= s0, in0, 0.0) + in1


def _register_dve_op(name, spec, ref):
    for o in _dve_ops.OPS:
        if o.name == name:
            return o
    ver = _dve_ver_for("TRN2")
    op = _DveOp(name, spec, subdim=False, uops_sha={}, reference=ref)
    try:
        op.compile(ver)
        sha = None
    except ValueError as e:
        import re as _re
        m = _re.search(r"\(%s: (\w+)" % ver, str(e))
        sha = m.group(1)
    if sha is not None:
        op = _DveOp(name, spec, subdim=False, uops_sha={ver: sha}, reference=ref)
        op.compile(ver)
    _dve_ops.OPS.append(op)
    _dve_ops._SUB_OPCODE_FOR_NAME[op.name] = (_dve_ops._CUSTOM_DVE_ROW_BASE
                                              + len(_dve_ops.OPS) - 1)
    _dve_ops.CUSTOM_DVE_SPECS[op.name] = spec
    _dve_ops.CUSTOM_DVE_REFS[op.name] = ref
    return op


XBAR_TRANSPOSE = True


def _host_consts():
    pos = np.arange(T, dtype=np.float64)[:, None]
    inv = np.exp(-math.log(10000.0) * np.arange(0, 128, 2, dtype=np.float64) / 128.0)
    ang = pos * inv[None, :]
    pe = np.zeros((T, 128), dtype=np.float64)
    pe[:, 0::2] = np.sin(ang)
    pe[:, 1::2] = np.cos(ang)
    pet = np.ascontiguousarray(pe.T.astype(np.float32))
    pp = np.arange(128)[:, None, None]
    cc = np.arange(4)[None, :, None]
    jj = np.arange(512)[None, None, :]
    stair = ((128 * cc + pp) <= jj).astype(np.float32).reshape(128, 4 * 512)
    ident = np.eye(128, dtype=np.float32)
    return pet, stair, ident


def kernel(**inp):
    pet, stair, ident = _host_consts()
    x = np.ascontiguousarray(inp["L0_attn_x"])
    com = {
        "wq": np.ascontiguousarray(inp["L0_attn_Wq"]),
        "wkv": np.ascontiguousarray(inp["L0_attn_Wkv"]),
        "wo": np.ascontiguousarray(inp["L0_attn_Wo"]),
        "wg": np.ascontiguousarray(inp["L0_ffn_Wg"]),
        "wu": np.ascontiguousarray(inp["L0_ffn_Wu"]),
        "wd": np.ascontiguousarray(inp["L0_ffn_Wd"]),
        "wqf": np.ascontiguousarray(inp["L0_ffn_Wq"]),
        "dte": np.ascontiguousarray(inp["L0_ffn_down_embed"].T),
        "ue": np.ascontiguousarray(inp["L0_ffn_up_embed"]),
        "keyst": np.ascontiguousarray(inp["L0_ffn_keys"].transpose(2, 0, 3, 1)),
        "bqc": np.ascontiguousarray(inp["L0_attn_bq"].reshape(8, 128).T),
        "bkvc": np.ascontiguousarray(inp["L0_attn_bkv"].reshape(2, 128).T),
        "bgc": np.ascontiguousarray(inp["L0_ffn_bg"].reshape(32, 128).T),
        "buc": np.ascontiguousarray(inp["L0_ffn_bu"].reshape(32, 128).T),
        "bdr": np.ascontiguousarray(inp["L0_ffn_bd"].reshape(1, 1024)),
        "pet": pet, "stair": stair, "ident": ident,
    }
    nc = build_nc()
    in_maps = []
    for b in range(NCORE):
        m = dict(com)
        m["xt"] = np.ascontiguousarray(x[b].T)
        in_maps.append(m)
    res = run_bass_kernel_spmd(nc, in_maps, core_ids=list(range(NCORE)))
    out = np.stack([np.ascontiguousarray(res.results[b]["out"].T) for b in range(NCORE)], axis=0)
    return out.astype(np.float32)


class Cnt:
    def __init__(self, ctx, nc, name):
        self.h = ctx.enter_context(nc.semaphore(name))
        self.n = 0

    def inc(self, instr, k=1):
        self.n += k
        instr.then_inc(self.h, k)
        return self.n


class LanePair:

    def __init__(self, ctx, nc, name):
        self.lanes = [Cnt(ctx, nc, name + "a"), Cnt(ctx, nc, name + "b")]
        self.i = 0

    def issue(self, eng, out, in_, **kw):
        lane = self.lanes[self.i % 2]
        self.i += 1
        eng.wait_ge(lane.h, lane.n)
        tok = lane.inc(eng.dma_start(out=out, in_=in_, **kw), 16)
        return (lane, tok)


class P:
    pass


def build_nc():
    nc = bass.Bass("TRN2", target_bir_lowering=False)
    p = P()
    p.nc = nc
    dt = nc.dram_tensor
    p.d_xt = dt("xt", [D, T], F32, kind="ExternalInput").ap()
    p.d_wq = dt("wq", [D, D], F32, kind="ExternalInput").ap()
    p.d_wkv = dt("wkv", [D, 256], F32, kind="ExternalInput").ap()
    p.d_wo = dt("wo", [D, D], F32, kind="ExternalInput").ap()
    p.d_wg = dt("wg", [D, DC], F32, kind="ExternalInput").ap()
    p.d_wu = dt("wu", [D, DC], F32, kind="ExternalInput").ap()
    p.d_wd = dt("wd", [DC, D], F32, kind="ExternalInput").ap()
    p.d_wqf = dt("wqf", [D, 512], F32, kind="ExternalInput").ap()
    p.d_dt = dt("dte", [D, NE], F32, kind="ExternalInput").ap()
    p.d_ue = dt("ue", [NE, D], F32, kind="ExternalInput").ap()
    p.d_keyst = dt("keyst", [2, NH, 64, 64], F32, kind="ExternalInput").ap()
    p.d_bqc = dt("bqc", [128, 8], F32, kind="ExternalInput").ap()
    p.d_bkvc = dt("bkvc", [128, 2], F32, kind="ExternalInput").ap()
    p.d_bgc = dt("bgc", [128, 32], F32, kind="ExternalInput").ap()
    p.d_buc = dt("buc", [128, 32], F32, kind="ExternalInput").ap()
    p.d_bdr = dt("bdr", [1, D], F32, kind="ExternalInput").ap()
    p.d_pet = dt("pet", [128, T], F32, kind="ExternalInput").ap()
    p.d_stair = dt("stair", [128, 4 * 512], F32, kind="ExternalInput").ap()
    p.d_ident = dt("ident", [128, 128], F32, kind="ExternalInput").ap()
    p.d_gd = dt("gspill", [2, NTL, 128, NE], BF16, kind="Internal").ap()
    p.d_out = dt("out", [D, T], F32, kind="ExternalOutput").ap()
    p.taps = {name: dt("tap_" + name, list(shape), dtp, kind="ExternalOutput").ap()
              for name, (shape, dtp) in TAPS.items()}

    ctx = contextlib.ExitStack()

    def al(name, shape, dtp):
        return ctx.enter_context(nc.sbuf_tensor(name, shape, dtp))

    p.x = al("x", [128, CH, T], F32)
    p.big = al("big", [128, 32, TG], BF16)
    p.xn = al("xn", [128, CH, TG], BF16)
    p.scrG = al("scrG", [128, NE], BF16)
    p.scrP = al("scrP", [128, NE], BF16)
    p.gT = al("gT", [128, 32, 128], BF16)
    p.wr1 = al("wr1", [128, 4096], BF16)
    p.wr2 = al("wr2", [128, 2048], BF16)
    p.cmpb = al("cmpb", [128, 896], BF16)
    p.simbuf = al("simbuf", [128, 4096], F32)
    p.s1v8 = al("s1v8", [128, 8, 8, 8], F32)
    p.e8all = al("e8all", [128, 8, 64], BF16)
    p.bias = al("bias", [128, 80], F32)
    p.q8 = al("q8", [128, NH, 8], F32)
    p.q8S = al("q8S", [128, NH, 8], BF16)
    p.q9S = al("q9S", [128, NH, 8], BF16)
    p.zal = al("zal", [128, 3 * NH], F32)
    p.ub = al("ub", [128, 128], BF16)
    p.ident = p.ub[:, :]
    p.bdrt = al("bdr_sb", [1, 1024], BF16)
    p.bdr = p.bdrt[0:1, :]
    p.ones_r = al("ones_r", [1, 512], BF16)
    p.ones_rf = al("ones_rf", [1, 128], F32)
    p.ones_c = al("ones_c", [128, 1], F32)
    p.onesb_c = al("onesb_c", [128, 1], BF16)
    p.eps_c = al("eps_c", [128, 1], F32)
    p.keysbd = al("keysbd", [128, 4, 128], BF16)
    p.sil = p.scrP[:].bitcast(F32)
    p.silb = p.scrP[:].rearrange("p (s n) -> p s n", s=4)
    p.pe = p.simbuf[:].bitcast(BF16)[:, 0:2048]
    p.rows = p.simbuf[0:1, 1024:2048]
    p.q2 = p.big[:, 28:32, :]
    p.stair = p.wr2[:].rearrange("p (a b) -> p a b", a=4)

    p.psA = ctx.enter_context(nc.psum_tensor("psA", [128, 1024], F32))
    p.psB = ctx.enter_context(nc.psum_tensor("psB", [128, 1024], F32))
    p.psC = ctx.enter_context(nc.psum_tensor("psC", [128, 1024], F32))
    p.psD = ctx.enter_context(nc.psum_tensor("psD", [128, 1024], F32))

    p.TE = Cnt(ctx, nc, "qTE")
    p.DV = Cnt(ctx, nc, "qDV")
    p.AC = Cnt(ctx, nc, "qAC")
    p.PO = Cnt(ctx, nc, "qPO")
    p.DW1 = LanePair(ctx, nc, "qDW1")
    p.DW2 = LanePair(ctx, nc, "qDW2")
    p.DW3 = Cnt(ctx, nc, "qDW3")
    p.DS = Cnt(ctx, nc, "qDS")
    p.DT = Cnt(ctx, nc, "qDT")
    p.DG = LanePair(ctx, nc, "qDG")
    p.x_dv = 0
    p.xn_free = (p.DV, 0)
    p.psD_free = (p.DV, 0)
    p.ln_psD_free_dv = 0
    p.last_trp = 0
    p.tok_wt_last = 0
    p.scr_free = (p.DV, 0)
    p.t_c4 = 0
    p.big_free = (p.TE, 0)
    p.ring_free = (p.TE, 0)
    p.ring2_free = (p.TE, 0)
    p.ub_free = (p.TE, 0)
    p.psC_free = 0
    p.q2buf_free = []
    p.simbuf_free = []
    p.scrG_free_l = []
    p.gT_free_l = []
    p.tap_hold = []
    p.next_tok_xn = None

    with nc.Block() as blk:
        p.blk = blk
        emit_init(p)
        emit_ffn(p)
        emit_attn(p)
        emit_ffn(p)
        emit_store(p)
    ctx.close()
    return nc


def dma(eng, lane, out, in_, **kw):
    if isinstance(lane, LanePair):
        return lane.issue(eng, out, in_, **kw)
    eng.wait_ge(lane.h, lane.n)
    return lane.inc(eng.dma_start(out=out, in_=in_, **kw), 16)


def wdma(eng, token):
    lane, tok = token
    eng.wait_ge(lane.h, tok)


def wl(eng, lst):
    for h, v in lst:
        eng.wait_ge(h, v)


def tap(p, name, ap_src, cnt, tok):
    if name not in p.taps:
        return None
    blk = p.blk
    out = [None]

    @blk.sync
    def _(s):
        s.wait_ge(cnt.h, tok)
        out[0] = dma(s, p.DS, p.taps[name], ap_src)
    return out[0]


def emit_init(p):
    blk = p.blk

    @blk.sync
    def _(s):
        for c in range(CH):
            dma(s, p.DS, p.x[:, c, :], p.d_xt[128 * c:128 * (c + 1), :])
    p.tok_xin = p.DS.n

    @blk.gpsimd
    def _(g):
        dma(g, p.DW3, p.bias[:, 0:32], p.d_bgc)
        dma(g, p.DW3, p.bias[:, 32:64], p.d_buc)
        dma(g, p.DW3, p.bias[:, 64:72], p.d_bqc)
        dma(g, p.DW3, p.bias[:, 72:74], p.d_bkvc)
        tz = p.PO.inc(g.memset(p.keysbd[:], 0.0))
        g.wait_ge(p.PO.h, tz)
        for c in range(4):
            pp, hlo = c // 2, (c % 2) * 2
            dma(g, p.DW3, p.keysbd[0:64, c, 0:64], p.d_keyst[pp, hlo, :, :])
            dma(g, p.DW3, p.keysbd[64:128, c, 64:128], p.d_keyst[pp, hlo + 1, :, :])
    p.tok_const = p.DW3.n

    @blk.vector
    def _(v):
        p.DV.inc(v.memset(p.ones_r[:], 1.0))
        p.DV.inc(v.memset(p.ones_rf[:], 1.0))
        p.DV.inc(v.memset(p.ones_c[:], 1.0))
        p.DV.inc(v.memset(p.onesb_c[:], 1.0))
        p.DV.inc(v.memset(p.eps_c[:], float(EPS)))
    p.tok_ones = p.DV.n


def emit_ln(p, t0, scratch, scr_free, defer=False, bank=None):
    blk = p.blk
    psb = bank if bank is not None else p.psD
    xnf, psDf = p.xn_free, p.psD_free
    x_dv = p.x_dv
    hold0 = list(p.tap_hold)
    fins = []
    prev_stats = [0]
    tok_rm_l = [0, 0]
    norm_prev = [0]
    for b in range(TG // 512):
        tb = t0 + 512 * b
        base = 1024 + 1024 * b
        mb = p.simbuf[0:1, base:base + 512]
        varb = p.simbuf[0:1, base + 512:base + 1024]
        r = varb
        mr = mb
        te0, ac0, dv0 = p.TE.n, p.AC.n, p.DV.n
        tok_sq = [ac0 + c + 1 for c in range(CH)]
        tok_sd = ac0 + CH + 1
        tok_sumx = te0 + 1
        tok_smm = [te0 + 1 + c + 1 for c in range(CH)]
        tok_stats = dv0 + 3
        tok_rm = dv0 + 5
        tok_rm_l[b] = tok_rm

        @blk.scalar
        def _(s, tb=tb, b=b, r=r, varb=varb, tok_sq=tok_sq, tok_smm=tok_smm,
              tok_stats=tok_stats, tok_sd=tok_sd):
            s.wait_ge(p.DS.h, p.tok_xin)
            s.wait_ge(p.DV.h, x_dv)
            if b == 0:
                s.wait_ge(scr_free[0].h, scr_free[1])
                wl(s, hold0)
            for c in range(CH):
                if c >= 2:
                    s.wait_ge(p.TE.h, tok_smm[c - 2])
                a = p.AC.inc(s.activation(scratch[:, 512 * (c % 2):512 * (c % 2) + 512],
                                          p.x[:, c, tb:tb + 512], AF.Square))
                assert a == tok_sq[c]
            s.wait_ge(p.DV.h, tok_stats)
            assert p.AC.inc(s.activation(r, varb, AF.Sqrt, bias=p.eps_c[0:1, :])) == tok_sd

        @blk.tensor
        def _(t, tb=tb, b=b, tok_sq=tok_sq, tok_smm=tok_smm, tok_sumx=tok_sumx,
              pstats=prev_stats[0]):
            t.wait_ge(p.DS.h, p.tok_xin)
            t.wait_ge(p.DV.h, max(x_dv, p.tok_ones, p.ln_psD_free_dv, pstats))
            if b == 0:
                t.wait_ge(psDf[0].h, psDf[1])
            last = None
            for c in range(CH):
                last = t.matmul(psb[0:1, 0:512], p.ones_c[:], p.x[:, c, tb:tb + 512],
                                start=(c == 0), stop=(c == CH - 1))
            assert p.TE.inc(last) == tok_sumx
            for c in range(CH):
                t.wait_ge(p.AC.h, tok_sq[c])
                mm = t.matmul(psb[32:33, 0:512], p.ones_c[:],
                              scratch[:, 512 * (c % 2):512 * (c % 2) + 512],
                              start=(c == 0), stop=(c == CH - 1))
                assert p.TE.inc(mm) == tok_smm[c]

        @blk.vector
        def _(v, b=b, mb=mb, varb=varb, r=r, mr=mr, tok_smm=tok_smm,
              tok_stats=tok_stats, tok_sd=tok_sd, tok_rm=tok_rm):
            v.wait_ge(p.TE.h, tok_smm[CH - 1])
            t1 = p.DV.inc(v.tensor_scalar_mul(mb, psb[0:1, 0:512], 1.0 / D))
            v.wait_ge(p.DV.h, t1)
            t2 = p.DV.inc(v.tensor_mul(varb, mb, mb))
            v.wait_ge(p.DV.h, t2)
            t3 = p.DV.inc(v.scalar_tensor_tensor(varb, psb[32:33, 0:512], 1.0 / D, varb,
                                                 ALU.mult, ALU.subtract))
            assert t3 == tok_stats
            v.wait_ge(p.AC.h, tok_sd)
            t4 = p.DV.inc(v.reciprocal(r, r))
            v.wait_ge(p.DV.h, t4)
            t5 = p.DV.inc(v.tensor_mul(mr, mb, r))
            assert t5 == tok_rm
        prev_stats[0] = tok_stats

        def fin(b=b, tb=tb, r=r, mr=mr, tok_rm=tok_rm):
            t_bc = [0]

            @blk.tensor
            def _(t):
                t.wait_ge(p.DV.h, max(tok_rm, tok_rm_l[1], norm_prev[0]))
                t.matmul(psb[:, 0:512], p.ones_rf[:], r, start=True, stop=True)
                t_bc[0] = p.TE.inc(t.matmul(psb[:, 512:1024], p.ones_rf[:], mr,
                                            start=True, stop=True))

            @blk.vector
            def _(v):
                v.wait_ge(p.TE.h, t_bc[0])
                v.wait_ge(xnf[0].h, xnf[1])
                if b == 0:
                    wl(v, hold0)
                for c in range(CH):
                    ta = p.DV.inc(v.tensor_mul(scratch[:, 1024:1536], p.x[:, c, tb:tb + 512],
                                               psb[:, 0:512]))
                    v.wait_ge(p.DV.h, ta)
                    t6 = p.DV.inc(v.tensor_sub(p.xn[:, c, 512 * b:512 * (b + 1)],
                                               scratch[:, 1024:1536], psb[:, 512:1024]))
                    v.wait_ge(p.DV.h, t6)
            norm_prev[0] = p.DV.n
            return p.DV.n
        fins.append(fin)

    def finish():
        tok = 0
        for f in fins:
            tok = f()
        p.ln_psD_free_dv = tok
        p.psD_free = (p.DV, tok)
        return tok
    if defer:
        return finish
    return finish()


FFN_CALL = [0]


def emit_ffn(p):
    call = FFN_CALL[0]
    FFN_CALL[0] += 1
    for g in range(2):
        if p.next_tok_xn is not None:
            tok_xn = p.next_tok_xn
            p.next_tok_xn = None
        else:
            with p.nc.named_scope(f"f{call}g{g}_ln"):
                tok_xn = emit_ln(p, TG * g, p.sil, p.scr_free)
        tk = tap(p, f"xn{call}_{g}", p.xn[:].rearrange("p c t -> p (c t)"), p.DV, tok_xn)
        if tk:
            p.tap_hold.append((p.DS.h, tk))
        if g == 0:
            def nl():
                with p.nc.named_scope(f"f{call}g1_ln"):
                    return emit_ln(p, TG, p.sil, p.scr_free, defer=True)
        elif call == 0:
            def nl():
                with p.nc.named_scope("fa0_ln"):
                    return emit_ln(p, 0, p.sil, p.scr_free, defer=True)
        else:
            nl = None
        emit_group(p, g, tok_xn, nl)


def emit_group(p, g, tok_xn, next_ln=None):
    blk = p.blk
    scope = p.nc.named_scope(f"c{FFN_CALL[0]}g{g}_grp")
    scope.__enter__()
    hold0 = list(p.tap_hold)

    te0, ac0, dv0 = p.TE.n, p.AC.n, p.DV.n
    tok_q2mm = [te0 + oc + 1 for oc in range(4)]
    tok_sim = [te0 + 4 + tt + 1 for tt in range(NTL)]
    tok_Gmm = [te0 + 12 + 2 * c + 1 for c in range(32)]
    tok_Umm = [te0 + 12 + 2 * c + 2 for c in range(32)]
    tok_dmid = [te0 + 76 + 2 * do + 1 for do in range(8)]
    tok_dmm = [te0 + 76 + 2 * do + 2 for do in range(8)]
    tok_hdm = [te0 + 92 + ec + 1 for ec in range(32)]
    tok_q2cp = [ac0 + oc + 1 for oc in range(4)]
    tok_scp = [ac0 + 4 + tt + 1 for tt in range(NTL)]
    t_exn = ac0 + 14
    t_e8 = ac0 + 15
    tok_sil = [ac0 + 15 + 2 * c + 1 for c in range(32)]
    tok_ucp = [ac0 + 15 + 2 * c + 2 for c in range(32)]
    tok_hds = [ac0 + 79 + ec + 1 for ec in range(32)]
    po0 = p.PO.n
    tok_Ap = [po0 + j + 1 for j in range(16)]
    t_max_last = dv0 + 64
    t_mask = dv0 + 65
    t_fneg = dv0 + 66
    assert p.DG.i % 2 == 0
    dg0 = [p.DG.lanes[0].n, p.DG.lanes[1].n]
    tok_st = [(p.DG.lanes[tt % 2], dg0[tt % 2] + 16 * (tt // 2 + 1)) for tt in range(NTL)]

    wgv4 = p.wr1[:].rearrange("p (b d n) -> p b d n", b=4, d=CH)
    wqv = p.wr2[:].rearrange("p (d n) -> p d n", d=CH)
    wdv = p.wr1[:].rearrange("p (b c n) -> p b c n", b=2, c=16)
    dtv = p.wr1[:].rearrange("p (b d n) -> p b d n", b=4, d=CH)
    wg_r = p.d_wg.rearrange("(d p) n -> p d n", p=128)
    wu_r = p.d_wu.rearrange("(d p) n -> p d n", p=128)
    wqf_r = p.d_wqf.rearrange("(d p) n -> p d n", p=128)
    wd_r = p.d_wd.rearrange("(c p) n -> p c n", p=128)
    dt_r = p.d_dt.rearrange("(d p) n -> p d n", p=128)
    sbb = p.simbuf[:].bitcast(BF16)
    fnall = sbb[:, 0:4096]
    gacc2 = sbb[:, 4096:8192]
    tmp = p.gT[:].rearrange("p c t -> p (c t)")
    Z, alpha, thresh = p.zal[:, 0:4], p.zal[:, 4:8], p.zal[:, 8:12]
    qc = p.cmpb[:, 0:256].rearrange("p (h k) -> p h k", h=NH)
    qS = p.cmpb[:, 256:512].rearrange("p (h k) -> p h k", h=NH)
    qtmp = p.cmpb[:, 512:576]
    fxs = p.cmpb[:, 576:832].rearrange("p (h n) -> p h n", h=NH)
    e8xs = p.cmpb[:, 832:864].rearrange("p (h k) -> p h k", h=NH)

    t_wg = [None] * 32
    t_wu = [None] * 32
    t_wd = [None] * 16
    t_dts = [None] * 32
    t_wqf = [None, None]

    @blk.gpsimd
    def _(gg):
        gg.wait_ge(p.ring_free[0].h, p.ring_free[1])
        for c in range(2):
            t_wg[c] = dma(gg, p.DW1, wgv4[:, c % 2, :, :], wg_r[:, :, 128 * c:128 * (c + 1)])
            t_wu[c] = dma(gg, p.DW1, wgv4[:, 2 + c % 2, :, :],
                          wu_r[:, :, 128 * c:128 * (c + 1)])
        gg.wait_ge(p.ub_free[0].h, p.ub_free[1])
        p.t_bdr = dma(gg, p.DW3, p.bdr, p.d_bdr)
        gg.wait_ge(p.ring2_free[0].h, p.ring2_free[1])
        t_wqf[0] = dma(gg, p.DW2, wqv[:, :, :], wqf_r[:, :, 0:256])
        gg.wait_ge(p.TE.h, tok_q2mm[1])
        t_wqf[1] = dma(gg, p.DW2, wqv[:, :, :], wqf_r[:, :, 256:512])

        def a_pair(j):
            gg.wait_ge(p.AC.h, tok_ucp[2 * j + 1])
            a = p.PO.inc(gg.tensor_tensor(
                p.big[:, 2 * j:2 * j + 2, :], p.big[:, 2 * j:2 * j + 2, :],
                p.silb[:, (2 * j) % 4:(2 * j) % 4 + 2, :], ALU.mult))
            assert a == tok_Ap[j]

        for c in range(2, 32):
            gg.wait_ge(p.TE.h, tok_Umm[c - 2])
            t_wg[c] = dma(gg, p.DW1, wgv4[:, c % 2, :, :], wg_r[:, :, 128 * c:128 * (c + 1)])
            t_wu[c] = dma(gg, p.DW1, wgv4[:, 2 + c % 2, :, :],
                          wu_r[:, :, 128 * c:128 * (c + 1)])
            if c % 2 == 1:
                a_pair((c - 3) // 2)
        gg.wait_ge(p.TE.h, tok_Umm[31])
        t_wd[0] = dma(gg, p.DW1, wdv[:, 0, :, :], wd_r[:, 0:16, 0:128])
        t_wd[1] = dma(gg, p.DW1, wdv[:, 1, :, :], wd_r[:, 16:32, 0:128])
        a_pair(15)
        for sl in range(2, 16):
            do, hf = sl // 2, sl % 2
            gg.wait_ge(p.TE.h, tok_dmid[do - 1] if hf == 0 else tok_dmm[do - 1])
            t_wd[sl] = dma(gg, p.DW1, wdv[:, sl % 2, :, :],
                           wd_r[:, 16 * hf:16 * (hf + 1), 128 * do:128 * (do + 1)])
        for ec in range(32):
            gg.wait_ge(p.TE.h, tok_dmm[7] if ec < 4 else tok_hdm[ec - 4])
            t_dts[ec] = dma(gg, p.DW1, dtv[:, ec % 4, :, :],
                            dt_r[:, :, 128 * ec:128 * (ec + 1)])

    tok_ev = [None] * 8
    t_G = [None] * NTL
    sg = [p.scrG[:], gacc2]

    def emit_p1b_ev(v, do):
        v.wait_ge(p.TE.h, tok_dmm[do])
        psO = [p.psA, p.psB, p.psC, p.psD][do % 4]
        tok_ev[do] = p.DV.inc(v.scalar_tensor_tensor(
            p.x[:, do, TG * g:TG * (g + 1)], psO[:], 0.5,
            p.x[:, do, TG * g:TG * (g + 1)], ALU.mult, ALU.add))

    def tile_units(tt):
        acc = sg[tt % 2]
        fn_ = fnall[:, 512 * tt:512 * (tt + 1)]
        fnx = fn_[:, 0:256].rearrange("p (h n) -> p h n", h=NH)
        fny = fn_[:, 256:512].rearrange("p (h n) -> p h n", h=NH)
        e8x = p.e8all[:, tt, 0:32].rearrange("p (h k) -> p h k", h=NH)
        e8y = p.e8all[:, tt, 32:64].rearrange("p (h k) -> p h k", h=NH)

        def c1(v):
            v.wait_ge(p.AC.h, t_e8)
            t = p.DV.inc(v.tensor_tensor(qc, e8x.unsqueeze(3).to_broadcast([128, NH, 8, 8]),
                                         e8y.unsqueeze(2).to_broadcast([128, NH, 8, 8]),
                                         ALU.mult))
            v.wait_ge(p.DV.h, t)
            for h in range(NH):
                p.DV.inc(v.max(out=p.q8[:, h, :], in_=qc[:, h, :]))
            v.wait_ge(p.DV.h, p.DV.n)
            t = p.DV.inc(v.reduce_sum(Z, p.q8[:], axis=AX.X))
            v.wait_ge(p.DV.h, t)
            t = p.DV.inc(v.reciprocal(alpha, Z))
            v.wait_ge(p.DV.h, t)
            t = p.DV.inc(v.tensor_scalar_mul(alpha, alpha, 0.5))
            v.wait_ge(p.DV.h, t)

        def c2(v):
            p.DV.inc(v.tensor_tensor(fxs, fnx,
                                     alpha.unsqueeze(2).to_broadcast([128, NH, 64]), ALU.mult))
            t = p.DV.inc(v.tensor_tensor(e8xs, e8x,
                                         alpha.unsqueeze(2).to_broadcast([128, NH, 8]), ALU.mult))
            v.wait_ge(p.DV.h, t)
            t = p.DV.inc(v.tensor_tensor(qS, e8xs.unsqueeze(3).to_broadcast([128, NH, 8, 8]),
                                         e8y.unsqueeze(2).to_broadcast([128, NH, 8, 8]), ALU.mult))
            v.wait_ge(p.DV.h, t)

        def c3(v):
            for h in range(NH):
                p.DV.inc(v.max(out=p.q8S[:, h, :], in_=qS[:, h, :]))
                v.wait_ge(p.DV.h, p.DV.n)
                p.DV.inc(v.match_replace(out=qtmp, in_to_replace=p.q8S[:, h, :],
                                         in_values=qS[:, h, :], imm_value=-1.0))
                v.wait_ge(p.DV.h, p.DV.n)
                p.DV.inc(v.max(out=p.q9S[:, h, :], in_=qtmp))
            v.wait_ge(p.DV.h, p.DV.n)
            t = p.DV.inc(v.tensor_tensor(thresh, p.q8S[:, :, 7], p.q9S[:, :, 0], ALU.add))
            v.wait_ge(p.DV.h, t)
            t = p.DV.inc(v.tensor_scalar_mul(thresh, thresh, 0.5))
            v.wait_ge(p.DV.h, t)

        def mk_b(h):
            def b_(v):
                dst = acc if h == 0 else tmp
                if h == 0 and tt >= 2:
                    wdma(v, tok_st[tt - 2])
                t = p.DV.inc(v.tensor_tensor(
                    dst.rearrange("p (a b) -> p a b", a=64),
                    fxs[:, h, :].unsqueeze(2).to_broadcast([128, 64, 64]),
                    fny[:, h, :].unsqueeze(1).to_broadcast([128, 64, 64]), ALU.mult))
                v.wait_ge(p.DV.h, t)
            return b_

        def mk_s(h):
            def s_(v):
                dst = acc if h == 0 else tmp
                t = p.DV.inc(v.scalar_tensor_tensor(dst, dst, thresh[:, h:h + 1], dst,
                                                    ALU.is_ge, ALU.mult))
                v.wait_ge(p.DV.h, t)
            return s_

        def mk_a():
            def a_(v):
                t = p.DV.inc(v.tensor_add(acc, acc, tmp))
                v.wait_ge(p.DV.h, t)
            return a_

        def fin(v):
            t_G[tt] = p.DV.n

        us = [(2.2, c1), (1.1, c2), (2.8, c3)]
        for h in range(NH):
            us.append((4.4, mk_b(h)))
            us.append((4.5, mk_s(h)))
            if h > 0:
                us.append((2.3, mk_a()))
        us.append((0.0, fin))
        return us

    units = []
    for tt in range(NTL):
        units += tile_units(tt)
    inserts = [(225.0 + 15.7 * do, 1, do) for do in range(8)]

    @blk.vector
    def _(v):
        wl(v, hold0)
        for tt in range(NTL):
            v.wait_ge(p.AC.h, tok_scp[tt])
            for ph in range(8):
                p.DV.inc(v.max(out=p.s1v8[:, tt, ph, :],
                               in_=p.simbuf[:, 512 * tt + 64 * ph:512 * tt + 64 * (ph + 1)]))
        assert p.DV.n == t_max_last
        wl(v, p.scrG_free_l)
        v.wait_ge(p.DV.h, t_max_last)
        a = p.DV.inc(v.tensor_tensor(
            p.scrG[:].rearrange("p (t h n) -> p t h n", t=NTL, h=8),
            p.simbuf[:].rearrange("p (t h n) -> p t h n", t=NTL, h=8),
            p.s1v8[:, :, :, 7:8].to_broadcast([128, NTL, 8, 64]), ALU.is_ge))
        assert a == t_mask
        v.wait_ge(p.DV.h, t_mask)
        v.wait_ge(p.AC.h, t_exn)
        a = p.DV.inc(v.tensor_mul(fnall, p.scrG[:], tmp))
        assert a == t_fneg
        v.wait_ge(p.DV.h, t_fneg)
        clock = 0.0
        ii = 0
        for dur, fn in units:
            while ii < len(inserts) and inserts[ii][0] <= clock:
                emit_p1b_ev(v, inserts[ii][2])
                clock += 1.2
                ii += 1
            fn(v)
            clock += dur
        while ii < len(inserts):
            emit_p1b_ev(v, inserts[ii][2])
            ii += 1

    @blk.sync
    def _(s):
        for tt in range(NTL):
            s.wait_ge(p.DV.h, t_G[tt])
            got = dma(s, p.DG, p.d_gd[g % 2, tt], sg[tt % 2])
            assert got == tok_st[tt]

    for tt in range(NTL):
        gname = f"G{FFN_CALL[0]}_{g}_{tt}"
        if gname in p.taps:
            @blk.sync
            def _(s, tt=tt, gname=gname):
                wdma(s, tok_st[tt])
                tk = dma(s, p.DS, p.taps[gname], p.d_gd[g % 2, tt])
                p.tap_hold.append((p.DS.h, tk))
    tokA_tap = tap(p, f"A{FFN_CALL[0]}_{g}", p.big[:].rearrange("p c t -> p (c t)"),
                   p.PO, tok_Ap[15])

    @blk.tensor
    def _(t):
        t.wait_ge(p.DV.h, max(tok_xn, p.tok_ones, p.psC_free))
        t.wait_ge(p.DW3.h, p.tok_const)
        for oc in range(4):
            pair, k = oc // 2, oc % 2
            if k == 0:
                wdma(t, t_wqf[pair])
            psQ = p.psC if oc % 2 == 0 else p.psD
            if oc >= 2:
                t.wait_ge(p.AC.h, tok_q2cp[oc - 2])
            last = None
            for dd in range(CH):
                for hh in range(2):
                    last = t.matmul(psQ[:, 512 * hh:512 * (hh + 1)],
                                    wqv[:, dd, 128 * k:128 * (k + 1)],
                                    p.xn[:, dd, 512 * hh:512 * (hh + 1)],
                                    start=(dd == 0), stop=(dd == CH - 1))
            assert p.TE.inc(last) == tok_q2mm[oc]
        for tt in range(NTL):
            t.wait_ge(p.AC.h, tok_q2cp[3] if tt < 2 else tok_scp[tt - 2])
            off = 512 * (tt % 2)
            last = None
            for c in range(4):
                last = t.matmul(p.psD[:, off + 128 * c:off + 128 * (c + 1)],
                                p.q2[:, c, 128 * tt:128 * (tt + 1)],
                                p.keysbd[:, c, :], start=True, stop=True)
            assert p.TE.inc(last) == tok_sim[tt]
        for c in range(32):
            wdma(t, t_wu[c])
            wdma(t, t_wg[c])
            if c == 0:
                t.wait_ge(p.DV.h, p.x_dv)
            if c == 1:
                t.wait_ge(p.AC.h, tok_scp[7])
            psG = p.psA if c % 2 == 0 else p.psC
            psU = p.psB if c % 2 == 0 else p.psD
            if c >= 2:
                t.wait_ge(p.AC.h, tok_ucp[c - 2])
            last = None
            for dd in range(CH):
                for hh in range(2):
                    last = t.matmul(psG[:, 512 * hh:512 * (hh + 1)],
                                    wgv4[:, c % 2, dd, :],
                                    p.xn[:, dd, 512 * hh:512 * (hh + 1)],
                                    start=(dd == 0), stop=(dd == CH - 1))
            assert p.TE.inc(last) == tok_Gmm[c]
            for dd in range(CH):
                for hh in range(2):
                    last = t.matmul(psU[:, 512 * hh:512 * (hh + 1)],
                                    wgv4[:, 2 + c % 2, dd, :],
                                    p.xn[:, dd, 512 * hh:512 * (hh + 1)],
                                    start=(dd == 0), stop=(dd == CH - 1))
            assert p.TE.inc(last) == tok_Umm[c]
        t.wait_ge(p.PO.h, tok_Ap[15])
        t.wait_ge(p.DW3.h, p.t_bdr)
        psOs = [p.psA, p.psB, p.psC, p.psD]
        for do in range(8):
            psO = psOs[do % 4]
            wdma(t, t_wd[2 * do])
            if do >= 4:
                t.wait_ge(p.DV.h, tok_ev[do - 4])
            last = None
            for c in range(16):
                for hh in range(2):
                    last = t.matmul(psO[:, 512 * hh:512 * (hh + 1)],
                                    wdv[:, 0, c, :],
                                    p.big[:, c, 512 * hh:512 * (hh + 1)],
                                    start=(c == 0), stop=False)
            assert p.TE.inc(last) == tok_dmid[do]
            wdma(t, t_wd[2 * do + 1])
            for c in range(16, 32):
                for hh in range(2):
                    t.matmul(psO[:, 512 * hh:512 * (hh + 1)],
                             wdv[:, 1, c % 16, :],
                             p.big[:, c, 512 * hh:512 * (hh + 1)],
                             start=False, stop=False)
            last = None
            for hh in range(2):
                last = t.matmul(psO[:, 512 * hh:512 * (hh + 1)],
                                p.bdr[0:1, 128 * do:128 * (do + 1)], p.ones_r[:],
                                start=False, stop=True)
            assert p.TE.inc(last) == tok_dmm[do]
        t.wait_ge(p.DV.h, tok_ev[5])
        for ec in range(32):
            wdma(t, t_dts[ec])
            psH = p.psA if ec % 2 == 0 else p.psB
            if ec >= 2:
                t.wait_ge(p.AC.h, tok_hds[ec - 2])
            last = None
            for dd in range(CH):
                for hh in range(2):
                    last = t.matmul(psH[:, 512 * hh:512 * (hh + 1)], dtv[:, ec % 4, dd, :],
                                    p.xn[:, dd, 512 * hh:512 * (hh + 1)],
                                    start=(dd == 0), stop=(dd == CH - 1))
            assert p.TE.inc(last) == tok_hdm[ec]

    @blk.scalar
    def _(s):
        wl(s, p.q2buf_free)
        wl(s, p.simbuf_free)
        wl(s, hold0)
        s.wait_ge(p.DW3.h, p.tok_const)
        for oc in range(4):
            psQ = p.psC if oc % 2 == 0 else p.psD
            s.wait_ge(p.TE.h, tok_q2mm[oc])
            a = p.AC.inc(s.activation(p.q2[:, oc, :], psQ[:], AF.Copy))
            assert a == tok_q2cp[oc]
        for tt in range(NTL):
            off = 512 * (tt % 2)
            s.wait_ge(p.TE.h, tok_sim[tt])
            a = p.AC.inc(s.activation(p.simbuf[:, 512 * tt:512 * (tt + 1)],
                                      p.psD[:, off:off + 512], AF.Copy))
            assert a == tok_scp[tt]
        s.wait_ge(p.AC.h, tok_scp[7])
        wl(s, p.gT_free_l)
        p.AC.inc(s.activation(tmp[:, 0:2048], p.simbuf[:, 0:2048], AF.Exp))
        a = p.AC.inc(s.activation(tmp[:, 2048:4096], p.simbuf[:, 2048:4096], AF.Exp))
        assert a == t_exn
        s.wait_ge(p.DV.h, t_max_last)
        a = p.AC.inc(s.activation(p.e8all[:].rearrange("p a b -> p (a b)"),
                                  p.s1v8[:].rearrange("p a b c -> p (a b c)"), AF.Exp))
        assert a == t_e8
        for c in range(32):
            psG = p.psA if c % 2 == 0 else p.psC
            psU = p.psB if c % 2 == 0 else p.psD
            s.wait_ge(p.TE.h, tok_Gmm[c])
            if c >= 4:
                s.wait_ge(p.PO.h, tok_Ap[(c - 4) // 2])
            a = p.AC.inc(s.activation(p.silb[:, c % 4, :],
                                      psG[:], AF.Silu, bias=p.bias[:, c:c + 1]))
            assert a == tok_sil[c]
            s.wait_ge(p.TE.h, tok_Umm[c])
            a = p.AC.inc(s.activation(p.big[:, c, :], psU[:], AF.Identity,
                                      bias=p.bias[:, 32 + c:33 + c]))
            assert a == tok_ucp[c]
        if tokA_tap:
            s.wait_ge(p.DS.h, tokA_tap)
        for ec in range(32):
            psH = p.psA if ec % 2 == 0 else p.psB
            s.wait_ge(p.TE.h, tok_hdm[ec])
            a = p.AC.inc(s.activation(p.big[:, ec, :], psH[:], AF.Silu))
            assert a == tok_hds[ec]

    dests = [p.gT[:], p.scrG[:].rearrange("p (c t) -> p c t", c=32),
             gacc2.rearrange("p (c t) -> p c t", c=32)]
    trp_tok = [None] * NTL
    tok_wt = [None] * NTL
    for tt in range(NTL):
        @blk.sync
        def _(s, tt=tt):
            wdma(s, tok_st[tt])
            if tt == 0:
                s.wait_ge(p.DV.h, t_G[NTL - 1])
            elif tt == 1:
                wdma(s, tok_st[6])
            elif tt == 2:
                wdma(s, tok_st[7])
            else:
                s.wait_ge(p.DV.h, tok_wt[tt - 3])
            s.wait_ge(p.DT.h, p.DT.n)
            trp_tok[tt] = p.DT.inc(s.dma_start_transpose(dests[tt % 3], p.d_gd[g % 2, tt]), 16)

        @blk.vector
        def _(v, tt=tt):
            v.wait_ge(p.AC.h, tok_hds[31])
            v.wait_ge(p.DT.h, trp_tok[tt])
            sl = p.big[:, :, 128 * tt:128 * (tt + 1)]
            tok_wt[tt] = p.DV.inc(v.tensor_mul(sl, sl, dests[tt % 3]))

    tk = tap(p, f"WT{FFN_CALL[0]}_{g}", p.big[:].rearrange("p c t -> p (c t)"),
             p.DV, tok_wt[NTL - 1])
    if tk:
        p.tap_hold.append((p.DS.h, tk))

    p.psD_free = (p.DV, tok_ev[7])
    p.scr_free = (p.PO, tok_Ap[15])
    p.xn_free = (p.TE, tok_hdm[31])
    ln_fin = next_ln() if next_ln is not None else None

    t_p2b0 = p.TE.n
    assert t_p2b0 == te0 + 124 + (18 if next_ln is not None else 0)
    bump = 2 if next_ln is not None else 0
    tok_ueq = {(do, qq): t_p2b0 + 4 * do + qq + 1 + (bump if do >= 2 else 0)
               for do in range(8) for qq in range(4)}
    tok_umm = [tok_ueq[(do, 3)] for do in range(8)]

    uev = p.wr1[:].rearrange("p (b c n) -> p b c n", b=4, c=8)
    ue_r = p.d_ue.rearrange("(c p) n -> p c n", p=128)
    tok_ue = {}
    tok_uev = [0] * 8
    for do in range(8):
        psO = [p.psA, p.psB, p.psC][do % 3]
        for qq in range(4):
            j = 4 * do + qq

            @blk.gpsimd
            def _(gg, do=do, qq=qq, j=j):
                if j < 4:
                    gg.wait_ge(p.TE.h, tok_hdm[28 + j])
                else:
                    pj = j - 4
                    gg.wait_ge(p.TE.h, tok_ueq[(pj // 4, pj % 4)])
                tok_ue[(do, qq)] = dma(gg, p.DW2, uev[:, j % 4, :, :],
                                       ue_r[:, 8 * qq:8 * (qq + 1), 128 * do:128 * (do + 1)])

            @blk.tensor
            def _(t, do=do, qq=qq, j=j, psO=psO):
                wdma(t, tok_ue[(do, qq)])
                if qq == 0:
                    t.wait_ge(p.DV.h, tok_wt[NTL - 1])
                    if do >= 3:
                        t.wait_ge(p.DV.h, tok_uev[do - 3])
                last = None
                for cc in range(8):
                    c = 8 * qq + cc
                    for hh in range(2):
                        last = t.matmul(psO[:, 512 * hh:512 * (hh + 1)],
                                        uev[:, j % 4, cc, :],
                                        p.big[:, c, 512 * hh:512 * (hh + 1)],
                                        start=(c == 0), stop=(c == 31))
                a = p.TE.inc(last)
                assert a == tok_ueq[(do, qq)]

        @blk.vector
        def _(v, do=do, g=g, psO=psO):
            v.wait_ge(p.TE.h, tok_umm[do])
            tok_uev[do] = p.DV.inc(v.tensor_tensor(
                p.x[:, do, TG * g:TG * (g + 1)], psO[:],
                p.x[:, do, TG * g:TG * (g + 1)], ALU.add))

        if do == 1 and ln_fin is not None:
            p.next_tok_xn = ln_fin()
            ln_fin = None

    if ln_fin is not None:
        p.next_tok_xn = ln_fin()
    p.x_dv = tok_uev[7]
    p.psC_free = tok_uev[5]
    p.big_free = (p.TE, tok_umm[7])
    p.ring_free = (p.TE, tok_umm[7])
    p.ring2_free = (p.TE, tok_q2mm[3])
    p.ub_free = (p.TE, tok_dmm[7])
    p.p2b_last_te = tok_umm[7]
    p.tok_wt_last = tok_wt[NTL - 1]
    p.last_trp = trp_tok[NTL - 1]
    p.q2buf_free = [(p.TE.h, tok_umm[7]), (p.DV.h, tok_wt[NTL - 1])]
    p.simbuf_free = [(p.DV.h, tok_wt[NTL - 1]), (tok_st[7][0].h, tok_st[7][1])]
    p.scrG_free_l = [(p.DV.h, tok_wt[NTL - 1]), (tok_st[6][0].h, tok_st[6][1])]
    p.gT_free_l = [(p.DV.h, tok_wt[NTL - 1])]
    scope.__exit__(None, None, None)


def emit_attn(p):
    blk = p.blk
    scope = p.nc.named_scope("attn_proj")
    scope.__enter__()

    wkvv = p.q2.rearrange("p o t -> p (o t)")[:, 0:2048].rearrange("p (d n) -> p d n", d=CH)
    wkv_r = p.d_wkv.rearrange("(d p) n -> p d n", p=128)
    wq_r = p.d_wq.rearrange("(d p) n -> p d n", p=128)
    wqh = p.wr1[:, 0:2048].rearrange("p (b d n) -> p b d n", b=2, d=CH)
    t_wkv = [0]

    @blk.gpsimd
    def _(gg):
        gg.wait_ge(p.TE.h, p.p2b_last_te)
        gg.wait_ge(p.DV.h, p.x_dv)
        wl(gg, p.simbuf_free)
        wl(gg, p.tap_hold)
        p.t_pe = dma(gg, p.DW3, p.pe, p.d_pet)
        p.t_ident = dma(gg, p.DW3, p.ident, p.d_ident)
        p.t_stair = dma(gg, p.DW3, p.stair.rearrange("p a b -> p (a b)"), p.d_stair)
        t_wkv[0] = dma(gg, p.DW3, wkvv, wkv_r)

    kk = p.scrP[:, 0:2048]
    vv = p.scrP[:, 2048:4096]
    psT = p.psB[:].bitcast(BF16)
    tok_vt = [0] * 16

    def emit_vt(c):
        t_vtm = [0]

        @blk.tensor
        def _(t, c=c):
            t.wait_ge(p.DV.h, tok_kv[1])
            t.wait_ge(p.DW3.h, p.t_ident)
            if c > 0:
                t.wait_ge(p.DV.h, tok_vt[c - 1])
            t_vtm[0] = p.TE.inc(t.transpose(psT[:, 0:128], vv[:, 128 * c:128 * (c + 1)],
                                            p.ident[:]))

        @blk.vector
        def _(v, c=c):
            v.wait_ge(p.TE.h, t_vtm[0])
            if c == 0:
                v.wait_ge(p.DV.h, p.tok_wt_last)
            tok_vt[c] = p.DV.inc(v.tensor_copy(p.gT[:, c, :], psT[:, 0:128]))
    ln_scr = p.big[:, 24:27, :].rearrange("p a b -> p (a b)").bitcast(F32)
    tok_kv = [0, 0]
    t_kv_k = [0, 0]
    tok_qh = [[0] * 2 for _ in range(NHEAD)]
    t_qev = [0] * NHEAD
    t_kvev = [0]
    t_qmm = [0]
    p.t_wqh = [0] * 16
    p.t_qmm_j = [0] * 16
    for half in range(2):
        if half == 0 and p.next_tok_xn is not None:
            tok_xn = p.next_tok_xn
            p.next_tok_xn = None
        elif half == 0:
            tok_xn = emit_ln(p, TG * half, ln_scr, (p.TE, p.p2b_last_te))
        else:
            p.psD_free = (p.DV, t_kv_k[0])
            tok_xn = emit_ln(p, TG * half, ln_scr, (p.TE, p.p2b_last_te),
                             bank=p.psA)
        for j in range(2):
            psK = p.psA if j == 0 else p.psB

            @blk.tensor
            def _(t, j=j, half=half, psK=psK, tok_xn=tok_xn):
                t.wait_ge(p.DW3.h, t_wkv[0])
                t.wait_ge(p.DV.h, tok_xn)
                if half == 1:
                    t.wait_ge(p.DV.h, t_kvev[0])
                else:
                    t.wait_ge(p.DV.h, p.x_dv)
                last = None
                for dd in range(CH):
                    for hh in range(2):
                        last = t.matmul(psK[:, 512 * hh:512 * (hh + 1)],
                                        wkvv[:, dd, 128 * j:128 * (j + 1)],
                                        p.xn[:, dd, 512 * hh:512 * (hh + 1)],
                                        start=(dd == 0), stop=(dd == CH - 1))
                p.t_kvmm = p.TE.inc(last)

            @blk.vector
            def _(v, j=j, half=half, psK=psK):
                v.wait_ge(p.TE.h, p.t_kvmm)
                v.wait_ge(p.DW3.h, p.t_pe)
                dst = (kk if j == 0 else vv)[:, TG * half:TG * (half + 1)]
                if j == 0:
                    tok_kv[half] = p.DV.inc(v.scalar_tensor_tensor(
                        dst, psK[:], p.bias[:, 72:73], p.pe[:, TG * half:TG * (half + 1)],
                        ALU.add, ALU.add))
                    t_kv_k[half] = p.DV.n
                else:
                    tok_kv[half] = p.DV.inc(v.tensor_scalar_add(dst, psK[:], p.bias[:, 73:74]))
                t_kvev[0] = p.DV.n

        for h in range(NHEAD):
            psQ = p.psC if h % 2 == 0 else p.psD
            j = 8 * half + h

            @blk.gpsimd
            def _(gg, h=h, half=half, j=j):
                if j < 2:
                    gg.wait_ge(p.TE.h, p.p2b_last_te)
                else:
                    gg.wait_ge(p.TE.h, p.t_qmm_j[j - 2])
                p.t_wqh[j] = dma(gg, p.DW1, wqh[:, j % 2, :, :],
                                 wq_r[:, :, 128 * h:128 * (h + 1)])

            @blk.tensor
            def _(t, h=h, half=half, j=j, psQ=psQ, tok_xn=tok_xn):
                wdma(t, p.t_wqh[j])
                t.wait_ge(p.DV.h, tok_xn)
                prev = t_qev[h - 2] if h >= 2 else (t_qev[NHEAD - 2 + h] if half == 1 else 0)
                t.wait_ge(p.DV.h, prev)
                if half == 0 and h < 2:
                    t.wait_ge(p.DV.h, p.ln_psD_free_dv)
                last = None
                for dd in range(CH):
                    for hh in range(2):
                        last = t.matmul(psQ[:, 512 * hh:512 * (hh + 1)], wqh[:, j % 2, dd, :],
                                        p.xn[:, dd, 512 * hh:512 * (hh + 1)],
                                        start=(dd == 0), stop=(dd == CH - 1))
                t_qmm[0] = p.TE.inc(last)
                p.t_qmm_j[j] = t_qmm[0]

            @blk.vector
            def _(v, h=h, half=half, psQ=psQ):
                v.wait_ge(p.TE.h, t_qmm[0])
                v.wait_ge(p.DW3.h, p.t_pe)
                if half == 0 and h == 0:
                    v.wait_ge(p.TE.h, p.p2b_last_te)
                tok_qh[h][half] = p.DV.inc(v.scalar_tensor_tensor(
                    p.big[:, 2 * h + half, :], psQ[:], p.bias[:, 64 + h:65 + h],
                    p.pe[:, TG * half:TG * (half + 1)], ALU.add, ALU.add))
                t_qev[h] = p.DV.n

            if half == 1:
                emit_vt(2 * h)
                emit_vt(2 * h + 1)
        p.xn_free = (p.TE, t_qmm[0])

    scope.__exit__(None, None, None)
    scope = p.nc.named_scope("attn_heads")
    scope.__enter__()
    Ev = p.scrG[:].rearrange("p (s t) -> p s t", s=8)
    zr = p.rows[0:1, 0:512]
    zbs = p.gT[:, 16:20, :].rearrange("p a b -> p (a b)")
    tok_att = {-2: 0, -1: 0}
    t_zr = {-2: 0, -1: 0}
    t_zbs = {-2: 0, -1: 0}
    pv_tok = {}
    pv_last = {}
    exp_prev = [0, 0]
    t_zb = {}

    for h in range(NHEAD):
        for Q in range(4):
            nkc = 4 * Q + 4
            idx = 4 * h + Q
            npair = nkc // 2
            psOt = p.psA if idx % 2 == 0 else p.psB
            psO = psOt[:, 0:512]
            zrow = psOt[0:1, 512:1024]
            teI, acI, dvI = p.TE.n, p.AC.n, p.DV.n
            tok_exp = [acI + (kc // 2) + 1 for kc in range(nkc)]
            t_mask = dvI + 1
            t_sm_toks = []
            qblk = p.big[:, 2 * h + Q // 2, 512 * (Q % 2):512 * (Q % 2) + 512]

            def pv(t, kc, first, last_flag, idx=idx, psO=psO, zrow=zrow, nkc=nkc,
                   tok_exp=tok_exp, t_mask=t_mask):
                slot = kc % 8
                if kc >= nkc - 4:
                    t.wait_ge(p.DV.h, t_mask)
                else:
                    t.wait_ge(p.AC.h, tok_exp[kc])
                if first:
                    t.wait_ge(p.DV.h, tok_vt[15])
                    t.wait_ge(p.DV.h, t_zr[idx - 2])
                    t.wait_ge(p.DV.h, tok_att[idx - 2])
                    if idx < 2:
                        t.wait_ge(p.DV.h, t_qev[NHEAD - 1])
                t.matmul(zrow, p.onesb_c[:], Ev[:, slot, :], start=first, stop=last_flag)
                mm = t.matmul(psO, p.gT[:, kc, :], Ev[:, slot, :], start=first, stop=last_flag)
                pv_tok[slot] = p.TE.inc(mm)
                if last_flag:
                    pv_last[idx] = pv_tok[slot]

            @blk.tensor
            def _(t, h=h, Q=Q, idx=idx, nkc=nkc, npair=npair, qblk=qblk, tok_exp=tok_exp):
                t.wait_ge(p.DV.h, tok_kv[1])
                t.wait_ge(p.DV.h, tok_qh[h][Q // 2])
                pend_pv = []
                for pr in range(npair):
                    kc0 = 2 * pr
                    psS = p.psC if pr % 2 == 0 else p.psD
                    if kc0 >= 4:
                        t.wait_ge(p.AC.h, tok_exp[kc0 - 4])
                    else:
                        t.wait_ge(p.AC.h, exp_prev[pr % 2])
                        if idx == 0:
                            t.wait_ge(p.DV.h, t_qev[NHEAD - 1])
                    if pr % 2 == 1:
                        t.wait_ge(p.DV.h, t_zbs[idx - 2])
                    mm = None
                    for j in range(2):
                        kc = kc0 + j
                        mm = t.matmul(psS[:, 512 * j:512 * (j + 1)],
                                      kk[:, 128 * kc:128 * (kc + 1)], qblk,
                                      start=True, stop=True)
                    t_sm_toks.append(p.TE.inc(mm))
                    if kc0 + 1 < nkc - 4:
                        pend_pv.append(kc0)
                        if len(pend_pv) > 1:
                            k0 = pend_pv.pop(0)
                            pv(t, k0, first=(k0 == 0), last_flag=False)
                            pv(t, k0 + 1, first=False, last_flag=False)
                for k0 in pend_pv:
                    pv(t, k0, first=(k0 == 0), last_flag=False)
                    pv(t, k0 + 1, first=False, last_flag=False)
                for kc in range(nkc - 4, nkc):
                    pv(t, kc, first=(kc == 0), last_flag=(kc == nkc - 1))
                if idx >= 1:
                    t.wait_ge(p.DV.h, t_zr[idx - 1])
                    t.wait_ge(p.DV.h, t_zbs[idx - 2])
                    t.wait_ge(p.AC.h, tok_exp[2 * (npair - 1)])
                    t_zb[idx - 1] = p.TE.inc(t.matmul(p.psD[:, 0:512], p.ones_rf[:], zr,
                                                      start=True, stop=True))

            @blk.scalar
            def _(s, h=h, Q=Q, idx=idx, nkc=nkc, npair=npair, tok_exp=tok_exp,
                  t_sm_toks=t_sm_toks):
                for pr in range(npair):
                    kc0 = 2 * pr
                    slot = kc0 % 8
                    psS = p.psC if pr % 2 == 0 else p.psD
                    s.wait_ge(p.TE.h, t_sm_toks[pr])
                    for sl in (slot, slot + 1):
                        if sl in pv_tok and pv_tok[sl] < teI:
                            s.wait_ge(p.TE.h, pv_tok[sl])
                    if idx == 0 and kc0 == 0:
                        s.wait_ge(p.DT.h, p.last_trp)
                        s.wait_ge(p.DV.h, p.tok_wt_last)
                    a = p.AC.inc(s.activation(Ev[:, slot:slot + 2, :], psS[:, 0:1024],
                                              AF.Exp, scale=SCALE))
                    assert a == tok_exp[kc0]

            s0 = (nkc - 4) % 8

            @blk.vector
            def _(v, idx=idx, s0=s0, nkc=nkc, tok_exp=tok_exp, psO=psO, zrow=zrow):
                v.wait_ge(p.AC.h, tok_exp[nkc - 1])
                v.wait_ge(p.DW3.h, p.t_stair)
                a = p.DV.inc(v.tensor_mul(Ev[:, s0:s0 + 4, :], Ev[:, s0:s0 + 4, :],
                                          p.stair[:]))
                assert a == t_mask
                if idx >= 1:
                    v.wait_ge(p.TE.h, t_zb[idx - 1])
                    t1 = p.DV.inc(v.tensor_copy(zbs[:, 0:512], p.psD[:, 0:512]))
                    v.wait_ge(p.DV.h, t1)
                    t_zbs[idx - 1] = t1
                    pidx = idx - 1
                    ph, pQ = pidx // 4, pidx % 4
                    ppsO = (p.psA if pidx % 2 == 0 else p.psB)[:, 0:512]
                    tok_att[pidx] = p.DV.inc(v.tensor_mul(
                        p.big[:, 16 + 2 * ph + pQ // 2, 512 * (pQ % 2):512 * (pQ % 2) + 512],
                        ppsO, zbs[:, 0:512]))
                v.wait_ge(p.TE.h, pv_last[idx])
                t_zr[idx] = p.DV.inc(v.reciprocal(zr, zrow))

            exp_prev[0] = tok_exp[2 * (npair - 2)]
            exp_prev[1] = tok_exp[2 * (npair - 1)]

    lidx = 4 * NHEAD - 1

    @blk.tensor
    def _(t):
        t.wait_ge(p.DV.h, t_zr[lidx])
        t.wait_ge(p.DV.h, t_zbs[lidx - 1])
        t.wait_ge(p.AC.h, exp_prev[1])
        t_zb[lidx] = p.TE.inc(t.matmul(p.psD[:, 0:512], p.ones_rf[:], zr, start=True, stop=True))

    @blk.vector
    def _(v):
        v.wait_ge(p.TE.h, t_zb[lidx])
        t1 = p.DV.inc(v.tensor_copy(zbs[:, 0:512], p.psD[:, 0:512]))
        v.wait_ge(p.DV.h, t1)
        ph, pQ = lidx // 4, lidx % 4
        ppsO = (p.psA if lidx % 2 == 0 else p.psB)[:, 0:512]
        tok_att[lidx] = p.DV.inc(v.tensor_mul(
            p.big[:, 16 + 2 * ph + pQ // 2, 512 * (pQ % 2):512 * (pQ % 2) + 512],
            ppsO, zbs[:, 0:512]))
    scope.__exit__(None, None, None)
    scope = p.nc.named_scope("attn_wo")
    scope.__enter__()
    wov = p.wr1[:, 0:2048].rearrange("p (b d n) -> p b d n", b=2, d=CH)
    wo_r = p.d_wo.rearrange("(d p) n -> p d n", p=128)
    tok_wo = [0] * 8
    tok_omm = [0] * 8
    tok_oev = [0] * 16
    for oc in range(8):
        @blk.gpsimd
        def _(gg, oc=oc):
            if oc < 2:
                gg.wait_ge(p.TE.h, p.t_qmm_j[15])
            else:
                gg.wait_ge(p.TE.h, tok_omm[oc - 2])
            tok_wo[oc] = dma(gg, p.DW1, wov[:, oc % 2, :, :],
                             wo_r[:, :, 128 * oc:128 * (oc + 1)])
        for blkk in range(2):
            psX = p.psC if (2 * oc + blkk) % 2 == 0 else p.psD
            t_omm = [0]

            @blk.tensor
            def _(t, oc=oc, blkk=blkk, psX=psX):
                wdma(t, tok_wo[oc])
                t.wait_ge(p.DV.h, tok_att[4 * NHEAD - 1])
                j = 2 * oc + blkk
                if j >= 2:
                    t.wait_ge(p.DV.h, tok_oev[j - 2])
                last = None
                for hh in range(CH):
                    for q2_ in range(2):
                        last = t.matmul(psX[:, 512 * q2_:512 * (q2_ + 1)], wov[:, oc % 2, hh, :],
                                        p.big[:, 16 + 2 * hh + blkk, 512 * q2_:512 * (q2_ + 1)],
                                        start=(hh == 0), stop=(hh == CH - 1))
                t_omm[0] = p.TE.inc(last)
                tok_omm[oc] = t_omm[0]

            @blk.vector
            def _(v, oc=oc, blkk=blkk, psX=psX):
                v.wait_ge(p.TE.h, t_omm[0])
                if 2 * oc + blkk == 0:
                    v.wait_ge(p.DV.h, tok_att[4 * NHEAD - 1])
                tok_oev[2 * oc + blkk] = p.DV.inc(v.tensor_tensor(
                    p.x[:, oc, TG * blkk:TG * (blkk + 1)], psX[:],
                    p.x[:, oc, TG * blkk:TG * (blkk + 1)], ALU.add))
    scope.__exit__(None, None, None)
    p.x_dv = tok_oev[15]
    p.psC_free = tok_oev[14]
    p.psD_free = (p.DV, tok_oev[15])
    p.big_free = (p.TE, tok_omm[7])
    p.ring_free = (p.TE, tok_omm[7])
    p.ub_free = (p.TE, tok_omm[7])
    p.t_c4 = t_qev[NHEAD - 1]
    p.scr_free = (p.TE, tok_omm[7])
    p.q2buf_free = [(p.TE.h, tok_omm[7]), (p.DV.h, tok_oev[15])]
    p.scrG_free_l = [(p.TE.h, tok_omm[7])]
    p.gT_free_l = [(p.TE.h, tok_omm[7]), (p.DV.h, tok_oev[15])]
    p.simbuf_free = [(p.DV.h, tok_oev[15])]
    p.ring2_free = (p.DV, tok_oev[15])


def emit_store(p):
    blk = p.blk

    @blk.sync
    def _(s):
        s.wait_ge(p.DV.h, p.x_dv)
        for c in range(CH):
            dma(s, p.DS, p.d_out[128 * c:128 * (c + 1), :], p.x[:, c, :])
        s.wait_ge(p.DS.h, p.DS.n)
        s.wait_ge(p.DT.h, p.DT.n)


# version 27
# speedup vs baseline: 1.0494x; 1.0059x over previous
import math
import contextlib
import numpy as np
import concourse.bass as bass
import concourse.mybir as mybir
from concourse.bass_utils import run_bass_kernel_spmd
import concourse.dve_ops as _dve_ops
from concourse.dve_op import DveOp as _DveOp
from concourse.dve_spec import Spec as _Spec, Src0 as _Src0, Src1 as _Src1, C0 as _C0, \
    Zero as _Zero, select as _select
from concourse.dve_table_gen import dve_ver_for as _dve_ver_for

F32 = mybir.dt.float32
BF16 = mybir.dt.bfloat16
AF = mybir.ActivationFunctionType
ALU = mybir.AluOpType
AX = mybir.AxisListType

NCORE = 8
D = 1024
T = 2048
CH = 8
DC = 4096
NE = 4096
NH = 4
TG = 1024
NTL = 8
EPS = 1e-5
NHEAD = 8
SCALE = 1.0 / math.sqrt(128.0)

TAPS = {}


def _sel_add_ref(in0, in1, s0, s1, imm2):
    return np.where(in0 >= s0, in0, 0.0) + in1


def _register_dve_op(name, spec, ref):
    for o in _dve_ops.OPS:
        if o.name == name:
            return o
    ver = _dve_ver_for("TRN2")
    op = _DveOp(name, spec, subdim=False, uops_sha={}, reference=ref)
    try:
        op.compile(ver)
        sha = None
    except ValueError as e:
        import re as _re
        m = _re.search(r"\(%s: (\w+)" % ver, str(e))
        sha = m.group(1)
    if sha is not None:
        op = _DveOp(name, spec, subdim=False, uops_sha={ver: sha}, reference=ref)
        op.compile(ver)
    _dve_ops.OPS.append(op)
    _dve_ops._SUB_OPCODE_FOR_NAME[op.name] = (_dve_ops._CUSTOM_DVE_ROW_BASE
                                              + len(_dve_ops.OPS) - 1)
    _dve_ops.CUSTOM_DVE_SPECS[op.name] = spec
    _dve_ops.CUSTOM_DVE_REFS[op.name] = ref
    return op


XBAR_TRANSPOSE = True


def _host_consts():
    pos = np.arange(T, dtype=np.float64)[:, None]
    inv = np.exp(-math.log(10000.0) * np.arange(0, 128, 2, dtype=np.float64) / 128.0)
    ang = pos * inv[None, :]
    pe = np.zeros((T, 128), dtype=np.float64)
    pe[:, 0::2] = np.sin(ang)
    pe[:, 1::2] = np.cos(ang)
    pet = np.ascontiguousarray(pe.T.astype(np.float32))
    pp = np.arange(128)[:, None, None]
    cc = np.arange(4)[None, :, None]
    jj = np.arange(512)[None, None, :]
    stair = ((128 * cc + pp) <= jj).astype(np.float32).reshape(128, 4 * 512)
    ident = np.eye(128, dtype=np.float32)
    return pet, stair, ident


def kernel(**inp):
    pet, stair, ident = _host_consts()
    x = np.ascontiguousarray(inp["L0_attn_x"])
    com = {
        "wq": np.ascontiguousarray(inp["L0_attn_Wq"]),
        "wkv": np.ascontiguousarray(inp["L0_attn_Wkv"]),
        "wo": np.ascontiguousarray(inp["L0_attn_Wo"]),
        "wg": np.ascontiguousarray(inp["L0_ffn_Wg"]),
        "wu": np.ascontiguousarray(inp["L0_ffn_Wu"]),
        "wd": np.ascontiguousarray(inp["L0_ffn_Wd"]),
        "wqf": np.ascontiguousarray(inp["L0_ffn_Wq"]),
        "dte": np.ascontiguousarray(inp["L0_ffn_down_embed"].T),
        "ue": np.ascontiguousarray(inp["L0_ffn_up_embed"]),
        "keyst": np.ascontiguousarray(inp["L0_ffn_keys"].transpose(2, 0, 3, 1)),
        "bqc": np.ascontiguousarray(inp["L0_attn_bq"].reshape(8, 128).T),
        "bkvc": np.ascontiguousarray(inp["L0_attn_bkv"].reshape(2, 128).T),
        "bgc": np.ascontiguousarray(inp["L0_ffn_bg"].reshape(32, 128).T),
        "buc": np.ascontiguousarray(inp["L0_ffn_bu"].reshape(32, 128).T),
        "bdr": np.ascontiguousarray(inp["L0_ffn_bd"].reshape(1, 1024)),
        "pet": pet, "stair": stair, "ident": ident,
    }
    nc = build_nc()
    in_maps = []
    for b in range(NCORE):
        m = dict(com)
        m["xt"] = np.ascontiguousarray(x[b].T)
        in_maps.append(m)
    res = run_bass_kernel_spmd(nc, in_maps, core_ids=list(range(NCORE)))
    out = np.stack([np.ascontiguousarray(res.results[b]["out"].T) for b in range(NCORE)], axis=0)
    return out.astype(np.float32)


class Cnt:
    def __init__(self, ctx, nc, name):
        self.h = ctx.enter_context(nc.semaphore(name))
        self.n = 0

    def inc(self, instr, k=1):
        self.n += k
        instr.then_inc(self.h, k)
        return self.n


class LanePair:

    def __init__(self, ctx, nc, name):
        self.lanes = [Cnt(ctx, nc, name + "a"), Cnt(ctx, nc, name + "b")]
        self.i = 0

    def issue(self, eng, out, in_, **kw):
        lane = self.lanes[self.i % 2]
        self.i += 1
        eng.wait_ge(lane.h, lane.n)
        tok = lane.inc(eng.dma_start(out=out, in_=in_, **kw), 16)
        return (lane, tok)


class P:
    pass


def build_nc():
    nc = bass.Bass("TRN2", target_bir_lowering=False)
    p = P()
    p.nc = nc
    dt = nc.dram_tensor
    p.d_xt = dt("xt", [D, T], F32, kind="ExternalInput").ap()
    p.d_wq = dt("wq", [D, D], F32, kind="ExternalInput").ap()
    p.d_wkv = dt("wkv", [D, 256], F32, kind="ExternalInput").ap()
    p.d_wo = dt("wo", [D, D], F32, kind="ExternalInput").ap()
    p.d_wg = dt("wg", [D, DC], F32, kind="ExternalInput").ap()
    p.d_wu = dt("wu", [D, DC], F32, kind="ExternalInput").ap()
    p.d_wd = dt("wd", [DC, D], F32, kind="ExternalInput").ap()
    p.d_wqf = dt("wqf", [D, 512], F32, kind="ExternalInput").ap()
    p.d_dt = dt("dte", [D, NE], F32, kind="ExternalInput").ap()
    p.d_ue = dt("ue", [NE, D], F32, kind="ExternalInput").ap()
    p.d_keyst = dt("keyst", [2, NH, 64, 64], F32, kind="ExternalInput").ap()
    p.d_bqc = dt("bqc", [128, 8], F32, kind="ExternalInput").ap()
    p.d_bkvc = dt("bkvc", [128, 2], F32, kind="ExternalInput").ap()
    p.d_bgc = dt("bgc", [128, 32], F32, kind="ExternalInput").ap()
    p.d_buc = dt("buc", [128, 32], F32, kind="ExternalInput").ap()
    p.d_bdr = dt("bdr", [1, D], F32, kind="ExternalInput").ap()
    p.d_pet = dt("pet", [128, T], F32, kind="ExternalInput").ap()
    p.d_stair = dt("stair", [128, 4 * 512], F32, kind="ExternalInput").ap()
    p.d_ident = dt("ident", [128, 128], F32, kind="ExternalInput").ap()
    p.d_gd = dt("gspill", [2, NTL, 128, NE], BF16, kind="Internal").ap()
    p.d_out = dt("out", [D, T], F32, kind="ExternalOutput").ap()
    p.taps = {name: dt("tap_" + name, list(shape), dtp, kind="ExternalOutput").ap()
              for name, (shape, dtp) in TAPS.items()}

    ctx = contextlib.ExitStack()

    def al(name, shape, dtp):
        return ctx.enter_context(nc.sbuf_tensor(name, shape, dtp))

    p.x = al("x", [128, CH, T], F32)
    p.big = al("big", [128, 32, TG], BF16)
    p.xn = al("xn", [128, CH, TG], BF16)
    p.scrG = al("scrG", [128, NE], BF16)
    p.scrP = al("scrP", [128, NE], BF16)
    p.gT = al("gT", [128, 32, 128], BF16)
    p.wr1 = al("wr1", [128, 4096], BF16)
    p.wr2 = al("wr2", [128, 2048], BF16)
    p.cmpb = al("cmpb", [128, 896], BF16)
    p.simbuf = al("simbuf", [128, 4096], F32)
    p.s1v8 = al("s1v8", [128, 8, 8, 8], F32)
    p.e8all = al("e8all", [128, 8, 64], BF16)
    p.bias = al("bias", [128, 80], F32)
    p.q8 = al("q8", [128, NH, 8], F32)
    p.q8S = al("q8S", [128, NH, 8], BF16)
    p.q9S = al("q9S", [128, NH, 8], BF16)
    p.zal = al("zal", [128, 3 * NH], F32)
    p.ub = al("ub", [128, 128], BF16)
    p.ident = p.ub[:, :]
    p.bdrt = al("bdr_sb", [1, 1024], BF16)
    p.bdr = p.bdrt[0:1, :]
    p.ones_r = al("ones_r", [1, 512], BF16)
    p.ones_rf = al("ones_rf", [1, 128], F32)
    p.ones_c = al("ones_c", [128, 1], F32)
    p.onesb_c = al("onesb_c", [128, 1], BF16)
    p.eps_c = al("eps_c", [128, 1], F32)
    p.keysbd = al("keysbd", [128, 4, 128], BF16)
    p.sil = p.scrP[:].bitcast(F32)
    p.silb = p.scrP[:].rearrange("p (s n) -> p s n", s=4)
    p.pe = p.simbuf[:].bitcast(BF16)[:, 0:2048]
    p.rows = p.simbuf[0:1, 1024:2048]
    p.q2 = p.big[:, 28:32, :]
    p.stair = p.wr2[:].rearrange("p (a b) -> p a b", a=4)

    p.psA = ctx.enter_context(nc.psum_tensor("psA", [128, 1024], F32))
    p.psB = ctx.enter_context(nc.psum_tensor("psB", [128, 1024], F32))
    p.psC = ctx.enter_context(nc.psum_tensor("psC", [128, 1024], F32))
    p.psD = ctx.enter_context(nc.psum_tensor("psD", [128, 1024], F32))

    p.TE = Cnt(ctx, nc, "qTE")
    p.DV = Cnt(ctx, nc, "qDV")
    p.AC = Cnt(ctx, nc, "qAC")
    p.PO = Cnt(ctx, nc, "qPO")
    p.DW1 = LanePair(ctx, nc, "qDW1")
    p.DW2 = LanePair(ctx, nc, "qDW2")
    p.DW3 = Cnt(ctx, nc, "qDW3")
    p.DS = Cnt(ctx, nc, "qDS")
    p.DT = Cnt(ctx, nc, "qDT")
    p.DG = LanePair(ctx, nc, "qDG")
    p.x_dv = 0
    p.xn_free = (p.DV, 0)
    p.psD_free = (p.DV, 0)
    p.ln_psD_free_dv = 0
    p.last_trp = 0
    p.tok_wt_last = 0
    p.scr_free = (p.DV, 0)
    p.t_c4 = 0
    p.big_free = (p.TE, 0)
    p.ring_free = (p.TE, 0)
    p.ring2_free = (p.TE, 0)
    p.ub_free = (p.TE, 0)
    p.psC_free = 0
    p.q2buf_free = []
    p.simbuf_free = []
    p.scrG_free_l = []
    p.gT_free_l = []
    p.tap_hold = []
    p.next_tok_xn = None

    with nc.Block() as blk:
        p.blk = blk
        emit_init(p)
        emit_ffn(p)
        emit_attn(p)
        emit_ffn(p)
        emit_store(p)
    ctx.close()
    return nc


def dma(eng, lane, out, in_, **kw):
    if isinstance(lane, LanePair):
        return lane.issue(eng, out, in_, **kw)
    eng.wait_ge(lane.h, lane.n)
    return lane.inc(eng.dma_start(out=out, in_=in_, **kw), 16)


def wdma(eng, token):
    lane, tok = token
    eng.wait_ge(lane.h, tok)


def wl(eng, lst):
    for h, v in lst:
        eng.wait_ge(h, v)


def tap(p, name, ap_src, cnt, tok):
    if name not in p.taps:
        return None
    blk = p.blk
    out = [None]

    @blk.sync
    def _(s):
        s.wait_ge(cnt.h, tok)
        out[0] = dma(s, p.DS, p.taps[name], ap_src)
    return out[0]


def emit_init(p):
    blk = p.blk

    @blk.sync
    def _(s):
        for c in range(CH):
            dma(s, p.DS, p.x[:, c, :], p.d_xt[128 * c:128 * (c + 1), :])
    p.tok_xin = p.DS.n

    @blk.gpsimd
    def _(g):
        dma(g, p.DW3, p.bias[:, 0:32], p.d_bgc)
        dma(g, p.DW3, p.bias[:, 32:64], p.d_buc)
        dma(g, p.DW3, p.bias[:, 64:72], p.d_bqc)
        dma(g, p.DW3, p.bias[:, 72:74], p.d_bkvc)
        tz = p.PO.inc(g.memset(p.keysbd[:], 0.0))
        g.wait_ge(p.PO.h, tz)
        for c in range(4):
            pp, hlo = c // 2, (c % 2) * 2
            dma(g, p.DW3, p.keysbd[0:64, c, 0:64], p.d_keyst[pp, hlo, :, :])
            dma(g, p.DW3, p.keysbd[64:128, c, 64:128], p.d_keyst[pp, hlo + 1, :, :])
    p.tok_const = p.DW3.n

    @blk.vector
    def _(v):
        p.DV.inc(v.memset(p.ones_r[:], 1.0))
        p.DV.inc(v.memset(p.ones_rf[:], 1.0))
        p.DV.inc(v.memset(p.ones_c[:], 1.0))
        p.DV.inc(v.memset(p.onesb_c[:], 1.0))
        p.DV.inc(v.memset(p.eps_c[:], float(EPS)))
    p.tok_ones = p.DV.n


def emit_ln(p, t0, scratch, scr_free, defer=False, bank=None):
    blk = p.blk
    psb = bank if bank is not None else p.psD
    xnf, psDf = p.xn_free, p.psD_free
    x_dv = p.x_dv
    hold0 = list(p.tap_hold)
    fins = []
    prev_stats = [0]
    tok_rm_l = [0, 0]
    norm_prev = [0]
    for b in range(TG // 512):
        tb = t0 + 512 * b
        base = 1024 + 1024 * b
        mb = p.simbuf[0:1, base:base + 512]
        varb = p.simbuf[0:1, base + 512:base + 1024]
        r = varb
        mr = mb
        te0, ac0, dv0 = p.TE.n, p.AC.n, p.DV.n
        tok_sq = [ac0 + c + 1 for c in range(CH)]
        tok_sd = ac0 + CH + 1
        tok_sumx = te0 + 1
        tok_smm = [te0 + 1 + c + 1 for c in range(CH)]
        tok_stats = dv0 + 3
        tok_rm = dv0 + 5
        tok_rm_l[b] = tok_rm

        @blk.scalar
        def _(s, tb=tb, b=b, r=r, varb=varb, tok_sq=tok_sq, tok_smm=tok_smm,
              tok_stats=tok_stats, tok_sd=tok_sd):
            s.wait_ge(p.DS.h, p.tok_xin)
            s.wait_ge(p.DV.h, x_dv)
            if b == 0:
                s.wait_ge(scr_free[0].h, scr_free[1])
                wl(s, hold0)
            for c in range(CH):
                if c >= 2:
                    s.wait_ge(p.TE.h, tok_smm[c - 2])
                a = p.AC.inc(s.activation(scratch[:, 512 * (c % 2):512 * (c % 2) + 512],
                                          p.x[:, c, tb:tb + 512], AF.Square))
                assert a == tok_sq[c]
            s.wait_ge(p.DV.h, tok_stats)
            assert p.AC.inc(s.activation(r, varb, AF.Sqrt, bias=p.eps_c[0:1, :])) == tok_sd

        @blk.tensor
        def _(t, tb=tb, b=b, tok_sq=tok_sq, tok_smm=tok_smm, tok_sumx=tok_sumx,
              pstats=prev_stats[0]):
            t.wait_ge(p.DS.h, p.tok_xin)
            t.wait_ge(p.DV.h, max(x_dv, p.tok_ones, p.ln_psD_free_dv, pstats))
            if b == 0:
                t.wait_ge(psDf[0].h, psDf[1])
            last = None
            for c in range(CH):
                last = t.matmul(psb[0:1, 0:512], p.ones_c[:], p.x[:, c, tb:tb + 512],
                                start=(c == 0), stop=(c == CH - 1))
            assert p.TE.inc(last) == tok_sumx
            for c in range(CH):
                t.wait_ge(p.AC.h, tok_sq[c])
                mm = t.matmul(psb[32:33, 0:512], p.ones_c[:],
                              scratch[:, 512 * (c % 2):512 * (c % 2) + 512],
                              start=(c == 0), stop=(c == CH - 1))
                assert p.TE.inc(mm) == tok_smm[c]

        @blk.vector
        def _(v, b=b, mb=mb, varb=varb, r=r, mr=mr, tok_smm=tok_smm,
              tok_stats=tok_stats, tok_sd=tok_sd, tok_rm=tok_rm):
            v.wait_ge(p.TE.h, tok_smm[CH - 1])
            t1 = p.DV.inc(v.tensor_scalar_mul(mb, psb[0:1, 0:512], 1.0 / D))
            v.wait_ge(p.DV.h, t1)
            t2 = p.DV.inc(v.tensor_mul(varb, mb, mb))
            v.wait_ge(p.DV.h, t2)
            t3 = p.DV.inc(v.scalar_tensor_tensor(varb, psb[32:33, 0:512], 1.0 / D, varb,
                                                 ALU.mult, ALU.subtract))
            assert t3 == tok_stats
            v.wait_ge(p.AC.h, tok_sd)
            t4 = p.DV.inc(v.reciprocal(r, r))
            v.wait_ge(p.DV.h, t4)
            t5 = p.DV.inc(v.tensor_mul(mr, mb, r))
            assert t5 == tok_rm
        prev_stats[0] = tok_stats

        def fin(b=b, tb=tb, r=r, mr=mr, tok_rm=tok_rm):
            t_bc = [0]

            @blk.tensor
            def _(t):
                t.wait_ge(p.DV.h, max(tok_rm, tok_rm_l[1], norm_prev[0]))
                t.matmul(psb[:, 0:512], p.ones_rf[:], r, start=True, stop=True)
                t_bc[0] = p.TE.inc(t.matmul(psb[:, 512:1024], p.ones_rf[:], mr,
                                            start=True, stop=True))

            @blk.vector
            def _(v):
                v.wait_ge(p.TE.h, t_bc[0])
                v.wait_ge(xnf[0].h, xnf[1])
                if b == 0:
                    wl(v, hold0)
                for c in range(CH):
                    ta = p.DV.inc(v.tensor_mul(scratch[:, 1024:1536], p.x[:, c, tb:tb + 512],
                                               psb[:, 0:512]))
                    v.wait_ge(p.DV.h, ta)
                    t6 = p.DV.inc(v.tensor_sub(p.xn[:, c, 512 * b:512 * (b + 1)],
                                               scratch[:, 1024:1536], psb[:, 512:1024]))
                    v.wait_ge(p.DV.h, t6)
            norm_prev[0] = p.DV.n
            return p.DV.n
        fins.append(fin)

    def finish():
        tok = 0
        for f in fins:
            tok = f()
        p.ln_psD_free_dv = tok
        p.psD_free = (p.DV, tok)
        return tok
    if defer:
        return finish
    return finish()


FFN_CALL = [0]


def emit_ffn(p):
    call = FFN_CALL[0]
    FFN_CALL[0] += 1
    for g in range(2):
        if p.next_tok_xn is not None:
            tok_xn = p.next_tok_xn
            p.next_tok_xn = None
        else:
            with p.nc.named_scope(f"f{call}g{g}_ln"):
                tok_xn = emit_ln(p, TG * g, p.sil, p.scr_free)
        tk = tap(p, f"xn{call}_{g}", p.xn[:].rearrange("p c t -> p (c t)"), p.DV, tok_xn)
        if tk:
            p.tap_hold.append((p.DS.h, tk))
        if g == 0:
            def nl():
                with p.nc.named_scope(f"f{call}g1_ln"):
                    return emit_ln(p, TG, p.sil, p.scr_free, defer=True)
        elif call == 0:
            def nl():
                with p.nc.named_scope("fa0_ln"):
                    return emit_ln(p, 0, p.sil, p.scr_free, defer=True)
        else:
            nl = None
        emit_group(p, g, tok_xn, nl)


def emit_group(p, g, tok_xn, next_ln=None):
    blk = p.blk
    scope = p.nc.named_scope(f"c{FFN_CALL[0]}g{g}_grp")
    scope.__enter__()
    hold0 = list(p.tap_hold)

    te0, ac0, dv0 = p.TE.n, p.AC.n, p.DV.n
    tok_q2mm = [te0 + oc + 1 for oc in range(4)]
    tok_sim = [te0 + 4 + tt + 1 for tt in range(NTL)]
    tok_Gmm = [te0 + 12 + 2 * c + 1 for c in range(32)]
    tok_Umm = [te0 + 12 + 2 * c + 2 for c in range(32)]
    tok_dmid = [te0 + 76 + 2 * do + 1 for do in range(8)]
    tok_dmm = [te0 + 76 + 2 * do + 2 for do in range(8)]
    tok_hdm = [te0 + 92 + ec + 1 for ec in range(32)]
    tok_q2cp = [ac0 + oc + 1 for oc in range(4)]
    tok_scp = [ac0 + 4 + tt + 1 for tt in range(NTL)]
    t_exn = ac0 + 14
    t_e8 = ac0 + 15
    tok_sil = [ac0 + 15 + 2 * c + 1 for c in range(32)]
    tok_ucp = [ac0 + 15 + 2 * c + 2 for c in range(32)]
    tok_hds = [ac0 + 79 + ec + 1 for ec in range(32)]
    po0 = p.PO.n
    tok_Ap = [po0 + j + 1 for j in range(16)]
    t_max_last = dv0 + 64
    t_mask = dv0 + 65
    t_fneg = dv0 + 66
    assert p.DG.i % 2 == 0
    dg0 = [p.DG.lanes[0].n, p.DG.lanes[1].n]
    tok_st = [(p.DG.lanes[tt % 2], dg0[tt % 2] + 16 * (tt // 2 + 1)) for tt in range(NTL)]

    wgv4 = p.wr1[:].rearrange("p (b d n) -> p b d n", b=4, d=CH)
    wqv = p.wr2[:].rearrange("p (d n) -> p d n", d=CH)
    wdv = p.wr1[:].rearrange("p (b c n) -> p b c n", b=2, c=16)
    dtv = p.wr1[:].rearrange("p (b d n) -> p b d n", b=4, d=CH)
    wg_r = p.d_wg.rearrange("(d p) n -> p d n", p=128)
    wu_r = p.d_wu.rearrange("(d p) n -> p d n", p=128)
    wqf_r = p.d_wqf.rearrange("(d p) n -> p d n", p=128)
    wd_r = p.d_wd.rearrange("(c p) n -> p c n", p=128)
    dt_r = p.d_dt.rearrange("(d p) n -> p d n", p=128)
    sbb = p.simbuf[:].bitcast(BF16)
    fnall = sbb[:, 0:4096]
    gacc2 = sbb[:, 4096:8192]
    tmp = p.gT[:].rearrange("p c t -> p (c t)")
    Z, alpha, thresh = p.zal[:, 0:4], p.zal[:, 4:8], p.zal[:, 8:12]
    qc = p.cmpb[:, 0:256].rearrange("p (h k) -> p h k", h=NH)
    qS = p.cmpb[:, 256:512].rearrange("p (h k) -> p h k", h=NH)
    qtmp = p.cmpb[:, 512:576]
    fxs = p.cmpb[:, 576:832].rearrange("p (h n) -> p h n", h=NH)
    e8xs = p.cmpb[:, 832:864].rearrange("p (h k) -> p h k", h=NH)

    t_wg = [None] * 32
    t_wu = [None] * 32
    t_wd = [None] * 16
    t_dts = [None] * 32
    t_wqf = [None, None]

    @blk.gpsimd
    def _(gg):
        gg.wait_ge(p.ring_free[0].h, p.ring_free[1])
        for c in range(2):
            t_wg[c] = dma(gg, p.DW1, wgv4[:, c % 2, :, :], wg_r[:, :, 128 * c:128 * (c + 1)])
            t_wu[c] = dma(gg, p.DW1, wgv4[:, 2 + c % 2, :, :],
                          wu_r[:, :, 128 * c:128 * (c + 1)])
        gg.wait_ge(p.ub_free[0].h, p.ub_free[1])
        p.t_bdr = dma(gg, p.DW3, p.bdr, p.d_bdr)
        gg.wait_ge(p.ring2_free[0].h, p.ring2_free[1])
        t_wqf[0] = dma(gg, p.DW2, wqv[:, :, :], wqf_r[:, :, 0:256])
        gg.wait_ge(p.TE.h, tok_q2mm[1])
        t_wqf[1] = dma(gg, p.DW2, wqv[:, :, :], wqf_r[:, :, 256:512])

        def a_pair(j):
            gg.wait_ge(p.AC.h, tok_ucp[2 * j + 1])
            a = p.PO.inc(gg.tensor_tensor(
                p.big[:, 2 * j:2 * j + 2, :], p.big[:, 2 * j:2 * j + 2, :],
                p.silb[:, (2 * j) % 4:(2 * j) % 4 + 2, :], ALU.mult))
            assert a == tok_Ap[j]

        for c in range(2, 32):
            gg.wait_ge(p.TE.h, tok_Umm[c - 2])
            t_wg[c] = dma(gg, p.DW1, wgv4[:, c % 2, :, :], wg_r[:, :, 128 * c:128 * (c + 1)])
            t_wu[c] = dma(gg, p.DW1, wgv4[:, 2 + c % 2, :, :],
                          wu_r[:, :, 128 * c:128 * (c + 1)])
            if c % 2 == 1:
                a_pair((c - 3) // 2)
        gg.wait_ge(p.TE.h, tok_Umm[31])
        t_wd[0] = dma(gg, p.DW1, wdv[:, 0, :, :], wd_r[:, 0:16, 0:128])
        t_wd[1] = dma(gg, p.DW1, wdv[:, 1, :, :], wd_r[:, 16:32, 0:128])
        a_pair(15)
        for sl in range(2, 16):
            do, hf = sl // 2, sl % 2
            gg.wait_ge(p.TE.h, tok_dmid[do - 1] if hf == 0 else tok_dmm[do - 1])
            t_wd[sl] = dma(gg, p.DW1, wdv[:, sl % 2, :, :],
                           wd_r[:, 16 * hf:16 * (hf + 1), 128 * do:128 * (do + 1)])
        for ec in range(32):
            gg.wait_ge(p.TE.h, tok_dmm[7] if ec < 4 else tok_hdm[ec - 4])
            t_dts[ec] = dma(gg, p.DW1, dtv[:, ec % 4, :, :],
                            dt_r[:, :, 128 * ec:128 * (ec + 1)])

    tok_ev = [None] * 8
    t_G = [None] * NTL
    sg = [p.scrG[:], gacc2]

    def emit_p1b_ev(v, do):
        v.wait_ge(p.TE.h, tok_dmm[do])
        psO = [p.psA, p.psB, p.psC, p.psD][do % 4]
        tok_ev[do] = p.DV.inc(v.scalar_tensor_tensor(
            p.x[:, do, TG * g:TG * (g + 1)], psO[:], 0.5,
            p.x[:, do, TG * g:TG * (g + 1)], ALU.mult, ALU.add))

    def tile_units(tt):
        acc = sg[tt % 2]
        fn_ = fnall[:, 512 * tt:512 * (tt + 1)]
        fnx = fn_[:, 0:256].rearrange("p (h n) -> p h n", h=NH)
        fny = fn_[:, 256:512].rearrange("p (h n) -> p h n", h=NH)
        e8x = p.e8all[:, tt, 0:32].rearrange("p (h k) -> p h k", h=NH)
        e8y = p.e8all[:, tt, 32:64].rearrange("p (h k) -> p h k", h=NH)

        def c1(v):
            v.wait_ge(p.AC.h, t_e8)
            t = p.DV.inc(v.tensor_tensor(qc, e8x.unsqueeze(3).to_broadcast([128, NH, 8, 8]),
                                         e8y.unsqueeze(2).to_broadcast([128, NH, 8, 8]),
                                         ALU.mult))
            v.wait_ge(p.DV.h, t)
            for h in range(NH):
                p.DV.inc(v.max(out=p.q8[:, h, :], in_=qc[:, h, :]))
            v.wait_ge(p.DV.h, p.DV.n)
            t = p.DV.inc(v.reduce_sum(Z, p.q8[:], axis=AX.X))
            v.wait_ge(p.DV.h, t)
            t = p.DV.inc(v.reciprocal(alpha, Z))
            v.wait_ge(p.DV.h, t)
            t = p.DV.inc(v.tensor_scalar_mul(alpha, alpha, 0.5))
            v.wait_ge(p.DV.h, t)

        def c2(v):
            p.DV.inc(v.tensor_tensor(fxs, fnx,
                                     alpha.unsqueeze(2).to_broadcast([128, NH, 64]), ALU.mult))
            t = p.DV.inc(v.tensor_tensor(e8xs, e8x,
                                         alpha.unsqueeze(2).to_broadcast([128, NH, 8]), ALU.mult))
            v.wait_ge(p.DV.h, t)
            t = p.DV.inc(v.tensor_tensor(qS, e8xs.unsqueeze(3).to_broadcast([128, NH, 8, 8]),
                                         e8y.unsqueeze(2).to_broadcast([128, NH, 8, 8]), ALU.mult))
            v.wait_ge(p.DV.h, t)

        def c3(v):
            for h in range(NH):
                p.DV.inc(v.max(out=p.q8S[:, h, :], in_=qS[:, h, :]))
                v.wait_ge(p.DV.h, p.DV.n)
                p.DV.inc(v.match_replace(out=qtmp, in_to_replace=p.q8S[:, h, :],
                                         in_values=qS[:, h, :], imm_value=-1.0))
                v.wait_ge(p.DV.h, p.DV.n)
                p.DV.inc(v.max(out=p.q9S[:, h, :], in_=qtmp))
            v.wait_ge(p.DV.h, p.DV.n)
            t = p.DV.inc(v.tensor_tensor(thresh, p.q8S[:, :, 7], p.q9S[:, :, 0], ALU.add))
            v.wait_ge(p.DV.h, t)
            t = p.DV.inc(v.tensor_scalar_mul(thresh, thresh, 0.5))
            v.wait_ge(p.DV.h, t)

        def mk_b(h):
            def b_(v):
                dst = acc if h == 0 else tmp
                if h == 0 and tt >= 2:
                    wdma(v, tok_st[tt - 2])
                t = p.DV.inc(v.tensor_tensor(
                    dst.rearrange("p (a b) -> p a b", a=64),
                    fxs[:, h, :].unsqueeze(2).to_broadcast([128, 64, 64]),
                    fny[:, h, :].unsqueeze(1).to_broadcast([128, 64, 64]), ALU.mult))
                v.wait_ge(p.DV.h, t)
            return b_

        def mk_s(h):
            def s_(v):
                dst = acc if h == 0 else tmp
                t = p.DV.inc(v.scalar_tensor_tensor(dst, dst, thresh[:, h:h + 1], dst,
                                                    ALU.is_ge, ALU.mult))
                v.wait_ge(p.DV.h, t)
            return s_

        def mk_a():
            def a_(v):
                t = p.DV.inc(v.tensor_add(acc, acc, tmp))
                v.wait_ge(p.DV.h, t)
            return a_

        def fin(v):
            t_G[tt] = p.DV.n

        us = [(2.2, c1), (1.1, c2), (2.8, c3)]
        for h in range(NH):
            us.append((4.4, mk_b(h)))
            us.append((4.5, mk_s(h)))
            if h > 0:
                us.append((2.3, mk_a()))
        us.append((0.0, fin))
        return us

    units = []
    for tt in range(NTL):
        units += tile_units(tt)
    inserts = [(205.0 + 15.7 * do, 1, do) for do in range(8)]

    @blk.vector
    def _(v):
        wl(v, hold0)
        for tt in range(NTL):
            v.wait_ge(p.AC.h, tok_scp[tt])
            for ph in range(8):
                p.DV.inc(v.max(out=p.s1v8[:, tt, ph, :],
                               in_=p.simbuf[:, 512 * tt + 64 * ph:512 * tt + 64 * (ph + 1)]))
        assert p.DV.n == t_max_last
        wl(v, p.scrG_free_l)
        v.wait_ge(p.DV.h, t_max_last)
        a = p.DV.inc(v.tensor_tensor(
            p.scrG[:].rearrange("p (t h n) -> p t h n", t=NTL, h=8),
            p.simbuf[:].rearrange("p (t h n) -> p t h n", t=NTL, h=8),
            p.s1v8[:, :, :, 7:8].to_broadcast([128, NTL, 8, 64]), ALU.is_ge))
        assert a == t_mask
        v.wait_ge(p.DV.h, t_mask)
        v.wait_ge(p.AC.h, t_exn)
        a = p.DV.inc(v.tensor_mul(fnall, p.scrG[:], tmp))
        assert a == t_fneg
        v.wait_ge(p.DV.h, t_fneg)
        clock = 0.0
        ii = 0
        for dur, fn in units:
            while ii < len(inserts) and inserts[ii][0] <= clock:
                emit_p1b_ev(v, inserts[ii][2])
                clock += 1.2
                ii += 1
            fn(v)
            clock += dur
        while ii < len(inserts):
            emit_p1b_ev(v, inserts[ii][2])
            ii += 1

    @blk.sync
    def _(s):
        for tt in range(NTL):
            s.wait_ge(p.DV.h, t_G[tt])
            got = dma(s, p.DG, p.d_gd[g % 2, tt], sg[tt % 2])
            assert got == tok_st[tt]

    for tt in range(NTL):
        gname = f"G{FFN_CALL[0]}_{g}_{tt}"
        if gname in p.taps:
            @blk.sync
            def _(s, tt=tt, gname=gname):
                wdma(s, tok_st[tt])
                tk = dma(s, p.DS, p.taps[gname], p.d_gd[g % 2, tt])
                p.tap_hold.append((p.DS.h, tk))
    tokA_tap = tap(p, f"A{FFN_CALL[0]}_{g}", p.big[:].rearrange("p c t -> p (c t)"),
                   p.PO, tok_Ap[15])

    @blk.tensor
    def _(t):
        t.wait_ge(p.DV.h, max(tok_xn, p.tok_ones, p.psC_free))
        t.wait_ge(p.DW3.h, p.tok_const)
        for oc in range(4):
            pair, k = oc // 2, oc % 2
            if k == 0:
                wdma(t, t_wqf[pair])
            psQ = p.psC if oc % 2 == 0 else p.psD
            if oc >= 2:
                t.wait_ge(p.AC.h, tok_q2cp[oc - 2])
            last = None
            for dd in range(CH):
                for hh in range(2):
                    last = t.matmul(psQ[:, 512 * hh:512 * (hh + 1)],
                                    wqv[:, dd, 128 * k:128 * (k + 1)],
                                    p.xn[:, dd, 512 * hh:512 * (hh + 1)],
                                    start=(dd == 0), stop=(dd == CH - 1))
            assert p.TE.inc(last) == tok_q2mm[oc]
        for tt in range(NTL):
            t.wait_ge(p.AC.h, tok_q2cp[3] if tt < 2 else tok_scp[tt - 2])
            off = 512 * (tt % 2)
            last = None
            for c in range(4):
                last = t.matmul(p.psD[:, off + 128 * c:off + 128 * (c + 1)],
                                p.q2[:, c, 128 * tt:128 * (tt + 1)],
                                p.keysbd[:, c, :], start=True, stop=True)
            assert p.TE.inc(last) == tok_sim[tt]
        for c in range(32):
            wdma(t, t_wu[c])
            wdma(t, t_wg[c])
            if c == 0:
                t.wait_ge(p.DV.h, p.x_dv)
            if c == 1:
                t.wait_ge(p.AC.h, tok_scp[7])
            psG = p.psA if c % 2 == 0 else p.psC
            psU = p.psB if c % 2 == 0 else p.psD
            if c >= 2:
                t.wait_ge(p.AC.h, tok_ucp[c - 2])
            last = None
            for dd in range(CH):
                for hh in range(2):
                    last = t.matmul(psG[:, 512 * hh:512 * (hh + 1)],
                                    wgv4[:, c % 2, dd, :],
                                    p.xn[:, dd, 512 * hh:512 * (hh + 1)],
                                    start=(dd == 0), stop=(dd == CH - 1))
            assert p.TE.inc(last) == tok_Gmm[c]
            for dd in range(CH):
                for hh in range(2):
                    last = t.matmul(psU[:, 512 * hh:512 * (hh + 1)],
                                    wgv4[:, 2 + c % 2, dd, :],
                                    p.xn[:, dd, 512 * hh:512 * (hh + 1)],
                                    start=(dd == 0), stop=(dd == CH - 1))
            assert p.TE.inc(last) == tok_Umm[c]
        t.wait_ge(p.PO.h, tok_Ap[15])
        t.wait_ge(p.DW3.h, p.t_bdr)
        psOs = [p.psA, p.psB, p.psC, p.psD]
        for do in range(8):
            psO = psOs[do % 4]
            wdma(t, t_wd[2 * do])
            if do >= 4:
                t.wait_ge(p.DV.h, tok_ev[do - 4])
            last = None
            for c in range(16):
                for hh in range(2):
                    last = t.matmul(psO[:, 512 * hh:512 * (hh + 1)],
                                    wdv[:, 0, c, :],
                                    p.big[:, c, 512 * hh:512 * (hh + 1)],
                                    start=(c == 0), stop=False)
            assert p.TE.inc(last) == tok_dmid[do]
            wdma(t, t_wd[2 * do + 1])
            for c in range(16, 32):
                for hh in range(2):
                    t.matmul(psO[:, 512 * hh:512 * (hh + 1)],
                             wdv[:, 1, c % 16, :],
                             p.big[:, c, 512 * hh:512 * (hh + 1)],
                             start=False, stop=False)
            last = None
            for hh in range(2):
                last = t.matmul(psO[:, 512 * hh:512 * (hh + 1)],
                                p.bdr[0:1, 128 * do:128 * (do + 1)], p.ones_r[:],
                                start=False, stop=True)
            assert p.TE.inc(last) == tok_dmm[do]
        t.wait_ge(p.DV.h, tok_ev[5])
        for ec in range(32):
            wdma(t, t_dts[ec])
            psH = p.psA if ec % 2 == 0 else p.psB
            if ec >= 2:
                t.wait_ge(p.AC.h, tok_hds[ec - 2])
            last = None
            for dd in range(CH):
                for hh in range(2):
                    last = t.matmul(psH[:, 512 * hh:512 * (hh + 1)], dtv[:, ec % 4, dd, :],
                                    p.xn[:, dd, 512 * hh:512 * (hh + 1)],
                                    start=(dd == 0), stop=(dd == CH - 1))
            assert p.TE.inc(last) == tok_hdm[ec]

    @blk.scalar
    def _(s):
        wl(s, p.q2buf_free)
        wl(s, p.simbuf_free)
        wl(s, hold0)
        s.wait_ge(p.DW3.h, p.tok_const)
        for oc in range(4):
            psQ = p.psC if oc % 2 == 0 else p.psD
            s.wait_ge(p.TE.h, tok_q2mm[oc])
            a = p.AC.inc(s.activation(p.q2[:, oc, :], psQ[:], AF.Copy))
            assert a == tok_q2cp[oc]
        for tt in range(NTL):
            off = 512 * (tt % 2)
            s.wait_ge(p.TE.h, tok_sim[tt])
            a = p.AC.inc(s.activation(p.simbuf[:, 512 * tt:512 * (tt + 1)],
                                      p.psD[:, off:off + 512], AF.Copy))
            assert a == tok_scp[tt]
        s.wait_ge(p.AC.h, tok_scp[7])
        wl(s, p.gT_free_l)
        p.AC.inc(s.activation(tmp[:, 0:2048], p.simbuf[:, 0:2048], AF.Exp))
        a = p.AC.inc(s.activation(tmp[:, 2048:4096], p.simbuf[:, 2048:4096], AF.Exp))
        assert a == t_exn
        s.wait_ge(p.DV.h, t_max_last)
        a = p.AC.inc(s.activation(p.e8all[:].rearrange("p a b -> p (a b)"),
                                  p.s1v8[:].rearrange("p a b c -> p (a b c)"), AF.Exp))
        assert a == t_e8
        for c in range(32):
            psG = p.psA if c % 2 == 0 else p.psC
            psU = p.psB if c % 2 == 0 else p.psD
            s.wait_ge(p.TE.h, tok_Gmm[c])
            if c >= 4:
                s.wait_ge(p.PO.h, tok_Ap[(c - 4) // 2])
            a = p.AC.inc(s.activation(p.silb[:, c % 4, :],
                                      psG[:], AF.Silu, bias=p.bias[:, c:c + 1]))
            assert a == tok_sil[c]
            s.wait_ge(p.TE.h, tok_Umm[c])
            a = p.AC.inc(s.activation(p.big[:, c, :], psU[:], AF.Identity,
                                      bias=p.bias[:, 32 + c:33 + c]))
            assert a == tok_ucp[c]
        if tokA_tap:
            s.wait_ge(p.DS.h, tokA_tap)
        for ec in range(32):
            psH = p.psA if ec % 2 == 0 else p.psB
            s.wait_ge(p.TE.h, tok_hdm[ec])
            a = p.AC.inc(s.activation(p.big[:, ec, :], psH[:], AF.Silu))
            assert a == tok_hds[ec]

    dests = [p.gT[:], p.scrG[:].rearrange("p (c t) -> p c t", c=32),
             gacc2.rearrange("p (c t) -> p c t", c=32)]
    trp_tok = [None] * NTL
    tok_wt = [None] * NTL
    for tt in range(NTL):
        @blk.sync
        def _(s, tt=tt):
            wdma(s, tok_st[tt])
            if tt == 0:
                s.wait_ge(p.DV.h, t_G[NTL - 1])
            elif tt == 1:
                wdma(s, tok_st[6])
            elif tt == 2:
                wdma(s, tok_st[7])
            else:
                s.wait_ge(p.DV.h, tok_wt[tt - 3])
            s.wait_ge(p.DT.h, p.DT.n)
            trp_tok[tt] = p.DT.inc(s.dma_start_transpose(dests[tt % 3], p.d_gd[g % 2, tt]), 16)

        @blk.vector
        def _(v, tt=tt):
            v.wait_ge(p.AC.h, tok_hds[31])
            v.wait_ge(p.DT.h, trp_tok[tt])
            sl = p.big[:, :, 128 * tt:128 * (tt + 1)]
            tok_wt[tt] = p.DV.inc(v.tensor_mul(sl, sl, dests[tt % 3]))

    tk = tap(p, f"WT{FFN_CALL[0]}_{g}", p.big[:].rearrange("p c t -> p (c t)"),
             p.DV, tok_wt[NTL - 1])
    if tk:
        p.tap_hold.append((p.DS.h, tk))

    p.psD_free = (p.DV, tok_ev[7])
    p.scr_free = (p.PO, tok_Ap[15])
    p.xn_free = (p.TE, tok_hdm[31])
    ln_fin = next_ln() if next_ln is not None else None

    t_p2b0 = p.TE.n
    assert t_p2b0 == te0 + 124 + (18 if next_ln is not None else 0)
    bump = 2 if next_ln is not None else 0
    tok_ueq = {(do, qq): t_p2b0 + 4 * do + qq + 1 + (bump if do >= 2 else 0)
               for do in range(8) for qq in range(4)}
    tok_umm = [tok_ueq[(do, 3)] for do in range(8)]

    uev = p.wr1[:].rearrange("p (b c n) -> p b c n", b=4, c=8)
    ue_r = p.d_ue.rearrange("(c p) n -> p c n", p=128)
    tok_ue = {}
    tok_uev = [0] * 8
    for do in range(8):
        psO = [p.psA, p.psB, p.psC][do % 3]
        for qq in range(4):
            j = 4 * do + qq

            @blk.gpsimd
            def _(gg, do=do, qq=qq, j=j):
                if j < 4:
                    gg.wait_ge(p.TE.h, tok_hdm[28 + j])
                else:
                    pj = j - 4
                    gg.wait_ge(p.TE.h, tok_ueq[(pj // 4, pj % 4)])
                tok_ue[(do, qq)] = dma(gg, p.DW2, uev[:, j % 4, :, :],
                                       ue_r[:, 8 * qq:8 * (qq + 1), 128 * do:128 * (do + 1)])

            @blk.tensor
            def _(t, do=do, qq=qq, j=j, psO=psO):
                wdma(t, tok_ue[(do, qq)])
                if qq == 0:
                    t.wait_ge(p.DV.h, tok_wt[NTL - 1])
                    if do >= 3:
                        t.wait_ge(p.DV.h, tok_uev[do - 3])
                last = None
                for cc in range(8):
                    c = 8 * qq + cc
                    for hh in range(2):
                        last = t.matmul(psO[:, 512 * hh:512 * (hh + 1)],
                                        uev[:, j % 4, cc, :],
                                        p.big[:, c, 512 * hh:512 * (hh + 1)],
                                        start=(c == 0), stop=(c == 31))
                a = p.TE.inc(last)
                assert a == tok_ueq[(do, qq)]

        @blk.vector
        def _(v, do=do, g=g, psO=psO):
            v.wait_ge(p.TE.h, tok_umm[do])
            tok_uev[do] = p.DV.inc(v.tensor_tensor(
                p.x[:, do, TG * g:TG * (g + 1)], psO[:],
                p.x[:, do, TG * g:TG * (g + 1)], ALU.add))

        if do == 1 and ln_fin is not None:
            p.next_tok_xn = ln_fin()
            ln_fin = None

    if ln_fin is not None:
        p.next_tok_xn = ln_fin()
    p.x_dv = tok_uev[7]
    p.psC_free = tok_uev[5]
    p.big_free = (p.TE, tok_umm[7])
    p.ring_free = (p.TE, tok_umm[7])
    p.ring2_free = (p.TE, tok_q2mm[3])
    p.ub_free = (p.TE, tok_dmm[7])
    p.p2b_last_te = tok_umm[7]
    p.tok_wt_last = tok_wt[NTL - 1]
    p.last_trp = trp_tok[NTL - 1]
    p.q2buf_free = [(p.TE.h, tok_umm[7]), (p.DV.h, tok_wt[NTL - 1])]
    p.simbuf_free = [(p.DV.h, tok_wt[NTL - 1]), (tok_st[7][0].h, tok_st[7][1])]
    p.scrG_free_l = [(p.DV.h, tok_wt[NTL - 1]), (tok_st[6][0].h, tok_st[6][1])]
    p.gT_free_l = [(p.DV.h, tok_wt[NTL - 1])]
    scope.__exit__(None, None, None)


def emit_attn(p):
    blk = p.blk
    scope = p.nc.named_scope("attn_proj")
    scope.__enter__()

    wkvv = p.q2.rearrange("p o t -> p (o t)")[:, 0:2048].rearrange("p (d n) -> p d n", d=CH)
    wkv_r = p.d_wkv.rearrange("(d p) n -> p d n", p=128)
    wq_r = p.d_wq.rearrange("(d p) n -> p d n", p=128)
    wqh = p.wr1[:, 0:2048].rearrange("p (b d n) -> p b d n", b=2, d=CH)
    t_wkv = [0]

    @blk.gpsimd
    def _(gg):
        gg.wait_ge(p.TE.h, p.p2b_last_te)
        gg.wait_ge(p.DV.h, p.x_dv)
        wl(gg, p.simbuf_free)
        wl(gg, p.tap_hold)
        p.t_pe = dma(gg, p.DW3, p.pe, p.d_pet)
        p.t_ident = dma(gg, p.DW3, p.ident, p.d_ident)
        p.t_stair = dma(gg, p.DW3, p.stair.rearrange("p a b -> p (a b)"), p.d_stair)
        t_wkv[0] = dma(gg, p.DW3, wkvv, wkv_r)

    kk = p.scrP[:, 0:2048]
    vv = p.scrP[:, 2048:4096]
    psT = p.psB[:].bitcast(BF16)
    tok_vt = [0] * 16

    def emit_vt(c):
        t_vtm = [0]

        @blk.tensor
        def _(t, c=c):
            t.wait_ge(p.DV.h, tok_kv[1])
            t.wait_ge(p.DW3.h, p.t_ident)
            if c > 0:
                t.wait_ge(p.DV.h, tok_vt[c - 1])
            t_vtm[0] = p.TE.inc(t.transpose(psT[:, 0:128], vv[:, 128 * c:128 * (c + 1)],
                                            p.ident[:]))

        @blk.vector
        def _(v, c=c):
            v.wait_ge(p.TE.h, t_vtm[0])
            if c == 0:
                v.wait_ge(p.DV.h, p.tok_wt_last)
            tok_vt[c] = p.DV.inc(v.tensor_copy(p.gT[:, c, :], psT[:, 0:128]))
    ln_scr = p.big[:, 24:27, :].rearrange("p a b -> p (a b)").bitcast(F32)
    tok_kv = [0, 0]
    t_kv_k = [0, 0]
    tok_qh = [[0] * 2 for _ in range(NHEAD)]
    t_qev = [0] * NHEAD
    t_kvev = [0]
    t_qmm = [0]
    p.t_wqh = [0] * 16
    p.t_qmm_j = [0] * 16
    for half in range(2):
        if half == 0 and p.next_tok_xn is not None:
            tok_xn = p.next_tok_xn
            p.next_tok_xn = None
        elif half == 0:
            tok_xn = emit_ln(p, TG * half, ln_scr, (p.TE, p.p2b_last_te))
        else:
            p.psD_free = (p.DV, t_kv_k[0])
            tok_xn = emit_ln(p, TG * half, ln_scr, (p.TE, p.p2b_last_te),
                             bank=p.psA)
        for j in range(2):
            psK = p.psA if j == 0 else p.psB

            @blk.tensor
            def _(t, j=j, half=half, psK=psK, tok_xn=tok_xn):
                t.wait_ge(p.DW3.h, t_wkv[0])
                t.wait_ge(p.DV.h, tok_xn)
                if half == 1:
                    t.wait_ge(p.DV.h, t_kvev[0])
                else:
                    t.wait_ge(p.DV.h, p.x_dv)
                last = None
                for dd in range(CH):
                    for hh in range(2):
                        last = t.matmul(psK[:, 512 * hh:512 * (hh + 1)],
                                        wkvv[:, dd, 128 * j:128 * (j + 1)],
                                        p.xn[:, dd, 512 * hh:512 * (hh + 1)],
                                        start=(dd == 0), stop=(dd == CH - 1))
                p.t_kvmm = p.TE.inc(last)

            @blk.vector
            def _(v, j=j, half=half, psK=psK):
                v.wait_ge(p.TE.h, p.t_kvmm)
                v.wait_ge(p.DW3.h, p.t_pe)
                dst = (kk if j == 0 else vv)[:, TG * half:TG * (half + 1)]
                if j == 0:
                    tok_kv[half] = p.DV.inc(v.scalar_tensor_tensor(
                        dst, psK[:], p.bias[:, 72:73], p.pe[:, TG * half:TG * (half + 1)],
                        ALU.add, ALU.add))
                    t_kv_k[half] = p.DV.n
                else:
                    tok_kv[half] = p.DV.inc(v.tensor_scalar_add(dst, psK[:], p.bias[:, 73:74]))
                t_kvev[0] = p.DV.n

        for h in range(NHEAD):
            psQ = p.psC if h % 2 == 0 else p.psD
            j = 8 * half + h

            @blk.gpsimd
            def _(gg, h=h, half=half, j=j):
                if j < 2:
                    gg.wait_ge(p.TE.h, p.p2b_last_te)
                else:
                    gg.wait_ge(p.TE.h, p.t_qmm_j[j - 2])
                p.t_wqh[j] = dma(gg, p.DW1, wqh[:, j % 2, :, :],
                                 wq_r[:, :, 128 * h:128 * (h + 1)])

            @blk.tensor
            def _(t, h=h, half=half, j=j, psQ=psQ, tok_xn=tok_xn):
                wdma(t, p.t_wqh[j])
                t.wait_ge(p.DV.h, tok_xn)
                prev = t_qev[h - 2] if h >= 2 else (t_qev[NHEAD - 2 + h] if half == 1 else 0)
                t.wait_ge(p.DV.h, prev)
                if half == 0 and h < 2:
                    t.wait_ge(p.DV.h, p.ln_psD_free_dv)
                last = None
                for dd in range(CH):
                    for hh in range(2):
                        last = t.matmul(psQ[:, 512 * hh:512 * (hh + 1)], wqh[:, j % 2, dd, :],
                                        p.xn[:, dd, 512 * hh:512 * (hh + 1)],
                                        start=(dd == 0), stop=(dd == CH - 1))
                t_qmm[0] = p.TE.inc(last)
                p.t_qmm_j[j] = t_qmm[0]

            @blk.vector
            def _(v, h=h, half=half, psQ=psQ):
                v.wait_ge(p.TE.h, t_qmm[0])
                v.wait_ge(p.DW3.h, p.t_pe)
                if half == 0 and h == 0:
                    v.wait_ge(p.TE.h, p.p2b_last_te)
                tok_qh[h][half] = p.DV.inc(v.scalar_tensor_tensor(
                    p.big[:, 2 * h + half, :], psQ[:], p.bias[:, 64 + h:65 + h],
                    p.pe[:, TG * half:TG * (half + 1)], ALU.add, ALU.add))
                t_qev[h] = p.DV.n

            if half == 1:
                emit_vt(2 * h)
                emit_vt(2 * h + 1)
        p.xn_free = (p.TE, t_qmm[0])

    scope.__exit__(None, None, None)
    scope = p.nc.named_scope("attn_heads")
    scope.__enter__()
    Ev = p.scrG[:].rearrange("p (s t) -> p s t", s=8)
    zr = p.rows[0:1, 0:512]
    zbs = p.gT[:, 16:20, :].rearrange("p a b -> p (a b)")
    tok_att = {-2: 0, -1: 0}
    t_zr = {-2: 0, -1: 0}
    t_zbs = {-2: 0, -1: 0}
    pv_tok = {}
    pv_last = {}
    exp_prev = [0, 0]
    t_zb = {}

    for h in range(NHEAD):
        for Q in range(4):
            nkc = 4 * Q + 4
            idx = 4 * h + Q
            npair = nkc // 2
            psOt = p.psA if idx % 2 == 0 else p.psB
            psO = psOt[:, 0:512]
            zrow = psOt[0:1, 512:1024]
            teI, acI, dvI = p.TE.n, p.AC.n, p.DV.n
            tok_exp = [acI + (kc // 2) + 1 for kc in range(nkc)]
            t_mask = dvI + 1
            t_sm_toks = []
            qblk = p.big[:, 2 * h + Q // 2, 512 * (Q % 2):512 * (Q % 2) + 512]

            def pv(t, kc, first, last_flag, idx=idx, psO=psO, zrow=zrow, nkc=nkc,
                   tok_exp=tok_exp, t_mask=t_mask):
                slot = kc % 8
                if kc >= nkc - 4:
                    t.wait_ge(p.DV.h, t_mask)
                else:
                    t.wait_ge(p.AC.h, tok_exp[kc])
                if first:
                    t.wait_ge(p.DV.h, tok_vt[15])
                    t.wait_ge(p.DV.h, t_zr[idx - 2])
                    t.wait_ge(p.DV.h, tok_att[idx - 2])
                    if idx < 2:
                        t.wait_ge(p.DV.h, t_qev[NHEAD - 1])
                t.matmul(zrow, p.onesb_c[:], Ev[:, slot, :], start=first, stop=last_flag)
                mm = t.matmul(psO, p.gT[:, kc, :], Ev[:, slot, :], start=first, stop=last_flag)
                pv_tok[slot] = p.TE.inc(mm)
                if last_flag:
                    pv_last[idx] = pv_tok[slot]

            @blk.tensor
            def _(t, h=h, Q=Q, idx=idx, nkc=nkc, npair=npair, qblk=qblk, tok_exp=tok_exp):
                t.wait_ge(p.DV.h, tok_kv[1])
                t.wait_ge(p.DV.h, tok_qh[h][Q // 2])
                pend_pv = []
                for pr in range(npair):
                    kc0 = 2 * pr
                    psS = p.psC if pr % 2 == 0 else p.psD
                    if kc0 >= 4:
                        t.wait_ge(p.AC.h, tok_exp[kc0 - 4])
                    else:
                        t.wait_ge(p.AC.h, exp_prev[pr % 2])
                        if idx == 0:
                            t.wait_ge(p.DV.h, t_qev[NHEAD - 1])
                    if pr % 2 == 1:
                        t.wait_ge(p.DV.h, t_zbs[idx - 2])
                    mm = None
                    for j in range(2):
                        kc = kc0 + j
                        mm = t.matmul(psS[:, 512 * j:512 * (j + 1)],
                                      kk[:, 128 * kc:128 * (kc + 1)], qblk,
                                      start=True, stop=True)
                    t_sm_toks.append(p.TE.inc(mm))
                    if kc0 + 1 < nkc - 4:
                        pend_pv.append(kc0)
                        if len(pend_pv) > 1:
                            k0 = pend_pv.pop(0)
                            pv(t, k0, first=(k0 == 0), last_flag=False)
                            pv(t, k0 + 1, first=False, last_flag=False)
                for k0 in pend_pv:
                    pv(t, k0, first=(k0 == 0), last_flag=False)
                    pv(t, k0 + 1, first=False, last_flag=False)
                for kc in range(nkc - 4, nkc):
                    pv(t, kc, first=(kc == 0), last_flag=(kc == nkc - 1))
                if idx >= 1:
                    t.wait_ge(p.DV.h, t_zr[idx - 1])
                    t.wait_ge(p.DV.h, t_zbs[idx - 2])
                    t.wait_ge(p.AC.h, tok_exp[2 * (npair - 1)])
                    t_zb[idx - 1] = p.TE.inc(t.matmul(p.psD[:, 0:512], p.ones_rf[:], zr,
                                                      start=True, stop=True))

            @blk.scalar
            def _(s, h=h, Q=Q, idx=idx, nkc=nkc, npair=npair, tok_exp=tok_exp,
                  t_sm_toks=t_sm_toks):
                for pr in range(npair):
                    kc0 = 2 * pr
                    slot = kc0 % 8
                    psS = p.psC if pr % 2 == 0 else p.psD
                    s.wait_ge(p.TE.h, t_sm_toks[pr])
                    for sl in (slot, slot + 1):
                        if sl in pv_tok and pv_tok[sl] < teI:
                            s.wait_ge(p.TE.h, pv_tok[sl])
                    if idx == 0 and kc0 == 0:
                        s.wait_ge(p.DT.h, p.last_trp)
                        s.wait_ge(p.DV.h, p.tok_wt_last)
                    a = p.AC.inc(s.activation(Ev[:, slot:slot + 2, :], psS[:, 0:1024],
                                              AF.Exp, scale=SCALE))
                    assert a == tok_exp[kc0]

            s0 = (nkc - 4) % 8

            @blk.vector
            def _(v, idx=idx, s0=s0, nkc=nkc, tok_exp=tok_exp, psO=psO, zrow=zrow):
                v.wait_ge(p.AC.h, tok_exp[nkc - 1])
                v.wait_ge(p.DW3.h, p.t_stair)
                a = p.DV.inc(v.tensor_mul(Ev[:, s0:s0 + 4, :], Ev[:, s0:s0 + 4, :],
                                          p.stair[:]))
                assert a == t_mask
                if idx >= 1:
                    v.wait_ge(p.TE.h, t_zb[idx - 1])
                    t1 = p.DV.inc(v.tensor_copy(zbs[:, 0:512], p.psD[:, 0:512]))
                    v.wait_ge(p.DV.h, t1)
                    t_zbs[idx - 1] = t1
                    pidx = idx - 1
                    ph, pQ = pidx // 4, pidx % 4
                    ppsO = (p.psA if pidx % 2 == 0 else p.psB)[:, 0:512]
                    tok_att[pidx] = p.DV.inc(v.tensor_mul(
                        p.big[:, 16 + 2 * ph + pQ // 2, 512 * (pQ % 2):512 * (pQ % 2) + 512],
                        ppsO, zbs[:, 0:512]))
                v.wait_ge(p.TE.h, pv_last[idx])
                t_zr[idx] = p.DV.inc(v.reciprocal(zr, zrow))

            exp_prev[0] = tok_exp[2 * (npair - 2)]
            exp_prev[1] = tok_exp[2 * (npair - 1)]

    lidx = 4 * NHEAD - 1

    @blk.tensor
    def _(t):
        t.wait_ge(p.DV.h, t_zr[lidx])
        t.wait_ge(p.DV.h, t_zbs[lidx - 1])
        t.wait_ge(p.AC.h, exp_prev[1])
        t_zb[lidx] = p.TE.inc(t.matmul(p.psD[:, 0:512], p.ones_rf[:], zr, start=True, stop=True))

    @blk.vector
    def _(v):
        v.wait_ge(p.TE.h, t_zb[lidx])
        t1 = p.DV.inc(v.tensor_copy(zbs[:, 0:512], p.psD[:, 0:512]))
        v.wait_ge(p.DV.h, t1)
        ph, pQ = lidx // 4, lidx % 4
        ppsO = (p.psA if lidx % 2 == 0 else p.psB)[:, 0:512]
        tok_att[lidx] = p.DV.inc(v.tensor_mul(
            p.big[:, 16 + 2 * ph + pQ // 2, 512 * (pQ % 2):512 * (pQ % 2) + 512],
            ppsO, zbs[:, 0:512]))
    scope.__exit__(None, None, None)
    scope = p.nc.named_scope("attn_wo")
    scope.__enter__()
    wov = p.wr1[:, 0:2048].rearrange("p (b d n) -> p b d n", b=2, d=CH)
    wo_r = p.d_wo.rearrange("(d p) n -> p d n", p=128)
    tok_wo = [0] * 8
    tok_omm = [0] * 8
    tok_oev = [0] * 16
    for oc in range(8):
        @blk.gpsimd
        def _(gg, oc=oc):
            if oc < 2:
                gg.wait_ge(p.TE.h, p.t_qmm_j[15])
            else:
                gg.wait_ge(p.TE.h, tok_omm[oc - 2])
            tok_wo[oc] = dma(gg, p.DW1, wov[:, oc % 2, :, :],
                             wo_r[:, :, 128 * oc:128 * (oc + 1)])
        for blkk in range(2):
            psX = p.psC if (2 * oc + blkk) % 2 == 0 else p.psD
            t_omm = [0]

            @blk.tensor
            def _(t, oc=oc, blkk=blkk, psX=psX):
                wdma(t, tok_wo[oc])
                t.wait_ge(p.DV.h, tok_att[4 * NHEAD - 1])
                j = 2 * oc + blkk
                if j >= 2:
                    t.wait_ge(p.DV.h, tok_oev[j - 2])
                last = None
                for hh in range(CH):
                    for q2_ in range(2):
                        last = t.matmul(psX[:, 512 * q2_:512 * (q2_ + 1)], wov[:, oc % 2, hh, :],
                                        p.big[:, 16 + 2 * hh + blkk, 512 * q2_:512 * (q2_ + 1)],
                                        start=(hh == 0), stop=(hh == CH - 1))
                t_omm[0] = p.TE.inc(last)
                tok_omm[oc] = t_omm[0]

            @blk.vector
            def _(v, oc=oc, blkk=blkk, psX=psX):
                v.wait_ge(p.TE.h, t_omm[0])
                if 2 * oc + blkk == 0:
                    v.wait_ge(p.DV.h, tok_att[4 * NHEAD - 1])
                tok_oev[2 * oc + blkk] = p.DV.inc(v.tensor_tensor(
                    p.x[:, oc, TG * blkk:TG * (blkk + 1)], psX[:],
                    p.x[:, oc, TG * blkk:TG * (blkk + 1)], ALU.add))
    scope.__exit__(None, None, None)
    p.x_dv = tok_oev[15]
    p.psC_free = tok_oev[14]
    p.psD_free = (p.DV, tok_oev[15])
    p.big_free = (p.TE, tok_omm[7])
    p.ring_free = (p.TE, tok_omm[7])
    p.ub_free = (p.TE, tok_omm[7])
    p.t_c4 = t_qev[NHEAD - 1]
    p.scr_free = (p.TE, tok_omm[7])
    p.q2buf_free = [(p.TE.h, tok_omm[7]), (p.DV.h, tok_oev[15])]
    p.scrG_free_l = [(p.TE.h, tok_omm[7])]
    p.gT_free_l = [(p.TE.h, tok_omm[7]), (p.DV.h, tok_oev[15])]
    p.simbuf_free = [(p.DV.h, tok_oev[15])]
    p.ring2_free = (p.DV, tok_oev[15])


def emit_store(p):
    blk = p.blk

    @blk.sync
    def _(s):
        s.wait_ge(p.DV.h, p.x_dv)
        for c in range(CH):
            dma(s, p.DS, p.d_out[128 * c:128 * (c + 1), :], p.x[:, c, :])
        s.wait_ge(p.DS.h, p.DS.n)
        s.wait_ge(p.DT.h, p.DT.n)


# version 28
# speedup vs baseline: 1.0502x; 1.0007x over previous
import math
import contextlib
import numpy as np
import concourse.bass as bass
import concourse.mybir as mybir
from concourse.bass_utils import run_bass_kernel_spmd
import concourse.dve_ops as _dve_ops
from concourse.dve_op import DveOp as _DveOp
from concourse.dve_spec import Spec as _Spec, Src0 as _Src0, Src1 as _Src1, C0 as _C0, \
    Zero as _Zero, select as _select
from concourse.dve_table_gen import dve_ver_for as _dve_ver_for

F32 = mybir.dt.float32
BF16 = mybir.dt.bfloat16
AF = mybir.ActivationFunctionType
ALU = mybir.AluOpType
AX = mybir.AxisListType

NCORE = 8
D = 1024
T = 2048
CH = 8
DC = 4096
NE = 4096
NH = 4
TG = 1024
NTL = 8
EPS = 1e-5
NHEAD = 8
SCALE = 1.0 / math.sqrt(128.0)

TAPS = {}


def _sel_add_ref(in0, in1, s0, s1, imm2):
    return np.where(in0 >= s0, in0, 0.0) + in1


def _register_dve_op(name, spec, ref):
    for o in _dve_ops.OPS:
        if o.name == name:
            return o
    ver = _dve_ver_for("TRN2")
    op = _DveOp(name, spec, subdim=False, uops_sha={}, reference=ref)
    try:
        op.compile(ver)
        sha = None
    except ValueError as e:
        import re as _re
        m = _re.search(r"\(%s: (\w+)" % ver, str(e))
        sha = m.group(1)
    if sha is not None:
        op = _DveOp(name, spec, subdim=False, uops_sha={ver: sha}, reference=ref)
        op.compile(ver)
    _dve_ops.OPS.append(op)
    _dve_ops._SUB_OPCODE_FOR_NAME[op.name] = (_dve_ops._CUSTOM_DVE_ROW_BASE
                                              + len(_dve_ops.OPS) - 1)
    _dve_ops.CUSTOM_DVE_SPECS[op.name] = spec
    _dve_ops.CUSTOM_DVE_REFS[op.name] = ref
    return op


XBAR_TRANSPOSE = True


def _host_consts():
    pos = np.arange(T, dtype=np.float64)[:, None]
    inv = np.exp(-math.log(10000.0) * np.arange(0, 128, 2, dtype=np.float64) / 128.0)
    ang = pos * inv[None, :]
    pe = np.zeros((T, 128), dtype=np.float64)
    pe[:, 0::2] = np.sin(ang)
    pe[:, 1::2] = np.cos(ang)
    pet = np.ascontiguousarray(pe.T.astype(np.float32))
    pp = np.arange(128)[:, None, None]
    cc = np.arange(4)[None, :, None]
    jj = np.arange(512)[None, None, :]
    stair = ((128 * cc + pp) <= jj).astype(np.float32).reshape(128, 4 * 512)
    ident = np.eye(128, dtype=np.float32)
    return pet, stair, ident


def kernel(**inp):
    pet, stair, ident = _host_consts()
    x = np.ascontiguousarray(inp["L0_attn_x"])
    com = {
        "wq": np.ascontiguousarray(inp["L0_attn_Wq"]),
        "wkv": np.ascontiguousarray(inp["L0_attn_Wkv"]),
        "wo": np.ascontiguousarray(inp["L0_attn_Wo"]),
        "wg": np.ascontiguousarray(inp["L0_ffn_Wg"]),
        "wu": np.ascontiguousarray(inp["L0_ffn_Wu"]),
        "wd": np.ascontiguousarray(inp["L0_ffn_Wd"]),
        "wqf": np.ascontiguousarray(inp["L0_ffn_Wq"]),
        "dte": np.ascontiguousarray(inp["L0_ffn_down_embed"].T),
        "ue": np.ascontiguousarray(inp["L0_ffn_up_embed"]),
        "keyst": np.ascontiguousarray(inp["L0_ffn_keys"].transpose(2, 0, 3, 1)),
        "bqc": np.ascontiguousarray(inp["L0_attn_bq"].reshape(8, 128).T),
        "bkvc": np.ascontiguousarray(inp["L0_attn_bkv"].reshape(2, 128).T),
        "bgc": np.ascontiguousarray(inp["L0_ffn_bg"].reshape(32, 128).T),
        "buc": np.ascontiguousarray(inp["L0_ffn_bu"].reshape(32, 128).T),
        "bdr": np.ascontiguousarray(inp["L0_ffn_bd"].reshape(1, 1024)),
        "pet": pet, "stair": stair, "ident": ident,
    }
    nc = build_nc()
    in_maps = []
    for b in range(NCORE):
        m = dict(com)
        m["xt"] = np.ascontiguousarray(x[b].T)
        in_maps.append(m)
    res = run_bass_kernel_spmd(nc, in_maps, core_ids=list(range(NCORE)))
    out = np.stack([np.ascontiguousarray(res.results[b]["out"].T) for b in range(NCORE)], axis=0)
    return out.astype(np.float32)


class Cnt:
    def __init__(self, ctx, nc, name):
        self.h = ctx.enter_context(nc.semaphore(name))
        self.n = 0

    def inc(self, instr, k=1):
        self.n += k
        instr.then_inc(self.h, k)
        return self.n


class LanePair:

    def __init__(self, ctx, nc, name):
        self.lanes = [Cnt(ctx, nc, name + "a"), Cnt(ctx, nc, name + "b")]
        self.i = 0

    def issue(self, eng, out, in_, **kw):
        lane = self.lanes[self.i % 2]
        self.i += 1
        eng.wait_ge(lane.h, lane.n)
        tok = lane.inc(eng.dma_start(out=out, in_=in_, **kw), 16)
        return (lane, tok)


class P:
    pass


def build_nc():
    nc = bass.Bass("TRN2", target_bir_lowering=False)
    p = P()
    p.nc = nc
    dt = nc.dram_tensor
    p.d_xt = dt("xt", [D, T], F32, kind="ExternalInput").ap()
    p.d_wq = dt("wq", [D, D], F32, kind="ExternalInput").ap()
    p.d_wkv = dt("wkv", [D, 256], F32, kind="ExternalInput").ap()
    p.d_wo = dt("wo", [D, D], F32, kind="ExternalInput").ap()
    p.d_wg = dt("wg", [D, DC], F32, kind="ExternalInput").ap()
    p.d_wu = dt("wu", [D, DC], F32, kind="ExternalInput").ap()
    p.d_wd = dt("wd", [DC, D], F32, kind="ExternalInput").ap()
    p.d_wqf = dt("wqf", [D, 512], F32, kind="ExternalInput").ap()
    p.d_dt = dt("dte", [D, NE], F32, kind="ExternalInput").ap()
    p.d_ue = dt("ue", [NE, D], F32, kind="ExternalInput").ap()
    p.d_keyst = dt("keyst", [2, NH, 64, 64], F32, kind="ExternalInput").ap()
    p.d_bqc = dt("bqc", [128, 8], F32, kind="ExternalInput").ap()
    p.d_bkvc = dt("bkvc", [128, 2], F32, kind="ExternalInput").ap()
    p.d_bgc = dt("bgc", [128, 32], F32, kind="ExternalInput").ap()
    p.d_buc = dt("buc", [128, 32], F32, kind="ExternalInput").ap()
    p.d_bdr = dt("bdr", [1, D], F32, kind="ExternalInput").ap()
    p.d_pet = dt("pet", [128, T], F32, kind="ExternalInput").ap()
    p.d_stair = dt("stair", [128, 4 * 512], F32, kind="ExternalInput").ap()
    p.d_ident = dt("ident", [128, 128], F32, kind="ExternalInput").ap()
    p.d_gd = dt("gspill", [2, NTL, 128, NE], BF16, kind="Internal").ap()
    p.d_out = dt("out", [D, T], F32, kind="ExternalOutput").ap()
    p.taps = {name: dt("tap_" + name, list(shape), dtp, kind="ExternalOutput").ap()
              for name, (shape, dtp) in TAPS.items()}

    ctx = contextlib.ExitStack()

    def al(name, shape, dtp):
        return ctx.enter_context(nc.sbuf_tensor(name, shape, dtp))

    p.x = al("x", [128, CH, T], F32)
    p.big = al("big", [128, 32, TG], BF16)
    p.xn = al("xn", [128, CH, TG], BF16)
    p.scrG = al("scrG", [128, NE], BF16)
    p.scrP = al("scrP", [128, NE], BF16)
    p.gT = al("gT", [128, 32, 128], BF16)
    p.wr1 = al("wr1", [128, 4096], BF16)
    p.wr2 = al("wr2", [128, 2048], BF16)
    p.cmpb = al("cmpb", [128, 896], BF16)
    p.simbuf = al("simbuf", [128, 4096], F32)
    p.s1v8 = al("s1v8", [128, 8, 8, 8], F32)
    p.e8all = al("e8all", [128, 8, 64], BF16)
    p.bias = al("bias", [128, 80], F32)
    p.q8 = al("q8", [128, NH, 8], F32)
    p.q8S = al("q8S", [128, NH, 8], BF16)
    p.q9S = al("q9S", [128, NH, 8], BF16)
    p.zal = al("zal", [128, 3 * NH], F32)
    p.ub = al("ub", [128, 128], BF16)
    p.ident = p.ub[:, :]
    p.bdrt = al("bdr_sb", [1, 1024], BF16)
    p.bdr = p.bdrt[0:1, :]
    p.ones_r = al("ones_r", [1, 512], BF16)
    p.ones_rf = al("ones_rf", [1, 128], F32)
    p.ones_c = al("ones_c", [128, 1], F32)
    p.onesb_c = al("onesb_c", [128, 1], BF16)
    p.eps_c = al("eps_c", [128, 1], F32)
    p.keysbd = al("keysbd", [128, 4, 128], BF16)
    p.sil = p.scrP[:].bitcast(F32)
    p.silb = p.scrP[:].rearrange("p (s n) -> p s n", s=4)
    p.pe = p.simbuf[:].bitcast(BF16)[:, 0:2048]
    p.rows = p.simbuf[0:1, 1024:2048]
    p.q2 = p.big[:, 28:32, :]
    p.stair = p.wr2[:].rearrange("p (a b) -> p a b", a=4)

    p.psA = ctx.enter_context(nc.psum_tensor("psA", [128, 1024], F32))
    p.psB = ctx.enter_context(nc.psum_tensor("psB", [128, 1024], F32))
    p.psC = ctx.enter_context(nc.psum_tensor("psC", [128, 1024], F32))
    p.psD = ctx.enter_context(nc.psum_tensor("psD", [128, 1024], F32))

    p.TE = Cnt(ctx, nc, "qTE")
    p.DV = Cnt(ctx, nc, "qDV")
    p.AC = Cnt(ctx, nc, "qAC")
    p.PO = Cnt(ctx, nc, "qPO")
    p.DW1 = LanePair(ctx, nc, "qDW1")
    p.DW2 = LanePair(ctx, nc, "qDW2")
    p.DW3 = Cnt(ctx, nc, "qDW3")
    p.DS = Cnt(ctx, nc, "qDS")
    p.DT = Cnt(ctx, nc, "qDT")
    p.DG = LanePair(ctx, nc, "qDG")
    p.x_dv = 0
    p.xn_free = (p.DV, 0)
    p.psD_free = (p.DV, 0)
    p.ln_psD_free_dv = 0
    p.last_trp = 0
    p.tok_wt_last = 0
    p.scr_free = (p.DV, 0)
    p.t_c4 = 0
    p.big_free = (p.TE, 0)
    p.ring_free = (p.TE, 0)
    p.ring2_free = (p.TE, 0)
    p.ub_free = (p.TE, 0)
    p.psC_free = 0
    p.q2buf_free = []
    p.simbuf_free = []
    p.scrG_free_l = []
    p.gT_free_l = []
    p.tap_hold = []
    p.next_tok_xn = None

    with nc.Block() as blk:
        p.blk = blk
        emit_init(p)
        emit_ffn(p)
        emit_attn(p)
        emit_ffn(p)
        emit_store(p)
    ctx.close()
    return nc


def dma(eng, lane, out, in_, **kw):
    if isinstance(lane, LanePair):
        return lane.issue(eng, out, in_, **kw)
    eng.wait_ge(lane.h, lane.n)
    return lane.inc(eng.dma_start(out=out, in_=in_, **kw), 16)


def wdma(eng, token):
    lane, tok = token
    eng.wait_ge(lane.h, tok)


def wl(eng, lst):
    for h, v in lst:
        eng.wait_ge(h, v)


def tap(p, name, ap_src, cnt, tok):
    if name not in p.taps:
        return None
    blk = p.blk
    out = [None]

    @blk.sync
    def _(s):
        s.wait_ge(cnt.h, tok)
        out[0] = dma(s, p.DS, p.taps[name], ap_src)
    return out[0]


def emit_init(p):
    blk = p.blk

    @blk.sync
    def _(s):
        for c in range(CH):
            dma(s, p.DS, p.x[:, c, :], p.d_xt[128 * c:128 * (c + 1), :])
    p.tok_xin = p.DS.n

    @blk.gpsimd
    def _(g):
        dma(g, p.DW3, p.bias[:, 0:32], p.d_bgc)
        dma(g, p.DW3, p.bias[:, 32:64], p.d_buc)
        dma(g, p.DW3, p.bias[:, 64:72], p.d_bqc)
        dma(g, p.DW3, p.bias[:, 72:74], p.d_bkvc)
        tz = p.PO.inc(g.memset(p.keysbd[:], 0.0))
        g.wait_ge(p.PO.h, tz)
        for c in range(4):
            pp, hlo = c // 2, (c % 2) * 2
            dma(g, p.DW3, p.keysbd[0:64, c, 0:64], p.d_keyst[pp, hlo, :, :])
            dma(g, p.DW3, p.keysbd[64:128, c, 64:128], p.d_keyst[pp, hlo + 1, :, :])
    p.tok_const = p.DW3.n

    @blk.vector
    def _(v):
        p.DV.inc(v.memset(p.ones_r[:], 1.0))
        p.DV.inc(v.memset(p.ones_rf[:], 1.0))
        p.DV.inc(v.memset(p.ones_c[:], 1.0))
        p.DV.inc(v.memset(p.onesb_c[:], 1.0))
        p.DV.inc(v.memset(p.eps_c[:], float(EPS)))
    p.tok_ones = p.DV.n


def emit_ln(p, t0, scratch, scr_free, defer=False, bank=None):
    blk = p.blk
    psb = bank if bank is not None else p.psD
    xnf, psDf = p.xn_free, p.psD_free
    x_dv = p.x_dv
    hold0 = list(p.tap_hold)
    fins = []
    prev_stats = [0]
    tok_rm_l = [0, 0]
    norm_prev = [0]
    for b in range(TG // 512):
        tb = t0 + 512 * b
        base = 1024 + 1024 * b
        mb = p.simbuf[0:1, base:base + 512]
        varb = p.simbuf[0:1, base + 512:base + 1024]
        r = varb
        mr = mb
        te0, ac0, dv0 = p.TE.n, p.AC.n, p.DV.n
        tok_sq = [ac0 + c + 1 for c in range(CH)]
        tok_sd = ac0 + CH + 1
        tok_sumx = te0 + 1
        tok_smm = [te0 + 1 + c + 1 for c in range(CH)]
        tok_stats = dv0 + 3
        tok_rm = dv0 + 5
        tok_rm_l[b] = tok_rm

        @blk.scalar
        def _(s, tb=tb, b=b, r=r, varb=varb, tok_sq=tok_sq, tok_smm=tok_smm,
              tok_stats=tok_stats, tok_sd=tok_sd):
            s.wait_ge(p.DS.h, p.tok_xin)
            s.wait_ge(p.DV.h, x_dv)
            if b == 0:
                s.wait_ge(scr_free[0].h, scr_free[1])
                wl(s, hold0)
            for c in range(CH):
                if c >= 2:
                    s.wait_ge(p.TE.h, tok_smm[c - 2])
                a = p.AC.inc(s.activation(scratch[:, 512 * (c % 2):512 * (c % 2) + 512],
                                          p.x[:, c, tb:tb + 512], AF.Square))
                assert a == tok_sq[c]
            s.wait_ge(p.DV.h, tok_stats)
            assert p.AC.inc(s.activation(r, varb, AF.Sqrt, bias=p.eps_c[0:1, :])) == tok_sd

        @blk.tensor
        def _(t, tb=tb, b=b, tok_sq=tok_sq, tok_smm=tok_smm, tok_sumx=tok_sumx,
              pstats=prev_stats[0]):
            t.wait_ge(p.DS.h, p.tok_xin)
            t.wait_ge(p.DV.h, max(x_dv, p.tok_ones, p.ln_psD_free_dv, pstats))
            if b == 0:
                t.wait_ge(psDf[0].h, psDf[1])
            last = None
            for c in range(CH):
                last = t.matmul(psb[0:1, 0:512], p.ones_c[:], p.x[:, c, tb:tb + 512],
                                start=(c == 0), stop=(c == CH - 1))
            assert p.TE.inc(last) == tok_sumx
            for c in range(CH):
                t.wait_ge(p.AC.h, tok_sq[c])
                mm = t.matmul(psb[32:33, 0:512], p.ones_c[:],
                              scratch[:, 512 * (c % 2):512 * (c % 2) + 512],
                              start=(c == 0), stop=(c == CH - 1))
                assert p.TE.inc(mm) == tok_smm[c]

        @blk.vector
        def _(v, b=b, mb=mb, varb=varb, r=r, mr=mr, tok_smm=tok_smm,
              tok_stats=tok_stats, tok_sd=tok_sd, tok_rm=tok_rm):
            v.wait_ge(p.TE.h, tok_smm[CH - 1])
            t1 = p.DV.inc(v.tensor_scalar_mul(mb, psb[0:1, 0:512], 1.0 / D))
            v.wait_ge(p.DV.h, t1)
            t2 = p.DV.inc(v.tensor_mul(varb, mb, mb))
            v.wait_ge(p.DV.h, t2)
            t3 = p.DV.inc(v.scalar_tensor_tensor(varb, psb[32:33, 0:512], 1.0 / D, varb,
                                                 ALU.mult, ALU.subtract))
            assert t3 == tok_stats
            v.wait_ge(p.AC.h, tok_sd)
            t4 = p.DV.inc(v.reciprocal(r, r))
            v.wait_ge(p.DV.h, t4)
            t5 = p.DV.inc(v.tensor_mul(mr, mb, r))
            assert t5 == tok_rm
        prev_stats[0] = tok_stats

        def fin(b=b, tb=tb, r=r, mr=mr, tok_rm=tok_rm):
            t_bc = [0]

            @blk.tensor
            def _(t):
                t.wait_ge(p.DV.h, max(tok_rm, tok_rm_l[1], norm_prev[0]))
                t.matmul(psb[:, 0:512], p.ones_rf[:], r, start=True, stop=True)
                t_bc[0] = p.TE.inc(t.matmul(psb[:, 512:1024], p.ones_rf[:], mr,
                                            start=True, stop=True))

            @blk.vector
            def _(v):
                v.wait_ge(p.TE.h, t_bc[0])
                v.wait_ge(xnf[0].h, xnf[1])
                if b == 0:
                    wl(v, hold0)
                for c in range(CH):
                    ta = p.DV.inc(v.tensor_mul(scratch[:, 1024:1536], p.x[:, c, tb:tb + 512],
                                               psb[:, 0:512]))
                    v.wait_ge(p.DV.h, ta)
                    t6 = p.DV.inc(v.tensor_sub(p.xn[:, c, 512 * b:512 * (b + 1)],
                                               scratch[:, 1024:1536], psb[:, 512:1024]))
                    v.wait_ge(p.DV.h, t6)
            norm_prev[0] = p.DV.n
            return p.DV.n
        fins.append(fin)

    def finish():
        tok = 0
        for f in fins:
            tok = f()
        p.ln_psD_free_dv = tok
        p.psD_free = (p.DV, tok)
        return tok
    if defer:
        return finish
    return finish()


FFN_CALL = [0]


def emit_ffn(p):
    call = FFN_CALL[0]
    FFN_CALL[0] += 1
    for g in range(2):
        if p.next_tok_xn is not None:
            tok_xn = p.next_tok_xn
            p.next_tok_xn = None
        else:
            with p.nc.named_scope(f"f{call}g{g}_ln"):
                tok_xn = emit_ln(p, TG * g, p.sil, p.scr_free)
        tk = tap(p, f"xn{call}_{g}", p.xn[:].rearrange("p c t -> p (c t)"), p.DV, tok_xn)
        if tk:
            p.tap_hold.append((p.DS.h, tk))
        if g == 0:
            def nl():
                with p.nc.named_scope(f"f{call}g1_ln"):
                    return emit_ln(p, TG, p.sil, p.scr_free, defer=True)
        elif call == 0:
            def nl():
                with p.nc.named_scope("fa0_ln"):
                    return emit_ln(p, 0, p.sil, p.scr_free, defer=True)
        else:
            nl = None
        emit_group(p, g, tok_xn, nl)


def emit_group(p, g, tok_xn, next_ln=None):
    blk = p.blk
    scope = p.nc.named_scope(f"c{FFN_CALL[0]}g{g}_grp")
    scope.__enter__()
    hold0 = list(p.tap_hold)

    te0, ac0, dv0 = p.TE.n, p.AC.n, p.DV.n
    tok_q2mm = [te0 + oc + 1 for oc in range(4)]
    tok_sim = [te0 + 4 + tt + 1 for tt in range(NTL)]
    tok_Gmm = [te0 + 12 + 2 * c + 1 for c in range(32)]
    tok_Umm = [te0 + 12 + 2 * c + 2 for c in range(32)]
    tok_dmid = [te0 + 76 + 2 * do + 1 for do in range(8)]
    tok_dmm = [te0 + 76 + 2 * do + 2 for do in range(8)]
    tok_hdm = [te0 + 92 + ec + 1 for ec in range(32)]
    tok_q2cp = [ac0 + oc + 1 for oc in range(4)]
    tok_scp = [ac0 + 4 + tt + 1 for tt in range(NTL)]
    t_exn = ac0 + 14
    t_e8 = ac0 + 15
    tok_sil = [ac0 + 15 + 2 * c + 1 for c in range(32)]
    tok_ucp = [ac0 + 15 + 2 * c + 2 for c in range(32)]
    tok_hds = [ac0 + 79 + ec + 1 for ec in range(32)]
    po0 = p.PO.n
    tok_Ap = [po0 + j + 1 for j in range(16)]
    t_max_last = dv0 + 64
    t_mask = dv0 + 65
    t_fneg = dv0 + 66
    assert p.DG.i % 2 == 0
    dg0 = [p.DG.lanes[0].n, p.DG.lanes[1].n]
    tok_st = [(p.DG.lanes[tt % 2], dg0[tt % 2] + 16 * (tt // 2 + 1)) for tt in range(NTL)]

    wgv4 = p.wr1[:].rearrange("p (b d n) -> p b d n", b=4, d=CH)
    wqv = p.wr2[:].rearrange("p (d n) -> p d n", d=CH)
    wdv = p.wr1[:].rearrange("p (b c n) -> p b c n", b=2, c=16)
    dtv = p.wr1[:].rearrange("p (b d n) -> p b d n", b=4, d=CH)
    wg_r = p.d_wg.rearrange("(d p) n -> p d n", p=128)
    wu_r = p.d_wu.rearrange("(d p) n -> p d n", p=128)
    wqf_r = p.d_wqf.rearrange("(d p) n -> p d n", p=128)
    wd_r = p.d_wd.rearrange("(c p) n -> p c n", p=128)
    dt_r = p.d_dt.rearrange("(d p) n -> p d n", p=128)
    sbb = p.simbuf[:].bitcast(BF16)
    fnall = sbb[:, 0:4096]
    gacc2 = sbb[:, 4096:8192]
    tmp = p.gT[:].rearrange("p c t -> p (c t)")
    Z, alpha, thresh = p.zal[:, 0:4], p.zal[:, 4:8], p.zal[:, 8:12]
    qc = p.cmpb[:, 0:256].rearrange("p (h k) -> p h k", h=NH)
    qS = p.cmpb[:, 256:512].rearrange("p (h k) -> p h k", h=NH)
    qtmp = p.cmpb[:, 512:576]
    fxs = p.cmpb[:, 576:832].rearrange("p (h n) -> p h n", h=NH)
    e8xs = p.cmpb[:, 832:864].rearrange("p (h k) -> p h k", h=NH)

    t_wg = [None] * 32
    t_wu = [None] * 32
    t_wd = [None] * 16
    t_dts = [None] * 32
    t_wqf = [None, None]

    @blk.gpsimd
    def _(gg):
        gg.wait_ge(p.ring_free[0].h, p.ring_free[1])
        for c in range(2):
            t_wg[c] = dma(gg, p.DW1, wgv4[:, c % 2, :, :], wg_r[:, :, 128 * c:128 * (c + 1)])
            t_wu[c] = dma(gg, p.DW1, wgv4[:, 2 + c % 2, :, :],
                          wu_r[:, :, 128 * c:128 * (c + 1)])
        gg.wait_ge(p.ub_free[0].h, p.ub_free[1])
        p.t_bdr = dma(gg, p.DW3, p.bdr, p.d_bdr)
        gg.wait_ge(p.ring2_free[0].h, p.ring2_free[1])
        t_wqf[0] = dma(gg, p.DW2, wqv[:, :, :], wqf_r[:, :, 0:256])
        gg.wait_ge(p.TE.h, tok_q2mm[1])
        t_wqf[1] = dma(gg, p.DW2, wqv[:, :, :], wqf_r[:, :, 256:512])

        def a_pair(j):
            gg.wait_ge(p.AC.h, tok_ucp[2 * j + 1])
            a = p.PO.inc(gg.tensor_tensor(
                p.big[:, 2 * j:2 * j + 2, :], p.big[:, 2 * j:2 * j + 2, :],
                p.silb[:, (2 * j) % 4:(2 * j) % 4 + 2, :], ALU.mult))
            assert a == tok_Ap[j]

        for c in range(2, 32):
            gg.wait_ge(p.TE.h, tok_Umm[c - 2])
            t_wg[c] = dma(gg, p.DW1, wgv4[:, c % 2, :, :], wg_r[:, :, 128 * c:128 * (c + 1)])
            t_wu[c] = dma(gg, p.DW1, wgv4[:, 2 + c % 2, :, :],
                          wu_r[:, :, 128 * c:128 * (c + 1)])
            if c % 2 == 1:
                a_pair((c - 3) // 2)
        gg.wait_ge(p.TE.h, tok_Umm[31])
        t_wd[0] = dma(gg, p.DW1, wdv[:, 0, :, :], wd_r[:, 0:16, 0:128])
        t_wd[1] = dma(gg, p.DW1, wdv[:, 1, :, :], wd_r[:, 16:32, 0:128])
        a_pair(15)
        for sl in range(2, 16):
            do, hf = sl // 2, sl % 2
            gg.wait_ge(p.TE.h, tok_dmid[do - 1] if hf == 0 else tok_dmm[do - 1])
            t_wd[sl] = dma(gg, p.DW1, wdv[:, sl % 2, :, :],
                           wd_r[:, 16 * hf:16 * (hf + 1), 128 * do:128 * (do + 1)])
        for ec in range(32):
            gg.wait_ge(p.TE.h, tok_dmm[7] if ec < 4 else tok_hdm[ec - 4])
            t_dts[ec] = dma(gg, p.DW1, dtv[:, ec % 4, :, :],
                            dt_r[:, :, 128 * ec:128 * (ec + 1)])

    tok_ev = [None] * 8
    t_G = [None] * NTL
    sg = [p.scrG[:], gacc2]

    def emit_p1b_ev(v, do):
        v.wait_ge(p.TE.h, tok_dmm[do])
        psO = [p.psA, p.psB, p.psC, p.psD][do % 4]
        tok_ev[do] = p.DV.inc(v.scalar_tensor_tensor(
            p.x[:, do, TG * g:TG * (g + 1)], psO[:], 0.5,
            p.x[:, do, TG * g:TG * (g + 1)], ALU.mult, ALU.add))

    def tile_units(tt):
        acc = sg[tt % 2]
        fn_ = fnall[:, 512 * tt:512 * (tt + 1)]
        fnx = fn_[:, 0:256].rearrange("p (h n) -> p h n", h=NH)
        fny = fn_[:, 256:512].rearrange("p (h n) -> p h n", h=NH)
        e8x = p.e8all[:, tt, 0:32].rearrange("p (h k) -> p h k", h=NH)
        e8y = p.e8all[:, tt, 32:64].rearrange("p (h k) -> p h k", h=NH)

        def c1(v):
            v.wait_ge(p.AC.h, t_e8)
            t = p.DV.inc(v.tensor_tensor(qc, e8x.unsqueeze(3).to_broadcast([128, NH, 8, 8]),
                                         e8y.unsqueeze(2).to_broadcast([128, NH, 8, 8]),
                                         ALU.mult))
            v.wait_ge(p.DV.h, t)
            for h in range(NH):
                p.DV.inc(v.max(out=p.q8[:, h, :], in_=qc[:, h, :]))
            v.wait_ge(p.DV.h, p.DV.n)
            t = p.DV.inc(v.reduce_sum(Z, p.q8[:], axis=AX.X))
            v.wait_ge(p.DV.h, t)
            t = p.DV.inc(v.reciprocal(alpha, Z))
            v.wait_ge(p.DV.h, t)
            t = p.DV.inc(v.tensor_scalar_mul(alpha, alpha, 0.5))
            v.wait_ge(p.DV.h, t)

        def c2(v):
            p.DV.inc(v.tensor_tensor(fxs, fnx,
                                     alpha.unsqueeze(2).to_broadcast([128, NH, 64]), ALU.mult))
            t = p.DV.inc(v.tensor_tensor(e8xs, e8x,
                                         alpha.unsqueeze(2).to_broadcast([128, NH, 8]), ALU.mult))
            v.wait_ge(p.DV.h, t)
            t = p.DV.inc(v.tensor_tensor(qS, e8xs.unsqueeze(3).to_broadcast([128, NH, 8, 8]),
                                         e8y.unsqueeze(2).to_broadcast([128, NH, 8, 8]), ALU.mult))
            v.wait_ge(p.DV.h, t)

        def c3(v):
            for h in range(NH):
                p.DV.inc(v.max(out=p.q8S[:, h, :], in_=qS[:, h, :]))
                v.wait_ge(p.DV.h, p.DV.n)
                p.DV.inc(v.match_replace(out=qtmp, in_to_replace=p.q8S[:, h, :],
                                         in_values=qS[:, h, :], imm_value=-1.0))
                v.wait_ge(p.DV.h, p.DV.n)
                p.DV.inc(v.max(out=p.q9S[:, h, :], in_=qtmp))
            v.wait_ge(p.DV.h, p.DV.n)
            t = p.DV.inc(v.tensor_tensor(thresh, p.q8S[:, :, 7], p.q9S[:, :, 0], ALU.add))
            v.wait_ge(p.DV.h, t)
            t = p.DV.inc(v.tensor_scalar_mul(thresh, thresh, 0.5))
            v.wait_ge(p.DV.h, t)

        def mk_b(h):
            def b_(v):
                dst = acc if h == 0 else tmp
                if h == 0 and tt >= 2:
                    wdma(v, tok_st[tt - 2])
                t = p.DV.inc(v.tensor_tensor(
                    dst.rearrange("p (a b) -> p a b", a=64),
                    fxs[:, h, :].unsqueeze(2).to_broadcast([128, 64, 64]),
                    fny[:, h, :].unsqueeze(1).to_broadcast([128, 64, 64]), ALU.mult))
                v.wait_ge(p.DV.h, t)
            return b_

        def mk_s(h):
            def s_(v):
                dst = acc if h == 0 else tmp
                t = p.DV.inc(v.scalar_tensor_tensor(dst, dst, thresh[:, h:h + 1], dst,
                                                    ALU.is_ge, ALU.mult))
                v.wait_ge(p.DV.h, t)
            return s_

        def mk_a():
            def a_(v):
                t = p.DV.inc(v.tensor_add(acc, acc, tmp))
                v.wait_ge(p.DV.h, t)
            return a_

        def fin(v):
            t_G[tt] = p.DV.n

        us = [(2.2, c1), (1.1, c2), (2.8, c3)]
        for h in range(NH):
            us.append((4.4, mk_b(h)))
            us.append((4.5, mk_s(h)))
            if h > 0:
                us.append((2.3, mk_a()))
        us.append((0.0, fin))
        return us

    units = []
    for tt in range(NTL):
        units += tile_units(tt)
    inserts = [(190.0 + 15.7 * do, 1, do) for do in range(8)]

    @blk.vector
    def _(v):
        wl(v, hold0)
        for tt in range(NTL):
            v.wait_ge(p.AC.h, tok_scp[tt])
            for ph in range(8):
                p.DV.inc(v.max(out=p.s1v8[:, tt, ph, :],
                               in_=p.simbuf[:, 512 * tt + 64 * ph:512 * tt + 64 * (ph + 1)]))
        assert p.DV.n == t_max_last
        wl(v, p.scrG_free_l)
        v.wait_ge(p.DV.h, t_max_last)
        a = p.DV.inc(v.tensor_tensor(
            p.scrG[:].rearrange("p (t h n) -> p t h n", t=NTL, h=8),
            p.simbuf[:].rearrange("p (t h n) -> p t h n", t=NTL, h=8),
            p.s1v8[:, :, :, 7:8].to_broadcast([128, NTL, 8, 64]), ALU.is_ge))
        assert a == t_mask
        v.wait_ge(p.DV.h, t_mask)
        v.wait_ge(p.AC.h, t_exn)
        a = p.DV.inc(v.tensor_mul(fnall, p.scrG[:], tmp))
        assert a == t_fneg
        v.wait_ge(p.DV.h, t_fneg)
        clock = 0.0
        ii = 0
        for dur, fn in units:
            while ii < len(inserts) and inserts[ii][0] <= clock:
                emit_p1b_ev(v, inserts[ii][2])
                clock += 1.2
                ii += 1
            fn(v)
            clock += dur
        while ii < len(inserts):
            emit_p1b_ev(v, inserts[ii][2])
            ii += 1

    @blk.sync
    def _(s):
        for tt in range(NTL):
            s.wait_ge(p.DV.h, t_G[tt])
            got = dma(s, p.DG, p.d_gd[g % 2, tt], sg[tt % 2])
            assert got == tok_st[tt]

    for tt in range(NTL):
        gname = f"G{FFN_CALL[0]}_{g}_{tt}"
        if gname in p.taps:
            @blk.sync
            def _(s, tt=tt, gname=gname):
                wdma(s, tok_st[tt])
                tk = dma(s, p.DS, p.taps[gname], p.d_gd[g % 2, tt])
                p.tap_hold.append((p.DS.h, tk))
    tokA_tap = tap(p, f"A{FFN_CALL[0]}_{g}", p.big[:].rearrange("p c t -> p (c t)"),
                   p.PO, tok_Ap[15])

    @blk.tensor
    def _(t):
        t.wait_ge(p.DV.h, max(tok_xn, p.tok_ones, p.psC_free))
        t.wait_ge(p.DW3.h, p.tok_const)
        for oc in range(4):
            pair, k = oc // 2, oc % 2
            if k == 0:
                wdma(t, t_wqf[pair])
            psQ = p.psC if oc % 2 == 0 else p.psD
            if oc >= 2:
                t.wait_ge(p.AC.h, tok_q2cp[oc - 2])
            last = None
            for dd in range(CH):
                for hh in range(2):
                    last = t.matmul(psQ[:, 512 * hh:512 * (hh + 1)],
                                    wqv[:, dd, 128 * k:128 * (k + 1)],
                                    p.xn[:, dd, 512 * hh:512 * (hh + 1)],
                                    start=(dd == 0), stop=(dd == CH - 1))
            assert p.TE.inc(last) == tok_q2mm[oc]
        for tt in range(NTL):
            t.wait_ge(p.AC.h, tok_q2cp[3] if tt < 2 else tok_scp[tt - 2])
            off = 512 * (tt % 2)
            last = None
            for c in range(4):
                last = t.matmul(p.psD[:, off + 128 * c:off + 128 * (c + 1)],
                                p.q2[:, c, 128 * tt:128 * (tt + 1)],
                                p.keysbd[:, c, :], start=True, stop=True)
            assert p.TE.inc(last) == tok_sim[tt]
        for c in range(32):
            wdma(t, t_wu[c])
            wdma(t, t_wg[c])
            if c == 0:
                t.wait_ge(p.DV.h, p.x_dv)
            if c == 1:
                t.wait_ge(p.AC.h, tok_scp[7])
            psG = p.psA if c % 2 == 0 else p.psC
            psU = p.psB if c % 2 == 0 else p.psD
            if c >= 2:
                t.wait_ge(p.AC.h, tok_ucp[c - 2])
            last = None
            for dd in range(CH):
                for hh in range(2):
                    last = t.matmul(psG[:, 512 * hh:512 * (hh + 1)],
                                    wgv4[:, c % 2, dd, :],
                                    p.xn[:, dd, 512 * hh:512 * (hh + 1)],
                                    start=(dd == 0), stop=(dd == CH - 1))
            assert p.TE.inc(last) == tok_Gmm[c]
            for dd in range(CH):
                for hh in range(2):
                    last = t.matmul(psU[:, 512 * hh:512 * (hh + 1)],
                                    wgv4[:, 2 + c % 2, dd, :],
                                    p.xn[:, dd, 512 * hh:512 * (hh + 1)],
                                    start=(dd == 0), stop=(dd == CH - 1))
            assert p.TE.inc(last) == tok_Umm[c]
        t.wait_ge(p.PO.h, tok_Ap[15])
        t.wait_ge(p.DW3.h, p.t_bdr)
        psOs = [p.psA, p.psB, p.psC, p.psD]
        for do in range(8):
            psO = psOs[do % 4]
            wdma(t, t_wd[2 * do])
            if do >= 4:
                t.wait_ge(p.DV.h, tok_ev[do - 4])
            last = None
            for c in range(16):
                for hh in range(2):
                    last = t.matmul(psO[:, 512 * hh:512 * (hh + 1)],
                                    wdv[:, 0, c, :],
                                    p.big[:, c, 512 * hh:512 * (hh + 1)],
                                    start=(c == 0), stop=False)
            assert p.TE.inc(last) == tok_dmid[do]
            wdma(t, t_wd[2 * do + 1])
            for c in range(16, 32):
                for hh in range(2):
                    t.matmul(psO[:, 512 * hh:512 * (hh + 1)],
                             wdv[:, 1, c % 16, :],
                             p.big[:, c, 512 * hh:512 * (hh + 1)],
                             start=False, stop=False)
            last = None
            for hh in range(2):
                last = t.matmul(psO[:, 512 * hh:512 * (hh + 1)],
                                p.bdr[0:1, 128 * do:128 * (do + 1)], p.ones_r[:],
                                start=False, stop=True)
            assert p.TE.inc(last) == tok_dmm[do]
        t.wait_ge(p.DV.h, tok_ev[5])
        for ec in range(32):
            wdma(t, t_dts[ec])
            psH = p.psA if ec % 2 == 0 else p.psB
            if ec >= 2:
                t.wait_ge(p.AC.h, tok_hds[ec - 2])
            last = None
            for dd in range(CH):
                for hh in range(2):
                    last = t.matmul(psH[:, 512 * hh:512 * (hh + 1)], dtv[:, ec % 4, dd, :],
                                    p.xn[:, dd, 512 * hh:512 * (hh + 1)],
                                    start=(dd == 0), stop=(dd == CH - 1))
            assert p.TE.inc(last) == tok_hdm[ec]

    @blk.scalar
    def _(s):
        wl(s, p.q2buf_free)
        wl(s, p.simbuf_free)
        wl(s, hold0)
        s.wait_ge(p.DW3.h, p.tok_const)
        for oc in range(4):
            psQ = p.psC if oc % 2 == 0 else p.psD
            s.wait_ge(p.TE.h, tok_q2mm[oc])
            a = p.AC.inc(s.activation(p.q2[:, oc, :], psQ[:], AF.Copy))
            assert a == tok_q2cp[oc]
        for tt in range(NTL):
            off = 512 * (tt % 2)
            s.wait_ge(p.TE.h, tok_sim[tt])
            a = p.AC.inc(s.activation(p.simbuf[:, 512 * tt:512 * (tt + 1)],
                                      p.psD[:, off:off + 512], AF.Copy))
            assert a == tok_scp[tt]
        s.wait_ge(p.AC.h, tok_scp[7])
        wl(s, p.gT_free_l)
        p.AC.inc(s.activation(tmp[:, 0:2048], p.simbuf[:, 0:2048], AF.Exp))
        a = p.AC.inc(s.activation(tmp[:, 2048:4096], p.simbuf[:, 2048:4096], AF.Exp))
        assert a == t_exn
        s.wait_ge(p.DV.h, t_max_last)
        a = p.AC.inc(s.activation(p.e8all[:].rearrange("p a b -> p (a b)"),
                                  p.s1v8[:].rearrange("p a b c -> p (a b c)"), AF.Exp))
        assert a == t_e8
        for c in range(32):
            psG = p.psA if c % 2 == 0 else p.psC
            psU = p.psB if c % 2 == 0 else p.psD
            s.wait_ge(p.TE.h, tok_Gmm[c])
            if c >= 4:
                s.wait_ge(p.PO.h, tok_Ap[(c - 4) // 2])
            a = p.AC.inc(s.activation(p.silb[:, c % 4, :],
                                      psG[:], AF.Silu, bias=p.bias[:, c:c + 1]))
            assert a == tok_sil[c]
            s.wait_ge(p.TE.h, tok_Umm[c])
            a = p.AC.inc(s.activation(p.big[:, c, :], psU[:], AF.Identity,
                                      bias=p.bias[:, 32 + c:33 + c]))
            assert a == tok_ucp[c]
        if tokA_tap:
            s.wait_ge(p.DS.h, tokA_tap)
        for ec in range(32):
            psH = p.psA if ec % 2 == 0 else p.psB
            s.wait_ge(p.TE.h, tok_hdm[ec])
            a = p.AC.inc(s.activation(p.big[:, ec, :], psH[:], AF.Silu))
            assert a == tok_hds[ec]

    dests = [p.gT[:], p.scrG[:].rearrange("p (c t) -> p c t", c=32),
             gacc2.rearrange("p (c t) -> p c t", c=32)]
    trp_tok = [None] * NTL
    tok_wt = [None] * NTL
    for tt in range(NTL):
        @blk.sync
        def _(s, tt=tt):
            wdma(s, tok_st[tt])
            if tt == 0:
                s.wait_ge(p.DV.h, t_G[NTL - 1])
            elif tt == 1:
                wdma(s, tok_st[6])
            elif tt == 2:
                wdma(s, tok_st[7])
            else:
                s.wait_ge(p.DV.h, tok_wt[tt - 3])
            s.wait_ge(p.DT.h, p.DT.n)
            trp_tok[tt] = p.DT.inc(s.dma_start_transpose(dests[tt % 3], p.d_gd[g % 2, tt]), 16)

        @blk.vector
        def _(v, tt=tt):
            v.wait_ge(p.AC.h, tok_hds[31])
            v.wait_ge(p.DT.h, trp_tok[tt])
            sl = p.big[:, :, 128 * tt:128 * (tt + 1)]
            tok_wt[tt] = p.DV.inc(v.tensor_mul(sl, sl, dests[tt % 3]))

    tk = tap(p, f"WT{FFN_CALL[0]}_{g}", p.big[:].rearrange("p c t -> p (c t)"),
             p.DV, tok_wt[NTL - 1])
    if tk:
        p.tap_hold.append((p.DS.h, tk))

    p.psD_free = (p.DV, tok_ev[7])
    p.scr_free = (p.PO, tok_Ap[15])
    p.xn_free = (p.TE, tok_hdm[31])
    ln_fin = next_ln() if next_ln is not None else None

    t_p2b0 = p.TE.n
    assert t_p2b0 == te0 + 124 + (18 if next_ln is not None else 0)
    bump = 2 if next_ln is not None else 0
    tok_ueq = {(do, qq): t_p2b0 + 4 * do + qq + 1 + (bump if do >= 2 else 0)
               for do in range(8) for qq in range(4)}
    tok_umm = [tok_ueq[(do, 3)] for do in range(8)]

    uev = p.wr1[:].rearrange("p (b c n) -> p b c n", b=4, c=8)
    ue_r = p.d_ue.rearrange("(c p) n -> p c n", p=128)
    tok_ue = {}
    tok_uev = [0] * 8
    for do in range(8):
        psO = [p.psA, p.psB, p.psC][do % 3]
        for qq in range(4):
            j = 4 * do + qq

            @blk.gpsimd
            def _(gg, do=do, qq=qq, j=j):
                if j < 4:
                    gg.wait_ge(p.TE.h, tok_hdm[28 + j])
                else:
                    pj = j - 4
                    gg.wait_ge(p.TE.h, tok_ueq[(pj // 4, pj % 4)])
                tok_ue[(do, qq)] = dma(gg, p.DW2, uev[:, j % 4, :, :],
                                       ue_r[:, 8 * qq:8 * (qq + 1), 128 * do:128 * (do + 1)])

            @blk.tensor
            def _(t, do=do, qq=qq, j=j, psO=psO):
                wdma(t, tok_ue[(do, qq)])
                if qq == 0:
                    t.wait_ge(p.DV.h, tok_wt[NTL - 1])
                    if do >= 3:
                        t.wait_ge(p.DV.h, tok_uev[do - 3])
                last = None
                for cc in range(8):
                    c = 8 * qq + cc
                    for hh in range(2):
                        last = t.matmul(psO[:, 512 * hh:512 * (hh + 1)],
                                        uev[:, j % 4, cc, :],
                                        p.big[:, c, 512 * hh:512 * (hh + 1)],
                                        start=(c == 0), stop=(c == 31))
                a = p.TE.inc(last)
                assert a == tok_ueq[(do, qq)]

        @blk.vector
        def _(v, do=do, g=g, psO=psO):
            v.wait_ge(p.TE.h, tok_umm[do])
            tok_uev[do] = p.DV.inc(v.tensor_tensor(
                p.x[:, do, TG * g:TG * (g + 1)], psO[:],
                p.x[:, do, TG * g:TG * (g + 1)], ALU.add))

        if do == 1 and ln_fin is not None:
            p.next_tok_xn = ln_fin()
            ln_fin = None

    if ln_fin is not None:
        p.next_tok_xn = ln_fin()
    p.x_dv = tok_uev[7]
    p.psC_free = tok_uev[5]
    p.big_free = (p.TE, tok_umm[7])
    p.ring_free = (p.TE, tok_umm[7])
    p.ring2_free = (p.TE, tok_q2mm[3])
    p.ub_free = (p.TE, tok_dmm[7])
    p.p2b_last_te = tok_umm[7]
    p.tok_wt_last = tok_wt[NTL - 1]
    p.last_trp = trp_tok[NTL - 1]
    p.q2buf_free = [(p.TE.h, tok_umm[7]), (p.DV.h, tok_wt[NTL - 1])]
    p.simbuf_free = [(p.DV.h, tok_wt[NTL - 1]), (tok_st[7][0].h, tok_st[7][1])]
    p.scrG_free_l = [(p.DV.h, tok_wt[NTL - 1]), (tok_st[6][0].h, tok_st[6][1])]
    p.gT_free_l = [(p.DV.h, tok_wt[NTL - 1])]
    scope.__exit__(None, None, None)


def emit_attn(p):
    blk = p.blk
    scope = p.nc.named_scope("attn_proj")
    scope.__enter__()

    wkvv = p.q2.rearrange("p o t -> p (o t)")[:, 0:2048].rearrange("p (d n) -> p d n", d=CH)
    wkv_r = p.d_wkv.rearrange("(d p) n -> p d n", p=128)
    wq_r = p.d_wq.rearrange("(d p) n -> p d n", p=128)
    wqh = p.wr1[:, 0:2048].rearrange("p (b d n) -> p b d n", b=2, d=CH)
    t_wkv = [0]

    @blk.gpsimd
    def _(gg):
        gg.wait_ge(p.TE.h, p.p2b_last_te)
        gg.wait_ge(p.DV.h, p.x_dv)
        wl(gg, p.simbuf_free)
        wl(gg, p.tap_hold)
        p.t_pe = dma(gg, p.DW3, p.pe, p.d_pet)
        p.t_ident = dma(gg, p.DW3, p.ident, p.d_ident)
        p.t_stair = dma(gg, p.DW3, p.stair.rearrange("p a b -> p (a b)"), p.d_stair)
        t_wkv[0] = dma(gg, p.DW3, wkvv, wkv_r)

    kk = p.scrP[:, 0:2048]
    vv = p.scrP[:, 2048:4096]
    psT = p.psB[:].bitcast(BF16)
    tok_vt = [0] * 16

    def emit_vt(c):
        t_vtm = [0]

        @blk.tensor
        def _(t, c=c):
            t.wait_ge(p.DV.h, tok_kv[1])
            t.wait_ge(p.DW3.h, p.t_ident)
            if c > 0:
                t.wait_ge(p.DV.h, tok_vt[c - 1])
            t_vtm[0] = p.TE.inc(t.transpose(psT[:, 0:128], vv[:, 128 * c:128 * (c + 1)],
                                            p.ident[:]))

        @blk.vector
        def _(v, c=c):
            v.wait_ge(p.TE.h, t_vtm[0])
            if c == 0:
                v.wait_ge(p.DV.h, p.tok_wt_last)
            tok_vt[c] = p.DV.inc(v.tensor_copy(p.gT[:, c, :], psT[:, 0:128]))
    ln_scr = p.big[:, 24:27, :].rearrange("p a b -> p (a b)").bitcast(F32)
    tok_kv = [0, 0]
    t_kv_k = [0, 0]
    tok_qh = [[0] * 2 for _ in range(NHEAD)]
    t_qev = [0] * NHEAD
    t_kvev = [0]
    t_qmm = [0]
    p.t_wqh = [0] * 16
    p.t_qmm_j = [0] * 16
    for half in range(2):
        if half == 0 and p.next_tok_xn is not None:
            tok_xn = p.next_tok_xn
            p.next_tok_xn = None
        elif half == 0:
            tok_xn = emit_ln(p, TG * half, ln_scr, (p.TE, p.p2b_last_te))
        else:
            p.psD_free = (p.DV, t_kv_k[0])
            tok_xn = emit_ln(p, TG * half, ln_scr, (p.TE, p.p2b_last_te),
                             bank=p.psA)
        for j in range(2):
            psK = p.psA if j == 0 else p.psB

            @blk.tensor
            def _(t, j=j, half=half, psK=psK, tok_xn=tok_xn):
                t.wait_ge(p.DW3.h, t_wkv[0])
                t.wait_ge(p.DV.h, tok_xn)
                if half == 1:
                    t.wait_ge(p.DV.h, t_kvev[0])
                else:
                    t.wait_ge(p.DV.h, p.x_dv)
                last = None
                for dd in range(CH):
                    for hh in range(2):
                        last = t.matmul(psK[:, 512 * hh:512 * (hh + 1)],
                                        wkvv[:, dd, 128 * j:128 * (j + 1)],
                                        p.xn[:, dd, 512 * hh:512 * (hh + 1)],
                                        start=(dd == 0), stop=(dd == CH - 1))
                p.t_kvmm = p.TE.inc(last)

            @blk.vector
            def _(v, j=j, half=half, psK=psK):
                v.wait_ge(p.TE.h, p.t_kvmm)
                v.wait_ge(p.DW3.h, p.t_pe)
                dst = (kk if j == 0 else vv)[:, TG * half:TG * (half + 1)]
                if j == 0:
                    tok_kv[half] = p.DV.inc(v.scalar_tensor_tensor(
                        dst, psK[:], p.bias[:, 72:73], p.pe[:, TG * half:TG * (half + 1)],
                        ALU.add, ALU.add))
                    t_kv_k[half] = p.DV.n
                else:
                    tok_kv[half] = p.DV.inc(v.tensor_scalar_add(dst, psK[:], p.bias[:, 73:74]))
                t_kvev[0] = p.DV.n

        for h in range(NHEAD):
            psQ = p.psC if h % 2 == 0 else p.psD
            j = 8 * half + h

            @blk.gpsimd
            def _(gg, h=h, half=half, j=j):
                if j < 2:
                    gg.wait_ge(p.TE.h, p.p2b_last_te)
                else:
                    gg.wait_ge(p.TE.h, p.t_qmm_j[j - 2])
                p.t_wqh[j] = dma(gg, p.DW1, wqh[:, j % 2, :, :],
                                 wq_r[:, :, 128 * h:128 * (h + 1)])

            @blk.tensor
            def _(t, h=h, half=half, j=j, psQ=psQ, tok_xn=tok_xn):
                wdma(t, p.t_wqh[j])
                t.wait_ge(p.DV.h, tok_xn)
                prev = t_qev[h - 2] if h >= 2 else (t_qev[NHEAD - 2 + h] if half == 1 else 0)
                t.wait_ge(p.DV.h, prev)
                if half == 0 and h < 2:
                    t.wait_ge(p.DV.h, p.ln_psD_free_dv)
                last = None
                for dd in range(CH):
                    for hh in range(2):
                        last = t.matmul(psQ[:, 512 * hh:512 * (hh + 1)], wqh[:, j % 2, dd, :],
                                        p.xn[:, dd, 512 * hh:512 * (hh + 1)],
                                        start=(dd == 0), stop=(dd == CH - 1))
                t_qmm[0] = p.TE.inc(last)
                p.t_qmm_j[j] = t_qmm[0]

            @blk.vector
            def _(v, h=h, half=half, psQ=psQ):
                v.wait_ge(p.TE.h, t_qmm[0])
                v.wait_ge(p.DW3.h, p.t_pe)
                if half == 0 and h == 0:
                    v.wait_ge(p.TE.h, p.p2b_last_te)
                tok_qh[h][half] = p.DV.inc(v.scalar_tensor_tensor(
                    p.big[:, 2 * h + half, :], psQ[:], p.bias[:, 64 + h:65 + h],
                    p.pe[:, TG * half:TG * (half + 1)], ALU.add, ALU.add))
                t_qev[h] = p.DV.n

            if half == 1:
                emit_vt(2 * h)
                emit_vt(2 * h + 1)
        p.xn_free = (p.TE, t_qmm[0])

    scope.__exit__(None, None, None)
    scope = p.nc.named_scope("attn_heads")
    scope.__enter__()
    Ev = p.scrG[:].rearrange("p (s t) -> p s t", s=8)
    zr = p.rows[0:1, 0:512]
    zbs = p.gT[:, 16:20, :].rearrange("p a b -> p (a b)")
    tok_att = {-2: 0, -1: 0}
    t_zr = {-2: 0, -1: 0}
    t_zbs = {-2: 0, -1: 0}
    pv_tok = {}
    pv_last = {}
    exp_prev = [0, 0]
    t_zb = {}

    for h in range(NHEAD):
        for Q in range(4):
            nkc = 4 * Q + 4
            idx = 4 * h + Q
            npair = nkc // 2
            psOt = p.psA if idx % 2 == 0 else p.psB
            psO = psOt[:, 0:512]
            zrow = psOt[0:1, 512:1024]
            teI, acI, dvI = p.TE.n, p.AC.n, p.DV.n
            tok_exp = [acI + (kc // 2) + 1 for kc in range(nkc)]
            t_mask = dvI + 1
            t_sm_toks = []
            qblk = p.big[:, 2 * h + Q // 2, 512 * (Q % 2):512 * (Q % 2) + 512]

            def pv(t, kc, first, last_flag, idx=idx, psO=psO, zrow=zrow, nkc=nkc,
                   tok_exp=tok_exp, t_mask=t_mask):
                slot = kc % 8
                if kc >= nkc - 4:
                    t.wait_ge(p.DV.h, t_mask)
                else:
                    t.wait_ge(p.AC.h, tok_exp[kc])
                if first:
                    t.wait_ge(p.DV.h, tok_vt[15])
                    t.wait_ge(p.DV.h, t_zr[idx - 2])
                    t.wait_ge(p.DV.h, tok_att[idx - 2])
                    if idx < 2:
                        t.wait_ge(p.DV.h, t_qev[NHEAD - 1])
                t.matmul(zrow, p.onesb_c[:], Ev[:, slot, :], start=first, stop=last_flag)
                mm = t.matmul(psO, p.gT[:, kc, :], Ev[:, slot, :], start=first, stop=last_flag)
                pv_tok[slot] = p.TE.inc(mm)
                if last_flag:
                    pv_last[idx] = pv_tok[slot]

            @blk.tensor
            def _(t, h=h, Q=Q, idx=idx, nkc=nkc, npair=npair, qblk=qblk, tok_exp=tok_exp):
                t.wait_ge(p.DV.h, tok_kv[1])
                t.wait_ge(p.DV.h, tok_qh[h][Q // 2])
                pend_pv = []
                for pr in range(npair):
                    kc0 = 2 * pr
                    psS = p.psC if pr % 2 == 0 else p.psD
                    if kc0 >= 4:
                        t.wait_ge(p.AC.h, tok_exp[kc0 - 4])
                    else:
                        t.wait_ge(p.AC.h, exp_prev[pr % 2])
                        if idx == 0:
                            t.wait_ge(p.DV.h, t_qev[NHEAD - 1])
                    if pr % 2 == 1:
                        t.wait_ge(p.DV.h, t_zbs[idx - 2])
                    mm = None
                    for j in range(2):
                        kc = kc0 + j
                        mm = t.matmul(psS[:, 512 * j:512 * (j + 1)],
                                      kk[:, 128 * kc:128 * (kc + 1)], qblk,
                                      start=True, stop=True)
                    t_sm_toks.append(p.TE.inc(mm))
                    if kc0 + 1 < nkc - 4:
                        pend_pv.append(kc0)
                        if len(pend_pv) > 1:
                            k0 = pend_pv.pop(0)
                            pv(t, k0, first=(k0 == 0), last_flag=False)
                            pv(t, k0 + 1, first=False, last_flag=False)
                for k0 in pend_pv:
                    pv(t, k0, first=(k0 == 0), last_flag=False)
                    pv(t, k0 + 1, first=False, last_flag=False)
                for kc in range(nkc - 4, nkc):
                    pv(t, kc, first=(kc == 0), last_flag=(kc == nkc - 1))
                if idx >= 1:
                    t.wait_ge(p.DV.h, t_zr[idx - 1])
                    t.wait_ge(p.DV.h, t_zbs[idx - 2])
                    t.wait_ge(p.AC.h, tok_exp[2 * (npair - 1)])
                    t_zb[idx - 1] = p.TE.inc(t.matmul(p.psD[:, 0:512], p.ones_rf[:], zr,
                                                      start=True, stop=True))

            @blk.scalar
            def _(s, h=h, Q=Q, idx=idx, nkc=nkc, npair=npair, tok_exp=tok_exp,
                  t_sm_toks=t_sm_toks):
                for pr in range(npair):
                    kc0 = 2 * pr
                    slot = kc0 % 8
                    psS = p.psC if pr % 2 == 0 else p.psD
                    s.wait_ge(p.TE.h, t_sm_toks[pr])
                    for sl in (slot, slot + 1):
                        if sl in pv_tok and pv_tok[sl] < teI:
                            s.wait_ge(p.TE.h, pv_tok[sl])
                    if idx == 0 and kc0 == 0:
                        s.wait_ge(p.DT.h, p.last_trp)
                        s.wait_ge(p.DV.h, p.tok_wt_last)
                    a = p.AC.inc(s.activation(Ev[:, slot:slot + 2, :], psS[:, 0:1024],
                                              AF.Exp, scale=SCALE))
                    assert a == tok_exp[kc0]

            s0 = (nkc - 4) % 8

            @blk.vector
            def _(v, idx=idx, s0=s0, nkc=nkc, tok_exp=tok_exp, psO=psO, zrow=zrow):
                v.wait_ge(p.AC.h, tok_exp[nkc - 1])
                v.wait_ge(p.DW3.h, p.t_stair)
                a = p.DV.inc(v.tensor_mul(Ev[:, s0:s0 + 4, :], Ev[:, s0:s0 + 4, :],
                                          p.stair[:]))
                assert a == t_mask
                if idx >= 1:
                    v.wait_ge(p.TE.h, t_zb[idx - 1])
                    t1 = p.DV.inc(v.tensor_copy(zbs[:, 0:512], p.psD[:, 0:512]))
                    v.wait_ge(p.DV.h, t1)
                    t_zbs[idx - 1] = t1
                    pidx = idx - 1
                    ph, pQ = pidx // 4, pidx % 4
                    ppsO = (p.psA if pidx % 2 == 0 else p.psB)[:, 0:512]
                    tok_att[pidx] = p.DV.inc(v.tensor_mul(
                        p.big[:, 16 + 2 * ph + pQ // 2, 512 * (pQ % 2):512 * (pQ % 2) + 512],
                        ppsO, zbs[:, 0:512]))
                v.wait_ge(p.TE.h, pv_last[idx])
                t_zr[idx] = p.DV.inc(v.reciprocal(zr, zrow))

            exp_prev[0] = tok_exp[2 * (npair - 2)]
            exp_prev[1] = tok_exp[2 * (npair - 1)]

    lidx = 4 * NHEAD - 1

    @blk.tensor
    def _(t):
        t.wait_ge(p.DV.h, t_zr[lidx])
        t.wait_ge(p.DV.h, t_zbs[lidx - 1])
        t.wait_ge(p.AC.h, exp_prev[1])
        t_zb[lidx] = p.TE.inc(t.matmul(p.psD[:, 0:512], p.ones_rf[:], zr, start=True, stop=True))

    @blk.vector
    def _(v):
        v.wait_ge(p.TE.h, t_zb[lidx])
        t1 = p.DV.inc(v.tensor_copy(zbs[:, 0:512], p.psD[:, 0:512]))
        v.wait_ge(p.DV.h, t1)
        ph, pQ = lidx // 4, lidx % 4
        ppsO = (p.psA if lidx % 2 == 0 else p.psB)[:, 0:512]
        tok_att[lidx] = p.DV.inc(v.tensor_mul(
            p.big[:, 16 + 2 * ph + pQ // 2, 512 * (pQ % 2):512 * (pQ % 2) + 512],
            ppsO, zbs[:, 0:512]))
    scope.__exit__(None, None, None)
    scope = p.nc.named_scope("attn_wo")
    scope.__enter__()
    wov = p.wr1[:, 0:2048].rearrange("p (b d n) -> p b d n", b=2, d=CH)
    wo_r = p.d_wo.rearrange("(d p) n -> p d n", p=128)
    tok_wo = [0] * 8
    tok_omm = [0] * 8
    tok_oev = [0] * 16
    for oc in range(8):
        @blk.gpsimd
        def _(gg, oc=oc):
            if oc < 2:
                gg.wait_ge(p.TE.h, p.t_qmm_j[15])
            else:
                gg.wait_ge(p.TE.h, tok_omm[oc - 2])
            tok_wo[oc] = dma(gg, p.DW1, wov[:, oc % 2, :, :],
                             wo_r[:, :, 128 * oc:128 * (oc + 1)])
        for blkk in range(2):
            psX = p.psC if (2 * oc + blkk) % 2 == 0 else p.psD
            t_omm = [0]

            @blk.tensor
            def _(t, oc=oc, blkk=blkk, psX=psX):
                wdma(t, tok_wo[oc])
                t.wait_ge(p.DV.h, tok_att[4 * NHEAD - 1])
                j = 2 * oc + blkk
                if j >= 2:
                    t.wait_ge(p.DV.h, tok_oev[j - 2])
                last = None
                for hh in range(CH):
                    for q2_ in range(2):
                        last = t.matmul(psX[:, 512 * q2_:512 * (q2_ + 1)], wov[:, oc % 2, hh, :],
                                        p.big[:, 16 + 2 * hh + blkk, 512 * q2_:512 * (q2_ + 1)],
                                        start=(hh == 0), stop=(hh == CH - 1))
                t_omm[0] = p.TE.inc(last)
                tok_omm[oc] = t_omm[0]

            @blk.vector
            def _(v, oc=oc, blkk=blkk, psX=psX):
                v.wait_ge(p.TE.h, t_omm[0])
                if 2 * oc + blkk == 0:
                    v.wait_ge(p.DV.h, tok_att[4 * NHEAD - 1])
                tok_oev[2 * oc + blkk] = p.DV.inc(v.tensor_tensor(
                    p.x[:, oc, TG * blkk:TG * (blkk + 1)], psX[:],
                    p.x[:, oc, TG * blkk:TG * (blkk + 1)], ALU.add))
    scope.__exit__(None, None, None)
    p.x_dv = tok_oev[15]
    p.psC_free = tok_oev[14]
    p.psD_free = (p.DV, tok_oev[15])
    p.big_free = (p.TE, tok_omm[7])
    p.ring_free = (p.TE, tok_omm[7])
    p.ub_free = (p.TE, tok_omm[7])
    p.t_c4 = t_qev[NHEAD - 1]
    p.scr_free = (p.TE, tok_omm[7])
    p.q2buf_free = [(p.TE.h, tok_omm[7]), (p.DV.h, tok_oev[15])]
    p.scrG_free_l = [(p.TE.h, tok_omm[7])]
    p.gT_free_l = [(p.TE.h, tok_omm[7]), (p.DV.h, tok_oev[15])]
    p.simbuf_free = [(p.DV.h, tok_oev[15])]
    p.ring2_free = (p.DV, tok_oev[15])


def emit_store(p):
    blk = p.blk

    @blk.sync
    def _(s):
        s.wait_ge(p.DV.h, p.x_dv)
        for c in range(CH):
            dma(s, p.DS, p.d_out[128 * c:128 * (c + 1), :], p.x[:, c, :])
        s.wait_ge(p.DS.h, p.DS.n)
        s.wait_ge(p.DT.h, p.DT.n)
